# version 14
# baseline (speedup 1.0000x reference)
import contextlib
import numpy as np
import ml_dtypes
import concourse.bass as bass
import concourse.mybir as mybir
from concourse.bass_utils import run_bass_kernel_spmd

F32 = mybir.dt.float32
BF16 = mybir.dt.bfloat16
AF = mybir.ActivationFunctionType
ALU = mybir.AluOpType
AX = mybir.AxisListType
BFD = ml_dtypes.bfloat16

N_CORES = 8
ENGINES = ("sync", "gpsimd", "vector", "scalar", "tensor")


def mk_cfg(dev=False):
    if dev:
        return dict(B=1, S=1024, E=512, LAT=256, HF=1024, TCH=256, HD=128, dbg=True)
    return dict(B=2, S=2048, E=2048, LAT=512, HF=8192, TCH=512, HD=128, dbg=False)


NPOOL = 12


class Prog:
    def __init__(self, nc):
        self.nc = nc
        self.ops = {e: [] for e in ENGINES}
        self.sem_objs = {}
        self.counts = {s: 0 for s in ("sT", "sV", "sS", "sG", "sY")}
        for q in ("Y", "G", "S"):
            for i in range(NPOOL):
                self.counts[f"d{q}{i}"] = 0
        self.dma_idx = {"sync": 0, "gpsimd": 0, "scalar": 0}

    def op(self, engine, fn, after=(), inc=None, inc_amt=1):
        waits = {}
        for tok in after:
            if tok is None:
                continue
            s, v = tok
            if waits.get(s, 0) < v:
                waits[s] = v
        tok_out = None
        if inc is not None:
            self.counts[inc] += inc_amt
            tok_out = (inc, self.counts[inc])
        self.ops[engine].append((fn, waits, (inc, inc_amt) if inc else None))
        return tok_out

    QL = {"sync": "Y", "gpsimd": "G", "scalar": "S"}

    def dma(self, queue, out, in_, after=(), **kw):
        i = self.dma_idx[queue]
        self.dma_idx[queue] = i + 1
        semn = f"d{self.QL[queue]}{i % NPOOL}"
        after = list(after)
        if i >= NPOOL:
            after.append((semn, self.counts[semn]))
        return self.op(queue, lambda e: e.dma_start(out=out, in_=in_, **kw),
                       after=after, inc=semn, inc_amt=16)

    def mm(self, out, lhsT, rhs, start, stop, after=(), inc=None):
        if inc is None:
            inc = stop
        return self.op(
            "tensor",
            lambda e: e.matmul(out, lhsT, rhs, start=start, stop=stop),
            after=after, inc=("sT" if inc else None))

    def vop(self, fn, after=(), inc=True):
        return self.op("vector", fn, after=after, inc=("sV" if inc else None))

    def sop(self, fn, after=(), inc=True):
        return self.op("scalar", fn, after=after, inc=("sS" if inc else None))

    def gop(self, fn, after=(), inc=True):
        return self.op("gpsimd", fn, after=after, inc=("sG" if inc else None))

    def barrier(self):
        waits = {}
        for e, sem in (("tensor", "sT"), ("vector", "sV"), ("scalar", "sS"),
                       ("gpsimd", "sG"), ("sync", "sY")):
            if not self.ops[e]:
                continue
            fn, w, inc = self.ops[e][-1]
            if inc is None:
                self.counts[sem] += 1
                inc = (sem, 1)
                self.ops[e][-1] = (fn, w, inc)
                waits[sem] = self.counts[sem]
            else:
                waits[inc[0]] = self.counts[inc[0]]
        for s, v in self.counts.items():
            if s[0] == "d" and v:
                waits[s] = v
        for e in ENGINES:
            self.ops[e].append((lambda eng: eng.nop(), dict(waits), None))

    def emit(self, es):
        nc = self.nc
        for s in self.counts:
            self.sem_objs[s] = es.enter_context(nc.semaphore(s))
        block = es.enter_context(nc.Block())
        for eng_name in ENGINES:
            ops = self.ops[eng_name]

            def make_emitter(ops=ops):
                def emitter(eng):
                    last = {}
                    for fn, waits, inc in ops:
                        for s, v in waits.items():
                            if last.get(s, 0) < v:
                                eng.wait_ge(self.sem_objs[s], v)
                                last[s] = v
                        inst = fn(eng)
                        if inc is not None:
                            s, amt = inc
                            inst.then_inc(self.sem_objs[s], amt)
                return emitter

            getattr(block, eng_name)(make_emitter())


class Builder:
    def __init__(self, cfg):
        self.cfg = cfg
        for k, v in cfg.items():
            setattr(self, k, v)
        self.NH = self.E // self.HD
        self.ET = self.E // 128
        self.LT = self.LAT // 128
        self.HA = self.HF // 128
        self.ND = self.TCH // 128
        self.NCH = self.S // self.TCH
        self.SH = self.S // 2
        self.FSLAB = min(256, self.S // 8)
        self.NSL = self.HF // self.FSLAB
        self.SC = 1.0 / float(np.sqrt(self.HD))
        self.FSC = 1.0 / float(np.sqrt(self.HF))

        self.nc = bass.Bass("TRN2", target_bir_lowering=False)
        self.p = Prog(self.nc)
        self._rope_last = {}
        self.es = contextlib.ExitStack()
        self._declare_dram()
        self._declare_sbuf()

    def _declare_dram(self):
        nc, B, S, E, LAT, HF = self.nc, self.B, self.S, self.E, self.LAT, self.HF
        dt_ = nc.dram_tensor
        self.xT = dt_("xT", [B, E, S], F32, kind="ExternalInput").ap()
        self.wq = dt_("wq", [E, E], F32, kind="ExternalInput").ap()
        self.wkv = dt_("wkv", [E, LAT], F32, kind="ExternalInput").ap()
        self.wku = dt_("wku", [LAT, E], F32, kind="ExternalInput").ap()
        self.wvu = dt_("wvu", [LAT, E], F32, kind="ExternalInput").ap()
        self.w1t = dt_("w1t", [E, HF], F32, kind="ExternalInput").ap()
        self.w3t = dt_("w3t", [E, HF], F32, kind="ExternalInput").ap()
        self.w2t = dt_("w2t", [HF, E], F32, kind="ExternalInput").ap()
        self.trigc = dt_("trigc", [128, S], F32, kind="ExternalInput").ap()
        self.h64c = dt_("h64c", [self.HA, self.HA], BF16, kind="ExternalInput").ap()
        self.h128c = dt_("h128c", [128, 128], BF16, kind="ExternalInput").ap()
        self.onescd = dt_("onescd", [128, 1], BF16, kind="ExternalInput").ap()
        self.onesrd = dt_("onesrd", [1, 128], F32, kind="ExternalInput").ap()
        self.nrowd = dt_("nrowd", [1, 128], F32, kind="ExternalInput").ap()
        self.masksd = dt_("masksd", [128, (self.ND - 1) * 128 + self.TCH], BF16,
                          kind="ExternalInput").ap()
        self.miscd = dt_("miscd", [128, 4], F32, kind="ExternalInput").ap()

        ET, LT, NH, HA = self.ET, self.LT, self.NH, self.HA
        NQS = 2
        FSLAB = self.FSLAB
        NSL = self.NSL
        TCH, NCH = self.TCH, self.NCH
        self.w2s1 = dt_("w2s1", [HF, E], BF16).ap()
        self.w2ppR = dt_("w2ppR", [ET, 128, HA, 128], BF16).ap()
        self.qfs = dt_("qfs", [2, 128, self.NH, self.SH], BF16).ap()
        self.xb = dt_("xb", [B, ET, 128, S], BF16).ap()
        self.wqR = dt_("wqR", [NH // NQS, 128, ET, 128 * NQS], BF16).ap()
        self.wkvR = dt_("wkvR", [LT, 128, ET, 128], BF16).ap()
        self.wkuR = dt_("wkuR", [NH, 128, LT, 128], BF16).ap()
        self.wvuR = dt_("wvuR", [E // 512, 128, LT, 512], BF16).ap()
        self.w1R = dt_("w1R", [NSL, 128, ET, FSLAB], BF16).ap()
        self.w3R = dt_("w3R", [NSL, 128, ET, FSLAB], BF16).ap()
        self.ysR = dt_("ysR", [B, NCH, 128, ET * TCH], F32).ap()
        self.ysb = dt_("ysb", [B, NCH, 128, ET * TCH], BF16).ap()
        self.zT = dt_("zT", [B, E, S], F32, kind="ExternalOutput").ap()
        if self.dbg:
            self.dbg_t = {}
            for nm, shp, dt2 in [("d_cf", [128, self.LT * S], BF16),
                                 ("d_kf", [128, self.NH * S], BF16),
                                 ("d_vt", [128, (S // 128) * E], BF16),
                                 ("d_qf", [2, 128, self.NH, self.SH], BF16),
                                 ("d_negm", [128, self.NH], F32),
                                 ("d_y", [B, self.NCH, 128, self.ET * self.TCH], F32),
                                 ("d_w2pp", [self.ET, 128, self.HA, 128], BF16)]:
                self.dbg_t[nm] = dt_(nm, shp, dt2, kind="ExternalOutput").ap()

    def _declare_sbuf(self):
        es, nc = self.es, self.nc
        S, E, TCH, ET, LT, NH = self.S, self.E, self.TCH, self.ET, self.LT, self.NH
        sbt = nc.sbuf_tensor
        self.A8 = es.enter_context(sbt("A8", [128, ET * S], BF16))
        self.B8 = es.enter_context(sbt("B8", [128, ET * S], BF16))
        self.M4W = max(ET * TCH, 2 * S + 1024)
        self.M4 = es.enter_context(sbt("M4", [128, self.M4W], F32))
        self.C2 = es.enter_context(sbt("C2", [128, max(LT * S, ET * TCH)], BF16))
        self.W2b = es.enter_context(sbt("W2b", [128, 4096], BF16))
        self.pb4 = es.enter_context(sbt("pb4", [128, 4 * TCH], BF16))
        self.pb2h = [self.pb4[:, 0:2 * TCH], self.pb4[:, 2 * TCH:4 * TCH]]
        self.pbuf = [self.pb4[:, i * TCH:(i + 1) * TCH] for i in range(4)]
        self.abuf = [es.enter_context(sbt(f"ab{i}", [128, TCH], BF16)) for i in range(2)]
        self.qbuf = [es.enter_context(sbt(f"qb{i}", [128, max(TCH, 512)], BF16))
                     for i in range(2)]
        self.bc = [es.enter_context(sbt(f"bc{i}", [128, max(TCH, 512)], F32))
                   for i in range(2)]
        self.h64s = es.enter_context(sbt("h64s", [self.HA, self.HA], BF16))
        self.h128s = es.enter_context(sbt("h128s", [128, 128], BF16))
        self.onescs = es.enter_context(sbt("onescs", [128, 1], BF16))
        self.onesrs = es.enter_context(sbt("onesrs", [1, 128], F32))
        self.nrows = es.enter_context(sbt("nrows", [1, 128], F32))
        self.maskss = es.enter_context(
            sbt("maskss", [128, (self.ND - 1) * 128 + TCH], BF16))
        self.miscs = es.enter_context(sbt("miscs", [128, 4], F32))
        self.rows = es.enter_context(sbt("rows", [1, 4 * TCH], F32))
        self.t11 = es.enter_context(sbt("t11", [1, 2], F32))
        self.nq2 = es.enter_context(sbt("nq2", [1, NH], F32))
        self.nk2 = es.enter_context(sbt("nk2", [1, NH], F32))
        self.mrow = es.enter_context(sbt("mrow", [1, NH], F32))
        self.negm = es.enter_context(sbt("negm", [128, NH], F32))
        self.psA = es.enter_context(nc.psum_tensor("psA", [128, 2048], F32))
        self.psB = es.enter_context(nc.psum_tensor("psB", [128, 2048], F32))
        self.trig = self.M4[:, self.M4W - S: self.M4W]
        self.trig2 = self.M4[:, self.M4W - 2 * S: self.M4W - S]

    def pA(self, i):
        return self.psA[:, i * 512:(i + 1) * 512]

    def pB(self, i):
        return self.psB[:, i * 512:(i + 1) * 512]

    def zcol(self):
        return self.miscs[:, 1:2]

    def zrow(self):
        return self.miscs[0:1, 1:2]

    def eps_row(self):
        return self.miscs[0:1, 0:1]

    def _conv_init(self, base, width):
        self._cv_base, self._cv_w = base, width
        self._cv_par = 0
        self._cv_tok = [None, None]

    def _conv_slab(self, src, dst, shp):
        p = self.p
        n = int(np.prod(shp))
        assert n <= self._cv_w, (n, self._cv_w)
        par = self._cv_par
        self._cv_par ^= 1
        h = self.M4[:, self._cv_base + par * self._cv_w:
                    self._cv_base + par * self._cv_w + n]
        if len(shp) == 2:
            hv = h.rearrange("p (x y) -> p x y", x=shp[0])
        else:
            hv = h
        tin = p.dma("sync", hv, src, after=[self._cv_tok[par]])
        tout = p.dma("gpsimd", dst, hv, after=[tin])
        self._cv_tok[par] = tout
        return tout

    def _conv_part1(self):
        ET, S, LT, NH = self.ET, self.S, self.LT, self.NH
        W = min(4096, self.M4W // 2)
        self._conv_init(0, W)
        toks = {}
        XA = max(1, W // S)
        for a0 in range(0, ET, XA):
            toks["xb0"] = self._conv_slab(
                self.xT[0, a0 * 128:(a0 + XA) * 128, :].rearrange(
                    "(a p) t -> p a t", p=128),
                self.xb[0, a0:a0 + XA, :, :].rearrange("a p t -> p a t"),
                (XA, S))
        for lt in range(LT):
            toks["wkv"] = self._conv_slab(
                self.wkv[:, lt * 128:(lt + 1) * 128].rearrange(
                    "(a p) o -> p a o", p=128),
                self.wkvR[lt, :, :, :], (ET, 128))
        for sl in range(NH // 2):
            toks["wq"] = self._conv_slab(
                self.wq[:, sl * 256:(sl + 1) * 256].rearrange(
                    "(a p) o -> p a o", p=128),
                self.wqR[sl, :, :, :], (ET, 256))
        for h in range(NH):
            toks["wku"] = self._conv_slab(
                self.wku[:, h * 128:(h + 1) * 128].rearrange(
                    "(a p) o -> p a o", p=128),
                self.wkuR[h, :, :, :], (LT, 128))
        for dch in range(self.E // 512):
            toks["wvu"] = self._conv_slab(
                self.wvu[:, dch * 512:(dch + 1) * 512].rearrange(
                    "(a p) o -> p a o", p=128),
                self.wvuR[dch, :, :, :], (LT, 512))
        self.tok_img = toks
        self.tok_conv_p1 = self._cv_tok[:]

    def _conv_part2_fill(self):
        ET, S = self.ET, self.S
        FSLAB = self.FSLAB
        NSL = self.NSL
        W = min(2048, (self.M4W - 2 * self.S) // 2)
        self._conv_w2 = W
        AW = max(1, W // FSLAB)
        jobs = []
        for b in range(1, self.B):
            for a in range(ET):
                jobs.append((self.xT[b, a * 128:(a + 1) * 128, :],
                             self.xb[b, a, :, :], (S,)))
        for wsrc, wdst in ((self.w1t, self.w1R), (self.w3t, self.w3R)):
            for si in range(NSL):
                for a0 in range(0, ET, AW):
                    jobs.append((
                        wsrc[a0 * 128:(a0 + AW) * 128,
                             si * FSLAB:(si + 1) * FSLAB].rearrange(
                            "(a p) o -> p a o", p=128),
                        wdst[si, :, a0:a0 + AW, :], (AW, FSLAB)))
        self._conv_jobs = jobs
        self._conv_ji = 0

    def _pump_conv(self, n):
        while n > 0 and self._conv_ji < len(self._conv_jobs):
            src, dst, shp = self._conv_jobs[self._conv_ji]
            self._conv_ji += 1
            self._conv_slab(src, dst, shp)
            n -= 1

    def _drain_conv(self):
        self._pump_conv(10 ** 9)
        self.tok_conv_p2 = self._cv_tok[:]

    def p0(self):
        p = self.p
        t = []
        t.append(p.dma("sync", self.h64s[:, :], self.h64c[:, :]))
        t.append(p.dma("sync", self.h128s[:, :], self.h128c[:, :]))
        t.append(p.dma("sync", self.onescs[:, :], self.onescd[:, :]))
        t.append(p.dma("sync", self.onesrs[:, :], self.onesrd[:, :]))
        t.append(p.dma("sync", self.nrows[:, :], self.nrowd[:, :]))
        t.append(p.dma("sync", self.maskss[:, :], self.masksd[:, :]))
        t.append(p.dma("sync", self.miscs[:, :], self.miscd[:, :]))
        self.tok_consts = list(t)
        self._conv_part1()
        self._fold_w2()
        p.barrier()

    def _fold_w2(self):
        p, HA, E = self.p, self.HA, self.E
        DCW = min(64, (self.ET * self.S) // 512)
        fins1 = [self.A8[:, i * 128 * DCW:(i + 1) * 128 * DCW] for i in range(2)]
        fouts1 = [self.B8[:, i * 128 * DCW:(i + 1) * 128 * DCW] for i in range(2)]
        fins2 = [self.A8[:, (2 + i) * 128 * DCW:(3 + i) * 128 * DCW] for i in range(2)]
        fouts2 = [self.B8[:, (2 + i) * 128 * DCW:(3 + i) * 128 * DCW] for i in range(2)]
        DC = E // DCW
        w1s = min(512, 128 * DCW)
        w2s = min(512, HA * DCW)
        s1_done = {}
        ev1, st1, ev2, st2 = {}, {}, {}, {}
        st_all = []

        def stage1(dc):
            par = dc % 2
            fin, fout = fins1[par], fouts1[par]
            src_ = self.w2t[:, dc * DCW:(dc + 1) * DCW].rearrange(
                "(a b) d -> a b d", a=HA)
            tin = p.dma("gpsimd", fin[0:HA, :].rearrange("a (b d) -> a b d", b=128),
                        src_, after=[ev1.get(dc - 2, None), self.tok_consts[0]])
            evs = []
            for j in range((128 * DCW) // w1s):
                tmm = p.mm(self.pA(j % 2)[0:HA, 0:w1s], self.h64s[:, :],
                           fin[0:HA, j * w1s:(j + 1) * w1s],
                           start=True, stop=True,
                           after=[tin] + ([evs[-2]] if len(evs) >= 2
                                          else [ev1.get(dc - 1, None)]))
                evs.append(p.sop(
                    lambda e, j=j, fout=fout: e.copy(
                        fout[0:HA, j * w1s:(j + 1) * w1s],
                        self.pA(j % 2)[0:HA, 0:w1s]),
                    after=[tmm, st1.get(dc - 2, None)]))
            ev1[dc] = evs[-1]
            dst = self.w2s1[:, dc * DCW:(dc + 1) * DCW].rearrange(
                "(a b) d -> a b d", a=HA)
            st1[dc] = p.dma(
                "sync", dst, fout[0:HA, :].rearrange("a (b d) -> a b d", b=128),
                after=[evs[-1]])
            s1_done[dc] = st1[dc]

        def stage2(dc):
            par = dc % 2
            fin, fout = fins2[par], fouts2[par]
            src_ = self.w2s1[:, dc * DCW:(dc + 1) * DCW].rearrange(
                "(a b) d -> b a d", b=128)
            tin = p.dma("gpsimd",
                        fin[:, 0:HA * DCW].rearrange("b (a d) -> b a d", a=HA),
                        src_, after=[ev2.get(dc - 2, None), s1_done[dc]])
            evs = []
            for j in range((HA * DCW) // w2s):
                tmm = p.mm(self.pA(2 + j % 2)[:, 0:w2s], self.h128s[:, :],
                           fin[:, j * w2s:(j + 1) * w2s],
                           start=True, stop=True,
                           after=[tin] + ([evs[-2]] if len(evs) >= 2
                                          else [ev2.get(dc - 1, None)]))
                evs.append(p.sop(
                    lambda e, j=j, fout=fout: e.activation(
                        fout[:, j * w2s:(j + 1) * w2s],
                        self.pA(2 + j % 2)[:, 0:w2s], AF.Copy,
                        bias=0.0, scale=self.FSC),
                    after=[tmm, st2.get(dc - 2, None)]))
            ev2[dc] = evs[-1]
            dt0 = (dc * DCW) // 128
            off = (dc * DCW) % 128
            dst = self.w2ppR[dt0, :, :, off:off + DCW]
            st2[dc] = p.dma(
                "sync", dst, fout[:, 0:HA * DCW].rearrange("b (a d) -> b a d", a=HA),
                after=[evs[-1]])
            st_all.append(st2[dc])

        for dc in range(DC):
            stage1(dc)
            if dc >= 1:
                stage2(dc - 1)
        stage2(DC - 1)
        self.tok_w2pp = st_all[-NPOOL:]
        if self.dbg:
            self.tok_w2pp = [p.dma("sync", self.dbg_t["d_w2pp"][:, :, :, :],
                                   self.w2ppR[:, :, :, :], after=self.tok_w2pp)]

    def _conv_init(self, base, width):
        self._cv_base, self._cv_w = base, width
        self._cv_par = 0
        self._cv_tok = [None, None]

    def _conv_slab(self, src, dst, shp):
        p = self.p
        n = int(np.prod(shp))
        assert n <= self._cv_w, (n, self._cv_w)
        par = self._cv_par
        self._cv_par ^= 1
        h = self.M4[:, self._cv_base + par * self._cv_w:
                    self._cv_base + par * self._cv_w + n]
        if len(shp) == 2:
            hv = h.rearrange("p (x y) -> p x y", x=shp[0])
        else:
            hv = h
        tin = p.dma("sync", hv, src, after=[self._cv_tok[par]])
        tout = p.dma("gpsimd", dst, hv, after=[tin])
        self._cv_tok[par] = tout
        return tout

    def _conv_part1(self):
        ET, S, LT, NH = self.ET, self.S, self.LT, self.NH
        W = min(4096, self.M4W // 2)
        self._conv_init(0, W)
        toks = {}
        XA = max(1, W // S)
        for a0 in range(0, ET, XA):
            toks["xb0"] = self._conv_slab(
                self.xT[0, a0 * 128:(a0 + XA) * 128, :].rearrange(
                    "(a p) t -> p a t", p=128),
                self.xb[0, a0:a0 + XA, :, :].rearrange("a p t -> p a t"),
                (XA, S))
        for lt in range(LT):
            toks["wkv"] = self._conv_slab(
                self.wkv[:, lt * 128:(lt + 1) * 128].rearrange(
                    "(a p) o -> p a o", p=128),
                self.wkvR[lt, :, :, :], (ET, 128))
        for sl in range(NH // 2):
            toks["wq"] = self._conv_slab(
                self.wq[:, sl * 256:(sl + 1) * 256].rearrange(
                    "(a p) o -> p a o", p=128),
                self.wqR[sl, :, :, :], (ET, 256))
        for h in range(NH):
            toks["wku"] = self._conv_slab(
                self.wku[:, h * 128:(h + 1) * 128].rearrange(
                    "(a p) o -> p a o", p=128),
                self.wkuR[h, :, :, :], (LT, 128))
        for dch in range(self.E // 512):
            toks["wvu"] = self._conv_slab(
                self.wvu[:, dch * 512:(dch + 1) * 512].rearrange(
                    "(a p) o -> p a o", p=128),
                self.wvuR[dch, :, :, :], (LT, 512))
        self.tok_img = toks
        self.tok_conv_p1 = self._cv_tok[:]

    def _conv_part2_fill(self):
        ET, S = self.ET, self.S
        FSLAB = self.FSLAB
        NSL = self.NSL
        W = min(2048, (self.M4W - 2 * self.S) // 2)
        self._conv_w2 = W
        AW = max(1, W // FSLAB)
        jobs = []
        for b in range(1, self.B):
            for a in range(ET):
                jobs.append((self.xT[b, a * 128:(a + 1) * 128, :],
                             self.xb[b, a, :, :], (S,)))
        for wsrc, wdst in ((self.w1t, self.w1R), (self.w3t, self.w3R)):
            for si in range(NSL):
                for a0 in range(0, ET, AW):
                    jobs.append((
                        wsrc[a0 * 128:(a0 + AW) * 128,
                             si * FSLAB:(si + 1) * FSLAB].rearrange(
                            "(a p) o -> p a o", p=128),
                        wdst[si, :, a0:a0 + AW, :], (AW, FSLAB)))
        self._conv_jobs = jobs
        self._conv_ji = 0

    def _pump_conv(self, n):
        while n > 0 and self._conv_ji < len(self._conv_jobs):
            src, dst, shp = self._conv_jobs[self._conv_ji]
            self._conv_ji += 1
            self._conv_slab(src, dst, shp)
            n -= 1

    def _drain_conv(self):
        self._pump_conv(10 ** 9)
        self.tok_conv_p2 = self._cv_tok[:]

    def p0(self):
        p = self.p
        t = []
        t.append(p.dma("sync", self.h64s[:, :], self.h64c[:, :]))
        t.append(p.dma("sync", self.h128s[:, :], self.h128c[:, :]))
        t.append(p.dma("sync", self.onescs[:, :], self.onescd[:, :]))
        t.append(p.dma("sync", self.onesrs[:, :], self.onesrd[:, :]))
        t.append(p.dma("sync", self.nrows[:, :], self.nrowd[:, :]))
        t.append(p.dma("sync", self.maskss[:, :], self.masksd[:, :]))
        t.append(p.dma("sync", self.miscs[:, :], self.miscd[:, :]))
        self.tok_consts = list(t)
        self._conv_part1()
        self._fold_w2()
        p.barrier()

    def _fold_w2(self):
        p, HA, E = self.p, self.HA, self.E
        DCW = min(128, (self.ET * self.S) // 256)
        fins = [self.A8[:, 0:128 * DCW], self.A8[:, 128 * DCW:256 * DCW]]
        fouts = [self.B8[:, 0:128 * DCW], self.B8[:, 128 * DCW:256 * DCW]]
        DC = E // DCW
        w1s = min(512, 128 * DCW)
        w2s = min(512, HA * DCW)
        s1_done = {}
        mm_by = {}
        ev_by = {}
        st_by = {}
        for dc in range(DC):
            par = dc % 2
            fin, fout = fins[par], fouts[par]
            src = self.w2t[:, dc * DCW:(dc + 1) * DCW].rearrange(
                "(a b) d -> a b d", a=HA)
            tin = p.dma("gpsimd", fin[0:HA, :].rearrange("a (b d) -> a b d", b=128),
                        src, after=[ev_by.get(dc - 2, None), self.tok_consts[0]])
            evs = []
            for j in range((128 * DCW) // w1s):
                tmm = p.mm(self.pA(j % 4)[0:HA, 0:w1s], self.h64s[:, :],
                           fin[0:HA, j * w1s:(j + 1) * w1s],
                           start=True, stop=True,
                           after=[tin] + ([evs[-4]] if len(evs) >= 4
                                          else [ev_by.get(dc - 1, None)]))
                evs.append(p.sop(
                    lambda e, j=j, fout=fout: e.copy(
                        fout[0:HA, j * w1s:(j + 1) * w1s],
                        self.pA(j % 4)[0:HA, 0:w1s]),
                    after=[tmm, st_by.get(dc - 2, None)]))
            ev_by[dc] = evs[-1]
            dst = self.w2s1[:, dc * DCW:(dc + 1) * DCW].rearrange(
                "(a b) d -> a b d", a=HA)
            st_by[dc] = p.dma(
                "sync", dst, fout[0:HA, :].rearrange("a (b d) -> a b d", b=128),
                after=[evs[-1]])
            s1_done[dc] = st_by[dc]
        s1_ev_final = ev_by[DC - 1]
        ev_by, st_by = {-1: s1_ev_final, -2: s1_ev_final}, {}
        last_st = None
        for dc in range(DC):
            par = dc % 2
            fin, fout = fins[par], fouts[par]
            src = self.w2s1[:, dc * DCW:(dc + 1) * DCW].rearrange(
                "(a b) d -> b a d", b=128)
            tin = p.dma("gpsimd",
                        fin[:, 0:HA * DCW].rearrange("b (a d) -> b a d", a=HA),
                        src, after=[ev_by.get(dc - 2), s1_done[dc]])
            evs = []
            for j in range((HA * DCW) // w2s):
                tmm = p.mm(self.pB(j % 4)[:, 0:w2s], self.h128s[:, :],
                           fin[:, j * w2s:(j + 1) * w2s],
                           start=True, stop=True,
                           after=[tin] + ([evs[-4]] if len(evs) >= 4
                                          else [ev_by.get(dc - 1)]))
                evs.append(p.sop(
                    lambda e, j=j, fout=fout: e.activation(
                        fout[:, j * w2s:(j + 1) * w2s],
                        self.pB(j % 4)[:, 0:w2s], AF.Copy,
                        bias=0.0, scale=self.FSC),
                    after=[tmm, st_by.get(dc - 2, s1_done[DC - 2 + (dc % 2)])]))
            ev_by[dc] = evs[-1]
            dt0 = (dc * DCW) // 128
            off = (dc * DCW) % 128
            dst = self.w2ppR[dt0, :, :, off:off + DCW]
            st_by[dc] = p.dma(
                "sync", dst, fout[:, 0:HA * DCW].rearrange("b (a d) -> b a d", a=HA),
                after=[evs[-1]])
        self.tok_w2pp = [st_by[dc] for dc in range(max(0, DC - NPOOL), DC)]
        if self.dbg:
            self.tok_w2pp = [p.dma("sync", self.dbg_t["d_w2pp"][:, :, :, :],
                                   self.w2ppR[:, :, :, :], after=self.tok_w2pp)]

    def _rope_evac(self, pq, dst, tcol, tok_mm, eng=0, parity=0,
                   sq_war=None):
        p = self.p
        cos = self.trig[0:48, tcol:tcol + 512]
        sin = self.trig[64:112, tcol:tcol + 512]
        sin2 = self.trig2[0:48, tcol:tcol + 512]
        cos2 = self.trig2[64:112, tcol:tcol + 512]
        T = self.bc[eng]
        ta, tb = T[64:112, 0:512], T[0:48, 0:512]
        op = p.vop if eng == 0 else p.gop
        t1 = p.sop(lambda e: e.copy(dst, pq[:, :]), after=[tok_mm])
        sq = self.qbuf[parity][:, 0:512]
        tsq = p.vop(lambda e: e.tensor_mul(sq, dst, dst),
                    after=[t1, sq_war])
        prev = self._rope_last.get(eng)
        o1 = op(lambda e: e.tensor_mul(ta, dst[0:48, :], sin2), after=[t1, prev])
        o2 = op(lambda e: e.tensor_mul(tb, dst[64:112, :], sin), after=[t1])
        o3 = op(lambda e: e.tensor_mul(dst[0:48, :], dst[0:48, :], cos),
                after=[o1, tsq])
        o4 = op(lambda e: e.tensor_sub(dst[0:48, :], dst[0:48, :], tb), after=[o2, o3])
        o5 = op(lambda e: e.tensor_mul(dst[64:112, :], dst[64:112, :], cos2),
                after=[o2, tsq])
        o6 = op(lambda e: e.tensor_add(dst[64:112, :], dst[64:112, :], ta),
                after=[o1, o5])
        self._rope_last[eng] = o6
        return t1, o6, tsq

    def _norm_fin(self, h, tc, ntc, parity, tsq_tok, nrm, prev_red):
        p = self.p
        sq = self.qbuf[parity][:, 0:512]
        tmm = p.mm(self.pB(3)[0:1, 0:512], self.onescs[:, 0:1], sq,
                   start=(tc == 0), stop=(tc == ntc - 1),
                   after=[tsq_tok, prev_red if tc == 0 else None], inc=True)
        if tc < ntc - 1:
            return None, tmm
        tmx = p.vop(lambda e: e.reduce_max(self.t11[0:1, 0:1],
                                           self.pB(3)[0:1, 0:512],
                                           axis=AX.X), after=[tmm])
        tac = p.vop(lambda e: e.tensor_max(nrm[0:1, h:h + 1], nrm[0:1, h:h + 1],
                                           self.t11[0:1, 0:1]), after=[tmx])
        return tac, tmx

    def p1a(self, b):
        p = self.p
        ET, SH, NH, LT, S = self.ET, self.SH, self.NH, self.LT, self.S
        xf = self.A8[:, 0:ET * SH]
        qfh = self.A8[:, ET * SH:2 * ET * SH]
        t_trig = p.dma("sync", self.trig, self.trigc[:, :],
                       after=(self.tok_conv_p1 if b == 0 else []))
        t_trig2a = p.dma("sync", self.trig2[64:112, :], self.trigc[0:48, :],
                         after=[t_trig])
        t_trig = p.dma("sync", self.trig2[0:48, :], self.trigc[64:112, :],
                       after=[t_trig2a])
        if b == 0:
            self._conv_part2_fill()
            self._conv_init(0, self._conv_w2)
        p.vop(lambda e: e.memset(self.nq2[0:1, :], 0.0))
        p.vop(lambda e: e.memset(self.nk2[0:1, :], 0.0))
        spill_tok = None
        cf_ev = None
        q_ev_last = None
        tac = None
        for half in range(2):
            t0 = half * SH
            t_xf = p.dma("sync", xf.rearrange("p (a t) -> p a t", a=ET),
                         self.xb[b, :, :, t0:t0 + SH].rearrange("a p t -> p a t"),
                         after=[spill_tok, q_ev_last, cf_ev,
                                self._rope_last.get(0), self._rope_last.get(1),
                                self.tok_img["xb0"] if b == 0 else None] +
                               (self.tok_conv_p2 if b > 0 else []))
            wslab = self.W2b[:, 0:ET * 128]
            cf_by = {}
            for lt in range(LT):
                t_w = p.dma("gpsimd", wslab.rearrange("p (a o) -> p a o", a=ET),
                            self.wkvR[lt, :, :, :],
                            after=[cf_ev, self.tok_img["wkv"]])
                if b == 0:
                    self._pump_conv(4)
                for tc in range(SH // 512):
                    gc = lt * (SH // 512) + tc
                    last = None
                    for dt in range(ET):
                        last = p.mm(
                            self.pA(gc % 2), wslab[:, dt * 128:(dt + 1) * 128],
                            xf[:, dt * SH + tc * 512: dt * SH + tc * 512 + 512],
                            start=(dt == 0), stop=(dt == ET - 1),
                            after=[t_w, t_xf, cf_by.get(gc - 2, None)])
                    cf_ev = p.sop(lambda e, lt=lt, t0=t0, tc=tc, gc=gc: e.copy(
                        self.C2[:, lt * S + t0 + tc * 512:
                                lt * S + t0 + tc * 512 + 512],
                        self.pA(gc % 2)[:, :]), after=[last])
                    cf_by[gc] = cf_ev
            NQS = 2
            qsl = [self.B8[:, 0:ET * 128 * NQS],
                   self.B8[:, ET * 128 * NQS:2 * ET * 128 * NQS]]
            t_qs = [None, None]
            ev_by_h = {}
            nrm_mm_prev = None
            pend_norm = None
            prev_red = None
            for h in range(NH):
                sbi = (h // NQS) % 2
                if h % NQS == 0:
                    t_qs[sbi] = p.dma(
                        "gpsimd", qsl[sbi].rearrange("p (a o) -> p a o", a=ET),
                        self.wqR[h // NQS, :, :, :],
                        after=[ev_by_h.get(h - NQS - 1, None),
                               self.tok_img["wq"]])
                    if b == 0:
                        self._pump_conv(5)
                for tc in range(SH // 512):
                    gq = h * (SH // 512) + tc
                    pq = self.pA(2 + gq % 2)
                    last = None
                    for dt in range(ET):
                        lhsT = qsl[sbi][:, dt * 128 * NQS + (h % NQS) * 128:
                                        dt * 128 * NQS + (h % NQS) * 128 + 128]
                        last = p.mm(pq, lhsT,
                                    xf[:, dt * SH + tc * 512:dt * SH + tc * 512 + 512],
                                    start=(dt == 0), stop=(dt == ET - 1),
                                    after=[t_qs[sbi], t_xf, t_trig,
                                           ev_by_h.get(("w", gq - 2), None)])
                    dst = qfh[:, h * SH + tc * 512: h * SH + tc * 512 + 512]
                    ev_t1, ev, tsq = self._rope_evac(
                        pq, dst, t0 + tc * 512, last, eng=(gq % 5 >= 3),
                        parity=gq % 2, sq_war=nrm_mm_prev)
                    ev_by_h[("w", gq)] = ev_t1
                    ev_by_h[h] = ev
                    if pend_norm is not None:
                        tac2, nrm_mm_prev = self._norm_fin(*pend_norm,
                                                           self.nq2, prev_red)
                        if tac2 is not None:
                            tac, prev_red = tac2, tac2
                    pend_norm = (h, tc, SH // 512, gq % 2, tsq)
            if pend_norm is not None:
                tac2, nrm_mm_prev = self._norm_fin(*pend_norm, self.nq2, prev_red)
                if tac2 is not None:
                    tac = tac2
                pend_norm = None
            q_ev_last = ev_by_h[NH - 1]
            spill_tok = p.dma("sync", self.qfs[half, :, :, :],
                              qfh.rearrange("p (h t) -> p h t", h=NH),
                              after=[q_ev_last, tac, self._rope_last.get(0),
                                     self._rope_last.get(1)])
        self.tok_cf = cf_ev
        self.tok_spill = spill_tok
        if self.dbg:
            p.dma("sync", self.dbg_t["d_cf"][:, :], self.C2[:, 0:LT * S],
                  after=[cf_ev])
            for half in range(2):
                p.dma("sync", self.dbg_t["d_qf"][half, :, :, :],
                      self.qfs[half, :, :, :], after=[spill_tok])

    def p1b(self, b):
        p = self.p
        ET, NH, LT, S, E = self.ET, self.NH, self.LT, self.S, self.E
        kf, vt = self.A8, self.B8
        ksl = [self.W2b[:, 0:LT * 128], self.W2b[:, LT * 128:2 * LT * 128]]
        t_ks = [None, None]
        ev_by = {}
        nrm_mm_prev = None
        tac = None
        pend_norm = None
        prev_red = None
        for h in range(NH):
            sbi = h % 2
            t_ks[sbi] = p.dma(
                "gpsimd", ksl[sbi].rearrange("p (a o) -> p a o", a=LT),
                self.wkuR[h, :, :, :],
                after=[ev_by.get(h - 2, None), self.tok_img["wku"]])
            if b == 0:
                self._pump_conv(5)
            for tc in range(S // 512):
                gq = h * (S // 512) + tc
                pq = self.pA(2 + gq % 2)
                last = None
                for lt in range(LT):
                    last = p.mm(pq, ksl[sbi][:, lt * 128:(lt + 1) * 128],
                                self.C2[:, lt * S + tc * 512: lt * S + tc * 512 + 512],
                                start=(lt == 0), stop=(lt == LT - 1),
                                after=[t_ks[sbi], self.tok_cf,
                                       ev_by.get(("w", gq - 2), None)])
                dst = kf[:, h * S + tc * 512: h * S + tc * 512 + 512]
                ev_t1, ev, tsq = self._rope_evac(
                    pq, dst, tc * 512, last, eng=(gq % 5 >= 3),
                    parity=gq % 2, sq_war=nrm_mm_prev)
                ev_by[("w", gq)] = ev_t1
                ev_by[h] = ev
                if pend_norm is not None:
                    tac2, nrm_mm_prev = self._norm_fin(*pend_norm,
                                                       self.nk2, prev_red)
                    if tac2 is not None:
                        tac, prev_red = tac2, tac2
                pend_norm = (h, tc, S // 512, gq % 2, tsq)
        if pend_norm is not None:
            tac2, nrm_mm_prev = self._norm_fin(*pend_norm, self.nk2, prev_red)
            if tac2 is not None:
                tac = tac2
        self.tok_kf = ev_by[NH - 1]
        self.tok_kf2 = [self._rope_last.get(0), self._rope_last.get(1)]
        tm1 = p.vop(lambda e: e.tensor_mul(self.mrow[0:1, :], self.nq2[0:1, :],
                                           self.nk2[0:1, :]), after=[tac])
        tm2 = p.sop(lambda e: e.activation(self.rows[0:1, 0:NH], self.mrow[0:1, :],
                                           AF.Ln, bias=self.eps_row(), scale=1.0),
                    after=[tm1])
        tm3 = p.sop(lambda e: e.activation(self.mrow[0:1, :], self.rows[0:1, 0:NH],
                                           AF.Exp, bias=self.zrow(), scale=0.5),
                    after=[tm2])
        tm4 = p.mm(self.psB[:, 1024:1024 + NH], self.nrows[0:1, :],
                   self.mrow[0:1, :], start=True, stop=True,
                   after=[tm3, nrm_mm_prev])
        tm5 = p.sop(lambda e: e.copy(self.negm[:, :], self.psB[:, 1024:1024 + NH]),
                    after=[tm4])
        self.tok_negm = tm5
        vsl = self.W2b[:, 2 * LT * 128: 2 * LT * 128 + LT * 512]
        evp = {}
        for dch in range(E // 512):
            t_wv = p.dma("gpsimd", vsl.rearrange("p (a o) -> p a o", a=LT),
                         self.wvuR[dch, :, :, :],
                         after=[evp.get(dch - 1, None), self.tok_img["wvu"]])
            if b == 0:
                self._pump_conv(8)
            evl = None
            for tt in range(S // 128):
                gv = dch * (S // 128) + tt
                last = None
                for lt in range(LT):
                    last = p.mm(self.pB(gv % 2),
                                self.C2[:, lt * S + tt * 128: lt * S + tt * 128 + 128],
                                vsl[:, lt * 512:(lt + 1) * 512],
                                start=(lt == 0), stop=(lt == LT - 1),
                                after=[t_wv, self.tok_cf,
                                       evp.get(("w", gv - 2), None)])
                evl = p.sop(lambda e, tt=tt, dch=dch, gv=gv: e.copy(
                    vt[:, tt * E + dch * 512: tt * E + dch * 512 + 512],
                    self.pB(gv % 2)[:, :]), after=[last])
                evp[("w", gv)] = evl
            evp[dch] = evl
        self.tok_vt = evp[E // 512 - 1]
        if b == 0:
            self._drain_conv()
        if self.dbg:
            p.dma("sync", self.dbg_t["d_kf"][:, :], kf[:, 0:NH * S],
                  after=[self.tok_kf])
            p.dma("sync", self.dbg_t["d_vt"][:, :], vt[:, 0:(S // 128) * E],
                  after=[self.tok_vt])
            p.dma("sync", self.dbg_t["d_negm"][:, :], self.negm[:, :],
                  after=[self.tok_negm])

    def p1c(self, b):
        p = self.p
        ET, NH, S, TCH, ND, E = self.ET, self.NH, self.S, self.TCH, self.ND, self.E
        kf, vt = self.A8, self.B8
        r_mean = self.rows[0:1, 0:TCH]
        r_m2 = self.rows[0:1, TCH:2 * TCH]
        r_tmp = self.rows[0:1, 2 * TCH:3 * TCH]
        r_rstd = self.rows[0:1, 3 * TCH:4 * TCH]
        tok_y, tok_yb = None, None
        self._tok_ys = getattr(self, "_tok_ys", {})
        exp_tok = {}
        sum_tok = {}
        ev1_tok, ev2_tok = {}, {}
        sum_war = {}
        bc_war = {}
        tok_tr = [None, None]
        tok_ap_last = None
        tbc_prev, tcp_prev = None, None
        gp = 0

        def sbank(par):
            return self.psA[:, par * 2 * TCH: (par + 1) * 2 * TCH]

        def att_bank(gh):
            return self.psB[:, (gh % 2) * 512:(gh % 2) * 512 + TCH]

        def sum_row(gh):
            return self.psB[0:1, 1024 + (gh % 2) * 512:1024 + (gh % 2) * 512 + TCH]

        def emit_tail(qc, h, gh, mm_att_l, mm_sum_l):
            nonlocal tbc_prev, tcp_prev
            tpar = gp % 2
            tln = p.sop(lambda e, gh=gh: e.activation(
                r_tmp, sum_row(gh), AF.Ln,
                bias=self.zrow(), scale=1.0), after=[mm_sum_l, tbc_prev])
            trs = p.sop(lambda e: e.activation(r_rstd, r_tmp, AF.Exp,
                                               bias=self.zrow(), scale=-1.0),
                        after=[tln])
            sum_war[gh] = tln
            bb = sbank(tpar)[:, 0:TCH]
            tbc = p.mm(bb, self.onesrs[0:1, :], r_rstd,
                       start=True, stop=True,
                       after=[trs, tcp_prev, exp_tok.get(gp - 2, None)])
            tcp = p.sop(lambda e, bb=bb: e.copy(self.bc[0][:, 0:TCH], bb),
                        after=[tbc, ev1_tok.get(gh - 1, None), tok_ap_last])
            bc_war[tpar] = tcp
            att_t = self.M4[:, h * TCH:(h + 1) * TCH]
            tt1 = p.vop(lambda e, gh=gh: e.tensor_mul(
                self.bc[1][:, 0:TCH], att_bank(gh), self.bc[0][:, 0:TCH]),
                after=[tcp, mm_att_l, ev2_tok.get(gh - 1, None)])
            tt2 = p.vop(lambda e, att_t=att_t: e.tensor_add(
                att_t, att_t, self.bc[1][:, 0:TCH]), after=[tt1, t_x])
            ev1_tok[gh] = tt1
            ev2_tok[gh] = tt2
            tbc_prev, tcp_prev = tbc, tcp

        for qc in range(self.NCH):
            q0 = qc * TCH
            half = q0 // self.SH
            tc0 = q0 % self.SH
            t_qfc = p.dma("sync",
                          self.C2[:, 0:NH * TCH].rearrange("p (h t) -> p h t", h=NH),
                          self.qfs[half, :, :, tc0:tc0 + TCH],
                          after=[self.tok_spill, exp_tok.get(gp - 1, None),
                                 sum_tok.get(qc * NH - 1, None)])
            t_x = p.dma("sync",
                        self.M4[:, 0:ET * TCH].rearrange("p (a t) -> p a t", a=ET),
                        self.xT[b, :, q0:q0 + TCH].rearrange("(a p) t -> p a t", p=128),
                        after=[tok_y, tok_yb, tok_ap_last])
            nk = (qc + 1) * ND
            npair = nk // 2
            pend = None
            for h in range(NH):
                gh = qc * NH + h
                mm_att_l, mm_sum_l = None, None
                lag = []
                for kp in range(npair + 2):
                    if len(lag) >= 2 or (kp >= npair and lag):
                        (kpd, tpar_d) = lag.pop(0)
                        for half2 in range(2):
                            k2 = kpd * 2 + half2
                            pbf2 = self.pbuf[tpar_d * 2 + half2]
                            mm_att_l = p.mm(
                                att_bank(gh),
                                vt[:, k2 * E + h * 128: k2 * E + h * 128 + 128],
                                pbf2, start=(k2 == 0), stop=(k2 == nk - 1),
                                after=[exp_tok[(gh, kpd)],
                                       ev1_tok.get(gh - 2, None)],
                                inc=True)
                            mm_sum_l = p.mm(
                                sum_row(gh), self.onescs[:, 0:1],
                                pbf2, start=(k2 == 0), stop=(k2 == nk - 1),
                                after=[exp_tok[(gh, kpd)],
                                       sum_war.get(gh - 2, None)],
                                inc=True)
                    if kp < npair:
                        tpar = gp % 2
                        kt0 = 2 * kp
                        mms = []
                        for half2 in range(2):
                            kt = kt0 + half2
                            mms.append(p.mm(
                                sbank(tpar)[:, half2 * TCH:half2 * TCH + TCH],
                                kf[:, h * S + kt * 128: h * S + kt * 128 + 128],
                                self.C2[:, h * TCH:(h + 1) * TCH],
                                start=True, stop=True,
                                after=[t_qfc, exp_tok.get(gp - 2, None),
                                       bc_war.get(tpar, None)]))
                        te = p.sop(lambda e, h=h, tpar=tpar: e.activation(
                            self.pb2h[tpar], sbank(tpar), AF.Exp,
                            bias=self.negm[:, h:h + 1], scale=self.SC),
                            after=[mms[0], mms[1], self.tok_negm,
                                   sum_tok.get(gh - 1, None) if kp < 2 else None,
                                   mm_sum_l if kp >= 2 else None])
                        for half2 in range(2):
                            kt = kt0 + half2
                            if kt >= qc * ND:
                                r = kt - qc * ND
                                m0 = (self.ND - 1 - r) * 128
                                pbf = self.pbuf[tpar * 2 + half2]
                                te = p.vop(lambda e, pbf=pbf, m0=m0: e.tensor_mul(
                                    pbf, pbf, self.maskss[:, m0:m0 + TCH]),
                                    after=[te])
                        exp_tok[(gh, kp)] = te
                        exp_tok[gp] = te
                        lag.append((kp, tpar))
                        gp += 1
                sum_tok[gh] = mm_sum_l
                if pend is not None:
                    emit_tail(*pend)
                pend = (qc, h, gh, mm_att_l, mm_sum_l)
            emit_tail(*pend)
            mm_mean_prev, mm_var_prev = None, None
            stat_m = self.psB[0:1, 1024:1024 + TCH]
            stat_v = self.psB[0:1, 1536:1536 + TCH]
            for i in range(ET):
                att_t = self.M4[:, i * TCH:(i + 1) * TCH]
                tca = p.sop(lambda e, att_t=att_t, i=i: e.copy(
                    self.abuf[i % 2][:, :], att_t),
                    after=[ev2_tok[qc * NH + i], mm_mean_prev, tok_tr[0]])
                tsq = p.vop(lambda e, i=i: e.tensor_mul(
                    self.qbuf[i % 2][:, 0:TCH], self.abuf[i % 2][:, :],
                    self.abuf[i % 2][:, :]), after=[tca, mm_var_prev, tok_tr[1]])
                mm_mean_prev = p.mm(stat_m, self.onescs[:, 0:1],
                                    self.abuf[i % 2][:, :],
                                    start=(i == 0), stop=(i == ET - 1),
                                    after=[tca, sum_war.get(qc * NH + NH - 2, None)
                                           if i == 0 else None],
                                    inc=True)
                mm_var_prev = p.mm(stat_v, self.onescs[:, 0:1],
                                   self.qbuf[i % 2][:, 0:TCH],
                                   start=(i == 0), stop=(i == ET - 1),
                                   after=[tsq, sum_war.get(qc * NH + NH - 1, None)
                                          if i == 0 else None],
                                   inc=True)
            tr1 = p.sop(lambda e: e.activation(r_mean, stat_m, AF.Copy,
                                               bias=0.0, scale=1.0 / E),
                        after=[mm_mean_prev, tbc_prev])
            tr2 = p.sop(lambda e: e.activation(r_m2, stat_v, AF.Copy,
                                               bias=0.0, scale=1.0 / E),
                        after=[mm_var_prev])
            tok_tr = [tr1, tr2]
            sum_war[qc * NH + NH - 2] = tr1
            sum_war[qc * NH + NH - 1] = tr2
            tr3 = p.vop(lambda e: e.tensor_mul(r_tmp, r_mean, r_mean),
                        after=[tr1])
            tr4 = p.vop(lambda e: e.tensor_sub(r_tmp, r_m2, r_tmp), after=[tr2, tr3])
            tr5 = p.sop(lambda e: e.activation(r_tmp, r_tmp, AF.Ln,
                                               bias=self.eps_row(), scale=1.0),
                        after=[tr4])
            tr6 = p.sop(lambda e: e.activation(r_rstd, r_tmp, AF.Exp,
                                               bias=self.zrow(), scale=-0.5),
                        after=[tr5])
            tr7 = p.vop(lambda e: e.tensor_mul(r_tmp, r_mean, r_rstd), after=[tr6])
            bpar = gp % 2
            bb1 = sbank(bpar)[:, 0:TCH]
            bb2 = sbank((bpar + 1) % 2)[:, 0:TCH]
            tb1 = p.mm(bb1, self.onesrs[0:1, :], r_rstd,
                       start=True, stop=True,
                       after=[tr6, exp_tok.get(gp - 2, None),
                              bc_war.get(bpar, None)])
            tc1 = p.sop(lambda e: e.copy(self.bc[0][:, 0:TCH], bb1),
                        after=[tb1, ev1_tok[qc * NH + NH - 1]])
            tb2 = p.mm(bb2, self.onesrs[0:1, :], r_tmp,
                       start=True, stop=True,
                       after=[tr7, exp_tok.get(gp - 1, None),
                              bc_war.get((bpar + 1) % 2, None), tcp_prev])
            tc2 = p.sop(lambda e: e.copy(self.bc[1][:, 0:TCH], bb2),
                        after=[tb2, ev2_tok[qc * NH + NH - 1]])
            bc_war[bpar] = tc1
            bc_war[(bpar + 1) % 2] = tc2
            tbc_prev, tcp_prev = tb2, tc2
            ap_toks = []
            for i in range(ET):
                att_t = self.M4[:, i * TCH:(i + 1) * TCH]
                ta1 = p.vop(lambda e, att_t=att_t: e.tensor_mul(
                    att_t, att_t, self.bc[0][:, 0:TCH]), after=[tc1])
                ap_toks.append(p.vop(lambda e, att_t=att_t: e.tensor_sub(
                    att_t, att_t, self.bc[1][:, 0:TCH]), after=[ta1, tc2]))
            tok_ap_last = ap_toks[-1]
            tok_y = p.dma("sync", self.ysR[b, qc, :, :],
                          self.M4[:, 0:ET * TCH], after=ap_toks)
            tok_yb = p.dma("gpsimd", self.ysb[b, qc, :, :],
                           self.M4[:, 0:ET * TCH], after=ap_toks)
            self._tok_ys[(b, qc)] = (tok_y, tok_yb)
        self.tok_ylast = tok_y
        self.tok_yblast = tok_yb
        if self.dbg:
            p.dma("sync", self.dbg_t["d_y"][b, :, :, :], self.ysR[b, :, :, :],
                  after=[tok_y])

    def p2(self, b):
        p = self.p
        ET, NH, S, TCH, E, HF = self.ET, self.NH, self.S, self.TCH, self.E, self.HF
        hid = self.A8
        FSLAB = self.FSLAB
        NSL = self.NSL
        NHT = FSLAB // 128
        NBUF = 4
        w1sl = [self.B8[:, i * ET * FSLAB:(i + 1) * ET * FSLAB] for i in range(NBUF)]
        w3sl = [self.B8[:, (NBUF + i) * ET * FSLAB:(NBUF + 1 + i) * ET * FSLAB]
                for i in range(NBUF)]
        W2Q = min(2048, HF // 2)
        NW2 = HF // W2Q
        w2sl = [self.W2b[:, 0:(W2Q // 128) * 128],
                self.W2b[:, (W2Q // 128) * 128:2 * (W2Q // 128) * 128]]
        r_mean = self.rows[0:1, 0:TCH]
        r_m2 = self.rows[0:1, TCH:2 * TCH]
        r_tmp = self.rows[0:1, 2 * TCH:3 * TCH]
        r_rstd = self.rows[0:1, 3 * TCH:4 * TCH]
        tok_z = None
        pend_ln = None
        gate_tok = {}
        hmul = {}
        silu_tok = {}
        lcp_tok = {}
        w2_slab_prev = [None, None]
        w2_last_of_chunk = {}
        tok_tr = [None, None]
        tok_tc2 = None
        tok_ap_last = None
        for ch in range(self.NCH):
            t0 = ch * TCH
            t_ybf = p.dma("sync",
                          self.C2[:, 0:ET * TCH],
                          self.ysb[b, ch, :, :],
                          after=[self._tok_ys[(b, ch)][1],
                                 gate_tok.get(ch - 1, None)])
            t_yres = None
            t_w1 = [None] * NBUF
            t_w3 = [None] * NBUF
            lastmm = None
            for si in range(NSL):
                par = si % NBUF
                conv_w = self.tok_conv_p2 if (b == 0 and ch == 0) else []
                t_w1[par] = p.dma("gpsimd",
                                  w1sl[par].rearrange("p (a o) -> p a o", a=ET),
                                  self.w1R[si, :, :, :],
                                  after=[gate_tok.get((ch, si - NBUF, par), None),
                                         w2_last_of_chunk.get(ch - 1, None)
                                         if si < NBUF else None] + conv_w)
                t_w3[par] = p.dma("gpsimd",
                                  w3sl[par].rearrange("p (a o) -> p a o", a=ET),
                                  self.w3R[si, :, :, :],
                                  after=[gate_tok.get((ch, si - NBUF, par), None),
                                         w2_last_of_chunk.get(ch - 1, None)
                                         if si < NBUF else None] + conv_w)
                for ht in range(NHT):
                    hg = si * NHT + ht
                    pg = self.pA(hg % 2)
                    pl = self.pA(2 + hg % 2)
                    lg, ll = None, None
                    for dt in range(ET):
                        lg = p.mm(pg[:, 0:TCH], w1sl[par][:, dt * FSLAB + ht * 128:
                                                dt * FSLAB + ht * 128 + 128],
                                  self.C2[:, dt * TCH:(dt + 1) * TCH],
                                  start=(dt == 0), stop=(dt == ET - 1),
                                  after=[t_w1[par], t_ybf,
                                         silu_tok.get((ch, hg - 2), None)],
                                  inc=(dt == ET - 1))
                    for dt in range(ET):
                        ll = p.mm(pl[:, 0:TCH], w3sl[par][:, dt * FSLAB + ht * 128:
                                                dt * FSLAB + ht * 128 + 128],
                                  self.C2[:, dt * TCH:(dt + 1) * TCH],
                                  start=(dt == 0), stop=(dt == ET - 1),
                                  after=[t_w3[par], t_ybf,
                                         lcp_tok.get((ch, hg - 2), None)],
                                  inc=(dt == ET - 1))
                    ts = p.sop(lambda e, pg=pg, hg=hg: e.activation(
                        self.pbuf[hg % 2], pg[:, 0:TCH], AF.Silu,
                        bias=self.zcol(), scale=1.0),
                        after=[lg, hmul.get((ch, hg - 2), None)])
                    tl = p.vop(lambda e, pl=pl, hg=hg: e.tensor_copy(
                        self.abuf[hg % 2][:, 0:TCH], pl[:, 0:TCH]),
                        after=[ll, hmul.get((ch, hg - 2), None)])
                    tm = p.vop(lambda e, hg=hg: e.tensor_mul(
                        hid[:, hg * TCH:(hg + 1) * TCH],
                        self.pbuf[hg % 2], self.abuf[hg % 2][:, 0:TCH]),
                        after=[ts, tl, w2_last_of_chunk.get(ch - 1, None)])
                    silu_tok[(ch, hg)] = ts
                    lcp_tok[(ch, hg)] = tl
                    hmul[(ch, hg)] = tm
                    lastmm = ll
                gate_tok[(ch, si, par)] = lastmm
                if si == 0:
                    if pend_ln is not None:
                        pend_ln()
                        pend_ln = None
                    t_yres = p.dma("sync", self.M4[:, 0:ET * TCH],
                                   self.ysR[b, ch, :, :],
                                   after=[self._tok_ys[(b, ch)][0], tok_z,
                                          tok_ap_last])
            gate_tok[ch] = lastmm
            mm_mean_prev, mm_var_prev = None, None
            zevs = []
            for dt in range(ET):
                pz = self.pB(dt % 2)
                lastz = None
                for q in range(NW2):
                    par = q % 2
                    tw2 = p.dma(
                        "sync", w2sl[par][:, 0:(W2Q // 128) * 128],
                        self.w2ppR[dt, :, q * (W2Q // 128):(q + 1) * (W2Q // 128), :]
                        .rearrange("p j d -> p (j d)"),
                        after=[w2_slab_prev[par]] + self.tok_w2pp)
                    for j in range(W2Q // 128):
                        hg = q * (W2Q // 128) + j
                        lastz = p.mm(pz[:, 0:TCH], w2sl[par][:, j * 128:(j + 1) * 128],
                                     hid[:, hg * TCH:(hg + 1) * TCH],
                                     start=(hg == 0), stop=(hg == HF // 128 - 1),
                                     after=[tw2, hmul[(ch, hg)],
                                            zevs[dt - 2] if dt >= 2 else None,
                                            tok_tc2 if (hg == 0 and dt < 2)
                                            else None],
                                     inc=(j == W2Q // 128 - 1))
                    w2_slab_prev[par] = lastz
                zt = self.M4[:, dt * TCH:(dt + 1) * TCH]
                tz1 = p.vop(lambda e, zt=zt, pz=pz: e.tensor_add(zt, zt,
                                                                 pz[:, 0:TCH]),
                            after=[lastz, t_yres])
                zevs.append(tz1)
                tca = p.sop(lambda e, zt=zt, dt=dt: e.copy(
                    self.abuf[dt % 2][:, :], zt),
                    after=[tz1, mm_mean_prev, tok_tr[0],
                           silu_tok.get((ch, HF // 128 - 2 + dt), None)])
                tsq = p.vop(lambda e, dt=dt: e.tensor_mul(
                    self.qbuf[dt % 2][:, 0:TCH], self.abuf[dt % 2][:, :],
                    self.abuf[dt % 2][:, :]), after=[tca, mm_var_prev, tok_tr[1]])
                mm_mean_prev = p.mm(self.pB(2)[0:1, 0:TCH], self.onescs[:, 0:1],
                                    self.abuf[dt % 2][:, :],
                                    start=(dt == 0), stop=(dt == ET - 1),
                                    after=[tca], inc=True)
                mm_var_prev = p.mm(self.pB(3)[0:1, 0:TCH], self.onescs[:, 0:1],
                                   self.qbuf[dt % 2][:, 0:TCH],
                                   start=(dt == 0), stop=(dt == ET - 1),
                                   after=[tsq], inc=True)
            w2_last_of_chunk[ch] = w2_slab_prev[(NW2 - 1) % 2]

            def emit_ln2(ch=ch, t0=t0, mm_mean_prev=mm_mean_prev,
                         mm_var_prev=mm_var_prev, zevs=tuple(zevs)):
                nonlocal tok_z, tok_ap_last, tok_tc2, tok_tr
                tr1 = p.sop(lambda e: e.activation(r_mean, self.pB(2)[0:1, 0:TCH],
                                                   AF.Copy, bias=0.0, scale=1.0 / E),
                            after=[mm_mean_prev])
                tr2 = p.sop(lambda e: e.activation(r_m2, self.pB(3)[0:1, 0:TCH],
                                                   AF.Copy, bias=0.0, scale=1.0 / E),
                            after=[mm_var_prev])
                tok_tr = [tr1, tr2]
                tr3 = p.vop(lambda e: e.tensor_mul(r_tmp, r_mean, r_mean), after=[tr1])
                tr4 = p.vop(lambda e: e.tensor_sub(r_tmp, r_m2, r_tmp),
                            after=[tr2, tr3])
                tr5 = p.sop(lambda e: e.activation(r_tmp, r_tmp, AF.Ln,
                                                   bias=self.eps_row(), scale=1.0),
                            after=[tr4])
                tr6 = p.sop(lambda e: e.activation(r_rstd, r_tmp, AF.Exp,
                                                   bias=self.zrow(), scale=-0.5),
                            after=[tr5])
                tr7 = p.vop(lambda e: e.tensor_mul(r_tmp, r_mean, r_rstd), after=[tr6])
                tb1 = p.mm(self.pB(1)[:, 0:TCH], self.onesrs[0:1, :], r_rstd,
                           start=True, stop=True, after=[tr6, zevs[-1], tr2])
                tc1 = p.sop(lambda e: e.copy(self.bc[0][:, 0:TCH],
                                             self.pB(1)[:, 0:TCH]),
                            after=[tb1, tok_ap_last])
                tb2 = p.mm(self.pB(1)[:, 0:TCH], self.onesrs[0:1, :], r_tmp,
                           start=True, stop=True, after=[tr7, tc1])
                tc2 = p.sop(lambda e: e.copy(self.bc[1][:, 0:TCH],
                                             self.pB(1)[:, 0:TCH]),
                            after=[tb2, tok_tc2])
                tok_tc2 = tc2
                ap_toks = []
                for i in range(ET):
                    zt = self.M4[:, i * TCH:(i + 1) * TCH]
                    ta1 = p.vop(lambda e, zt=zt: e.tensor_mul(
                        zt, zt, self.bc[0][:, 0:TCH]), after=[tc1])
                    ap_toks.append(p.vop(lambda e, zt=zt: e.tensor_sub(
                        zt, zt, self.bc[1][:, 0:TCH]), after=[ta1, tc2]))
                tok_ap_last = ap_toks[-1]
                tok_z = p.dma("sync", self.zT[b, :, t0:t0 + TCH].rearrange(
                    "(a p) t -> p a t", p=128),
                    self.M4[:, 0:ET * TCH].rearrange("p (a t) -> p a t", a=ET),
                    after=ap_toks)
            pend_ln = emit_ln2
        pend_ln()
        pend_ln = None
        self.tok_zlast = tok_z

    def build(self):
        p = self.p
        upto = self.cfg.get("phases", "all")
        self.p0()
        if upto != "p0":
            for b in range(self.B):
                self.p1a(b)
                p.barrier()
                if upto == "p1a":
                    break
                self.p1b(b)
                p.barrier()
                if upto == "p1b":
                    break
                self.p1c(b)
                p.barrier()
                if upto == "p1c":
                    break
                self.p2(b)
                p.barrier()
        p.barrier()
        p.emit(self.es)
        self.es.close()
        return self.nc


def head_perm(HD):
    nope = HD // 4
    xe = np.arange(nope, HD, 2)
    xo = np.arange(nope + 1, HD, 2)
    return np.concatenate([xe, np.arange(0, nope // 2), xo,
                           np.arange(nope // 2, nope)])


def host_consts(cfg):
    S, E, HD, HF, TCH = cfg["S"], cfg["E"], cfg["HD"], cfg["HF"], cfg["TCH"]
    HA = HF // 128
    ND = TCH // 128
    nope = HD // 4
    rope_d = HD - nope
    freqs = (np.float32(1.0) / (np.float32(10000.0) **
             (np.arange(0, rope_d, 2, dtype=np.float32) / np.float32(rope_d))))
    emb = np.arange(S, dtype=np.float32)[None, :] * freqs[:, None].astype(np.float32)
    trig = np.zeros((128, S), dtype=np.float32)
    trig[0:48] = np.cos(emb)
    trig[64:112] = np.sin(emb)

    def syl(n):
        H = np.array([[1.0]], dtype=np.float32)
        while H.shape[0] < n:
            H = np.block([[H, H], [H, -H]])
        return H

    h128 = syl(128).astype(BFD)
    h64 = syl(HA).astype(BFD)
    onesc = np.ones((128, 1), dtype=BFD)
    onesr = np.ones((1, 128), dtype=np.float32)
    nrow = np.full((1, 128), -1.0 / np.sqrt(HD), dtype=np.float32)
    k = np.arange(128)[:, None]
    c = np.arange((ND - 1) * 128 + TCH)[None, :]
    masks = (c >= k + (ND - 1) * 128).astype(BFD)
    misc = np.zeros((128, 4), dtype=np.float32)
    misc[:, 0] = 1e-5
    return dict(trigc=trig, h64c=np.ascontiguousarray(h64),
                h128c=np.ascontiguousarray(h128), onescd=onesc, onesrd=onesr,
                nrowd=nrow, masksd=masks, miscd=misc)


def host_in_maps(cfg, n_cores, x, Wq, Wkv, Wku, Wvu, W1, W2, W3):
    E, HD = cfg["E"], cfg["HD"]
    NH = E // HD
    perm = head_perm(HD)
    full_perm = np.concatenate([h * HD + perm for h in range(NH)])
    consts = host_consts(cfg)
    wq_p = np.ascontiguousarray(Wq[:, full_perm])
    wku_p = np.ascontiguousarray(Wku[:, full_perm])
    w1t = np.ascontiguousarray(W1.T)
    w3t = np.ascontiguousarray(W3.T)
    w2t = np.ascontiguousarray(W2.T)
    B = cfg["B"]
    maps = []
    for c in range(n_cores):
        xs = np.ascontiguousarray(x[c * B:(c + 1) * B].transpose(0, 2, 1))
        m = dict(xT=xs, wq=wq_p, wkv=np.ascontiguousarray(Wkv),
                 wku=wku_p, wvu=np.ascontiguousarray(Wvu),
                 w1t=w1t, w3t=w3t, w2t=w2t, **consts)
        maps.append(m)
    return maps


def kernel(L0_attn_x, L0_attn_Wq, L0_attn_Wkv_down, L0_attn_Wk_up, L0_attn_Wv_up,
           L0_ffn_W1, L0_ffn_W2, L0_ffn_W3):
    cfg = mk_cfg()
    x = np.asarray(L0_attn_x, dtype=np.float32)
    maps = host_in_maps(cfg, N_CORES, x,
                        np.asarray(L0_attn_Wq, np.float32),
                        np.asarray(L0_attn_Wkv_down, np.float32),
                        np.asarray(L0_attn_Wk_up, np.float32),
                        np.asarray(L0_attn_Wv_up, np.float32),
                        np.asarray(L0_ffn_W1, np.float32),
                        np.asarray(L0_ffn_W2, np.float32),
                        np.asarray(L0_ffn_W3, np.float32))
    nc = Builder(cfg).build()
    res = run_bass_kernel_spmd(nc, maps, core_ids=list(range(N_CORES)))
    B = cfg["B"]
    outs = []
    for c in range(N_CORES):
        zt = res.results[c]["zT"]
        outs.append(np.ascontiguousarray(zt.transpose(0, 2, 1)))
    return np.concatenate(outs, axis=0).astype(np.float32)


# version 15
# speedup vs baseline: 1.0515x; 1.0515x over previous
import contextlib
import numpy as np
import ml_dtypes
import concourse.bass as bass
import concourse.mybir as mybir
from concourse.bass_utils import run_bass_kernel_spmd

F32 = mybir.dt.float32
BF16 = mybir.dt.bfloat16
AF = mybir.ActivationFunctionType
ALU = mybir.AluOpType
AX = mybir.AxisListType
BFD = ml_dtypes.bfloat16

N_CORES = 8
ENGINES = ("sync", "gpsimd", "vector", "scalar", "tensor")


def mk_cfg(dev=False):
    if dev:
        return dict(B=1, S=1024, E=512, LAT=256, HF=1024, TCH=256, HD=128, dbg=True)
    return dict(B=2, S=2048, E=2048, LAT=512, HF=8192, TCH=512, HD=128, dbg=False)


NPOOL = 12


class Prog:
    def __init__(self, nc):
        self.nc = nc
        self.ops = {e: [] for e in ENGINES}
        self.sem_objs = {}
        self.counts = {s: 0 for s in ("sT", "sV", "sS", "sG", "sY")}
        for q in ("Y", "G", "S"):
            for i in range(NPOOL):
                self.counts[f"d{q}{i}"] = 0
        self.dma_idx = {"sync": 0, "gpsimd": 0, "scalar": 0}

    def op(self, engine, fn, after=(), inc=None, inc_amt=1):
        waits = {}
        for tok in after:
            if tok is None:
                continue
            s, v = tok
            if waits.get(s, 0) < v:
                waits[s] = v
        tok_out = None
        if inc is not None:
            self.counts[inc] += inc_amt
            tok_out = (inc, self.counts[inc])
        self.ops[engine].append((fn, waits, (inc, inc_amt) if inc else None))
        return tok_out

    QL = {"sync": "Y", "gpsimd": "G", "scalar": "S"}

    def dma(self, queue, out, in_, after=(), **kw):
        i = self.dma_idx[queue]
        self.dma_idx[queue] = i + 1
        semn = f"d{self.QL[queue]}{i % NPOOL}"
        after = list(after)
        if i >= NPOOL:
            after.append((semn, self.counts[semn]))
        return self.op(queue, lambda e: e.dma_start(out=out, in_=in_, **kw),
                       after=after, inc=semn, inc_amt=16)

    def mm(self, out, lhsT, rhs, start, stop, after=(), inc=None):
        if inc is None:
            inc = stop
        return self.op(
            "tensor",
            lambda e: e.matmul(out, lhsT, rhs, start=start, stop=stop),
            after=after, inc=("sT" if inc else None))

    def vop(self, fn, after=(), inc=True):
        return self.op("vector", fn, after=after, inc=("sV" if inc else None))

    def sop(self, fn, after=(), inc=True):
        return self.op("scalar", fn, after=after, inc=("sS" if inc else None))

    def gop(self, fn, after=(), inc=True):
        return self.op("gpsimd", fn, after=after, inc=("sG" if inc else None))

    def barrier(self):
        waits = {}
        for e, sem in (("tensor", "sT"), ("vector", "sV"), ("scalar", "sS"),
                       ("gpsimd", "sG"), ("sync", "sY")):
            if not self.ops[e]:
                continue
            fn, w, inc = self.ops[e][-1]
            if inc is None:
                self.counts[sem] += 1
                inc = (sem, 1)
                self.ops[e][-1] = (fn, w, inc)
                waits[sem] = self.counts[sem]
            else:
                waits[inc[0]] = self.counts[inc[0]]
        for s, v in self.counts.items():
            if s[0] == "d" and v:
                waits[s] = v
        for e in ENGINES:
            self.ops[e].append((lambda eng: eng.nop(), dict(waits), None))

    def emit(self, es):
        nc = self.nc
        for s in self.counts:
            self.sem_objs[s] = es.enter_context(nc.semaphore(s))
        block = es.enter_context(nc.Block())
        for eng_name in ENGINES:
            ops = self.ops[eng_name]

            def make_emitter(ops=ops):
                def emitter(eng):
                    last = {}
                    for fn, waits, inc in ops:
                        for s, v in waits.items():
                            if last.get(s, 0) < v:
                                eng.wait_ge(self.sem_objs[s], v)
                                last[s] = v
                        inst = fn(eng)
                        if inc is not None:
                            s, amt = inc
                            inst.then_inc(self.sem_objs[s], amt)
                return emitter

            getattr(block, eng_name)(make_emitter())


class Builder:
    def __init__(self, cfg):
        self.cfg = cfg
        for k, v in cfg.items():
            setattr(self, k, v)
        self.NH = self.E // self.HD
        self.ET = self.E // 128
        self.LT = self.LAT // 128
        self.HA = self.HF // 128
        self.ND = self.TCH // 128
        self.NCH = self.S // self.TCH
        self.SH = self.S // 2
        self.FSLAB = min(256, self.S // 8)
        self.NSL = self.HF // self.FSLAB
        self.SC = 1.0 / float(np.sqrt(self.HD))
        self.FSC = 1.0 / float(np.sqrt(self.HF))

        self.nc = bass.Bass("TRN2", target_bir_lowering=False)
        self.p = Prog(self.nc)
        self._rope_last = {}
        self.es = contextlib.ExitStack()
        self._declare_dram()
        self._declare_sbuf()

    def _declare_dram(self):
        nc, B, S, E, LAT, HF = self.nc, self.B, self.S, self.E, self.LAT, self.HF
        dt_ = nc.dram_tensor
        self.xT = dt_("xT", [B, E, S], F32, kind="ExternalInput").ap()
        self.wq = dt_("wq", [E, E], F32, kind="ExternalInput").ap()
        self.wkv = dt_("wkv", [E, LAT], F32, kind="ExternalInput").ap()
        self.wku = dt_("wku", [LAT, E], F32, kind="ExternalInput").ap()
        self.wvu = dt_("wvu", [LAT, E], F32, kind="ExternalInput").ap()
        self.w1t = dt_("w1t", [E, HF], F32, kind="ExternalInput").ap()
        self.w3t = dt_("w3t", [E, HF], F32, kind="ExternalInput").ap()
        self.w2t = dt_("w2t", [HF, E], F32, kind="ExternalInput").ap()
        self.trigc = dt_("trigc", [128, S], F32, kind="ExternalInput").ap()
        self.h64c = dt_("h64c", [self.HA, self.HA], BF16, kind="ExternalInput").ap()
        self.h128c = dt_("h128c", [128, 128], BF16, kind="ExternalInput").ap()
        self.onescd = dt_("onescd", [128, 1], BF16, kind="ExternalInput").ap()
        self.onesrd = dt_("onesrd", [1, 128], F32, kind="ExternalInput").ap()
        self.nrowd = dt_("nrowd", [1, 128], F32, kind="ExternalInput").ap()
        self.masksd = dt_("masksd", [128, (self.ND - 1) * 128 + self.TCH], BF16,
                          kind="ExternalInput").ap()
        self.miscd = dt_("miscd", [128, 4], F32, kind="ExternalInput").ap()

        ET, LT, NH, HA = self.ET, self.LT, self.NH, self.HA
        NQS = 2
        FSLAB = self.FSLAB
        NSL = self.NSL
        TCH, NCH = self.TCH, self.NCH
        self.w2s1 = dt_("w2s1", [HF, E], BF16).ap()
        self.w2ppR = dt_("w2ppR", [ET, 128, HA, 128], BF16).ap()
        self.qfs = dt_("qfs", [2, 128, self.NH, self.SH], BF16).ap()
        self.xb = dt_("xb", [B, ET, 128, S], BF16).ap()
        self.wqR = dt_("wqR", [NH // NQS, 128, ET, 128 * NQS], BF16).ap()
        self.wkvR = dt_("wkvR", [LT, 128, ET, 128], BF16).ap()
        self.wkuR = dt_("wkuR", [NH, 128, LT, 128], BF16).ap()
        self.wvuR = dt_("wvuR", [E // 512, 128, LT, 512], BF16).ap()
        self.w1R = dt_("w1R", [NSL, 128, ET, FSLAB], BF16).ap()
        self.w3R = dt_("w3R", [NSL, 128, ET, FSLAB], BF16).ap()
        self.ysR = dt_("ysR", [B, NCH, 128, ET * TCH], F32).ap()
        self.ysb = dt_("ysb", [B, NCH, 128, ET * TCH], BF16).ap()
        self.zT = dt_("zT", [B, E, S], F32, kind="ExternalOutput").ap()
        if self.dbg:
            self.dbg_t = {}
            for nm, shp, dt2 in [("d_cf", [128, self.LT * S], BF16),
                                 ("d_kf", [128, self.NH * S], BF16),
                                 ("d_vt", [128, (S // 128) * E], BF16),
                                 ("d_qf", [2, 128, self.NH, self.SH], BF16),
                                 ("d_negm", [128, self.NH], F32),
                                 ("d_y", [B, self.NCH, 128, self.ET * self.TCH], F32),
                                 ("d_w2pp", [self.ET, 128, self.HA, 128], BF16)]:
                self.dbg_t[nm] = dt_(nm, shp, dt2, kind="ExternalOutput").ap()

    def _declare_sbuf(self):
        es, nc = self.es, self.nc
        S, E, TCH, ET, LT, NH = self.S, self.E, self.TCH, self.ET, self.LT, self.NH
        sbt = nc.sbuf_tensor
        self.A8 = es.enter_context(sbt("A8", [128, ET * S], BF16))
        self.B8 = es.enter_context(sbt("B8", [128, ET * S], BF16))
        self.M4W = max(ET * TCH, 2 * S + 1024)
        self.M4 = es.enter_context(sbt("M4", [128, self.M4W], F32))
        self.C2 = es.enter_context(sbt("C2", [128, max(LT * S, ET * TCH)], BF16))
        self.W2b = es.enter_context(sbt("W2b", [128, 4096], BF16))
        self.pb4 = es.enter_context(sbt("pb4", [128, 4 * TCH], BF16))
        self.pb2h = [self.pb4[:, 0:2 * TCH], self.pb4[:, 2 * TCH:4 * TCH]]
        self.pbuf = [self.pb4[:, i * TCH:(i + 1) * TCH] for i in range(4)]
        self.abuf = [es.enter_context(sbt(f"ab{i}", [128, TCH], BF16)) for i in range(2)]
        self.qbuf = [es.enter_context(sbt(f"qb{i}", [128, max(TCH, 512)], BF16))
                     for i in range(2)]
        self.bc = [es.enter_context(sbt(f"bc{i}", [128, max(TCH, 512)], F32))
                   for i in range(2)]
        self.h64s = es.enter_context(sbt("h64s", [self.HA, self.HA], BF16))
        self.h128s = es.enter_context(sbt("h128s", [128, 128], BF16))
        self.onescs = es.enter_context(sbt("onescs", [128, 1], BF16))
        self.onesrs = es.enter_context(sbt("onesrs", [1, 128], F32))
        self.nrows = es.enter_context(sbt("nrows", [1, 128], F32))
        self.maskss = es.enter_context(
            sbt("maskss", [128, (self.ND - 1) * 128 + TCH], BF16))
        self.miscs = es.enter_context(sbt("miscs", [128, 4], F32))
        self.rows = es.enter_context(sbt("rows", [1, 4 * TCH], F32))
        self.t11 = es.enter_context(sbt("t11", [1, 2], F32))
        self.nq2 = es.enter_context(sbt("nq2", [1, NH], F32))
        self.nk2 = es.enter_context(sbt("nk2", [1, NH], F32))
        self.mrow = es.enter_context(sbt("mrow", [1, NH], F32))
        self.negm = es.enter_context(sbt("negm", [128, NH], F32))
        self.psA = es.enter_context(nc.psum_tensor("psA", [128, 2048], F32))
        self.psB = es.enter_context(nc.psum_tensor("psB", [128, 2048], F32))
        self.trig = self.M4[:, self.M4W - S: self.M4W]
        self.trig2 = self.M4[:, self.M4W - 2 * S: self.M4W - S]

    def pA(self, i):
        return self.psA[:, i * 512:(i + 1) * 512]

    def pB(self, i):
        return self.psB[:, i * 512:(i + 1) * 512]

    def zcol(self):
        return self.miscs[:, 1:2]

    def zrow(self):
        return self.miscs[0:1, 1:2]

    def eps_row(self):
        return self.miscs[0:1, 0:1]

    def _conv_init(self, base, width):
        self._cv_base, self._cv_w = base, width
        self._cv_par = 0
        self._cv_tok = [None, None]

    def _conv_slab(self, src, dst, shp):
        p = self.p
        n = int(np.prod(shp))
        assert n <= self._cv_w, (n, self._cv_w)
        par = self._cv_par
        self._cv_par ^= 1
        h = self.M4[:, self._cv_base + par * self._cv_w:
                    self._cv_base + par * self._cv_w + n]
        if len(shp) == 2:
            hv = h.rearrange("p (x y) -> p x y", x=shp[0])
        else:
            hv = h
        tin = p.dma("sync", hv, src, after=[self._cv_tok[par]])
        tout = p.dma("gpsimd", dst, hv, after=[tin])
        self._cv_tok[par] = tout
        return tout

    def _conv_part1(self):
        ET, S, LT, NH = self.ET, self.S, self.LT, self.NH
        W = min(4096, self.M4W // 2)
        self._conv_init(0, W)
        toks = {}
        XA = max(1, W // S)
        for a0 in range(0, ET, XA):
            toks["xb0"] = self._conv_slab(
                self.xT[0, a0 * 128:(a0 + XA) * 128, :].rearrange(
                    "(a p) t -> p a t", p=128),
                self.xb[0, a0:a0 + XA, :, :].rearrange("a p t -> p a t"),
                (XA, S))
        for lt in range(LT):
            toks["wkv"] = self._conv_slab(
                self.wkv[:, lt * 128:(lt + 1) * 128].rearrange(
                    "(a p) o -> p a o", p=128),
                self.wkvR[lt, :, :, :], (ET, 128))
        for sl in range(NH // 2):
            toks["wq"] = self._conv_slab(
                self.wq[:, sl * 256:(sl + 1) * 256].rearrange(
                    "(a p) o -> p a o", p=128),
                self.wqR[sl, :, :, :], (ET, 256))
        for h in range(NH):
            toks["wku"] = self._conv_slab(
                self.wku[:, h * 128:(h + 1) * 128].rearrange(
                    "(a p) o -> p a o", p=128),
                self.wkuR[h, :, :, :], (LT, 128))
        for dch in range(self.E // 512):
            toks["wvu"] = self._conv_slab(
                self.wvu[:, dch * 512:(dch + 1) * 512].rearrange(
                    "(a p) o -> p a o", p=128),
                self.wvuR[dch, :, :, :], (LT, 512))
        self.tok_img = toks
        self.tok_conv_p1 = self._cv_tok[:]

    def _conv_part2_fill(self):
        ET, S = self.ET, self.S
        FSLAB = self.FSLAB
        NSL = self.NSL
        W = min(2048, (self.M4W - 2 * self.S) // 2)
        self._conv_w2 = W
        AW = max(1, W // FSLAB)
        jobs = []
        for b in range(1, self.B):
            for a in range(ET):
                jobs.append((self.xT[b, a * 128:(a + 1) * 128, :],
                             self.xb[b, a, :, :], (S,)))
        for wsrc, wdst in ((self.w1t, self.w1R), (self.w3t, self.w3R)):
            for si in range(NSL):
                for a0 in range(0, ET, AW):
                    jobs.append((
                        wsrc[a0 * 128:(a0 + AW) * 128,
                             si * FSLAB:(si + 1) * FSLAB].rearrange(
                            "(a p) o -> p a o", p=128),
                        wdst[si, :, a0:a0 + AW, :], (AW, FSLAB)))
        self._conv_jobs = jobs
        self._conv_ji = 0

    def _pump_conv(self, n):
        while n > 0 and self._conv_ji < len(self._conv_jobs):
            src, dst, shp = self._conv_jobs[self._conv_ji]
            self._conv_ji += 1
            self._conv_slab(src, dst, shp)
            n -= 1

    def _drain_conv(self):
        self._pump_conv(10 ** 9)
        self.tok_conv_p2 = self._cv_tok[:]

    def p0(self):
        p = self.p
        t = []
        t.append(p.dma("sync", self.h64s[:, :], self.h64c[:, :]))
        t.append(p.dma("sync", self.h128s[:, :], self.h128c[:, :]))
        t.append(p.dma("sync", self.onescs[:, :], self.onescd[:, :]))
        t.append(p.dma("sync", self.onesrs[:, :], self.onesrd[:, :]))
        t.append(p.dma("sync", self.nrows[:, :], self.nrowd[:, :]))
        t.append(p.dma("sync", self.maskss[:, :], self.masksd[:, :]))
        t.append(p.dma("sync", self.miscs[:, :], self.miscd[:, :]))
        self.tok_consts = list(t)
        self._conv_part1()
        self._fold_w2()
        p.barrier()

    def _fold_w2(self):
        p, HA, E = self.p, self.HA, self.E
        DCW = min(64, (self.ET * self.S) // 512)
        fins1 = [self.A8[:, i * 128 * DCW:(i + 1) * 128 * DCW] for i in range(2)]
        fouts1 = [self.B8[:, i * 128 * DCW:(i + 1) * 128 * DCW] for i in range(2)]
        fins2 = [self.A8[:, (2 + i) * 128 * DCW:(3 + i) * 128 * DCW] for i in range(2)]
        fouts2 = [self.B8[:, (2 + i) * 128 * DCW:(3 + i) * 128 * DCW] for i in range(2)]
        DC = E // DCW
        w1s = min(512, 128 * DCW)
        w2s = min(512, HA * DCW)
        s1_done = {}
        ev1, st1, ev2, st2 = {}, {}, {}, {}
        st_all = []

        def stage1(dc):
            par = dc % 2
            fin, fout = fins1[par], fouts1[par]
            src_ = self.w2t[:, dc * DCW:(dc + 1) * DCW].rearrange(
                "(a b) d -> a b d", a=HA)
            tin = p.dma("gpsimd", fin[0:HA, :].rearrange("a (b d) -> a b d", b=128),
                        src_, after=[ev1.get(dc - 2, None), self.tok_consts[0]])
            evs = []
            for j in range((128 * DCW) // w1s):
                tmm = p.mm(self.pA(j % 2)[0:HA, 0:w1s], self.h64s[:, :],
                           fin[0:HA, j * w1s:(j + 1) * w1s],
                           start=True, stop=True,
                           after=[tin] + ([evs[-2]] if len(evs) >= 2
                                          else [ev1.get(dc - 1, None)]))
                evs.append(p.sop(
                    lambda e, j=j, fout=fout: e.copy(
                        fout[0:HA, j * w1s:(j + 1) * w1s],
                        self.pA(j % 2)[0:HA, 0:w1s]),
                    after=[tmm, st1.get(dc - 2, None)]))
            ev1[dc] = evs[-1]
            dst = self.w2s1[:, dc * DCW:(dc + 1) * DCW].rearrange(
                "(a b) d -> a b d", a=HA)
            st1[dc] = p.dma(
                "sync", dst, fout[0:HA, :].rearrange("a (b d) -> a b d", b=128),
                after=[evs[-1]])
            s1_done[dc] = st1[dc]

        def stage2(dc):
            par = dc % 2
            fin, fout = fins2[par], fouts2[par]
            src_ = self.w2s1[:, dc * DCW:(dc + 1) * DCW].rearrange(
                "(a b) d -> b a d", b=128)
            tin = p.dma("gpsimd",
                        fin[:, 0:HA * DCW].rearrange("b (a d) -> b a d", a=HA),
                        src_, after=[ev2.get(dc - 2, None), s1_done[dc]])
            evs = []
            for j in range((HA * DCW) // w2s):
                tmm = p.mm(self.pA(2 + j % 2)[:, 0:w2s], self.h128s[:, :],
                           fin[:, j * w2s:(j + 1) * w2s],
                           start=True, stop=True,
                           after=[tin] + ([evs[-2]] if len(evs) >= 2
                                          else [ev2.get(dc - 1, None)]))
                evs.append(p.sop(
                    lambda e, j=j, fout=fout: e.activation(
                        fout[:, j * w2s:(j + 1) * w2s],
                        self.pA(2 + j % 2)[:, 0:w2s], AF.Copy,
                        bias=0.0, scale=self.FSC),
                    after=[tmm, st2.get(dc - 2, None)]))
            ev2[dc] = evs[-1]
            dt0 = (dc * DCW) // 128
            off = (dc * DCW) % 128
            dst = self.w2ppR[dt0, :, :, off:off + DCW]
            st2[dc] = p.dma(
                "sync", dst, fout[:, 0:HA * DCW].rearrange("b (a d) -> b a d", a=HA),
                after=[evs[-1]])
            st_all.append(st2[dc])

        for dc in range(DC):
            stage1(dc)
            if dc >= 1:
                stage2(dc - 1)
        stage2(DC - 1)
        self.tok_w2pp = st_all[-NPOOL:]
        if self.dbg:
            self.tok_w2pp = [p.dma("sync", self.dbg_t["d_w2pp"][:, :, :, :],
                                   self.w2ppR[:, :, :, :], after=self.tok_w2pp)]

    def _conv_init(self, base, width):
        self._cv_base, self._cv_w = base, width
        self._cv_par = 0
        self._cv_tok = [None, None]

    def _conv_slab(self, src, dst, shp):
        p = self.p
        n = int(np.prod(shp))
        assert n <= self._cv_w, (n, self._cv_w)
        par = self._cv_par
        self._cv_par ^= 1
        h = self.M4[:, self._cv_base + par * self._cv_w:
                    self._cv_base + par * self._cv_w + n]
        if len(shp) == 2:
            hv = h.rearrange("p (x y) -> p x y", x=shp[0])
        else:
            hv = h
        tin = p.dma("sync", hv, src, after=[self._cv_tok[par]])
        tout = p.dma("gpsimd", dst, hv, after=[tin])
        self._cv_tok[par] = tout
        return tout

    def _conv_part1(self):
        ET, S, LT, NH = self.ET, self.S, self.LT, self.NH
        W = min(4096, self.M4W // 2)
        self._conv_init(0, W)
        toks = {}
        XA = max(1, W // S)
        for a0 in range(0, ET, XA):
            toks["xb0"] = self._conv_slab(
                self.xT[0, a0 * 128:(a0 + XA) * 128, :].rearrange(
                    "(a p) t -> p a t", p=128),
                self.xb[0, a0:a0 + XA, :, :].rearrange("a p t -> p a t"),
                (XA, S))
        for lt in range(LT):
            toks["wkv"] = self._conv_slab(
                self.wkv[:, lt * 128:(lt + 1) * 128].rearrange(
                    "(a p) o -> p a o", p=128),
                self.wkvR[lt, :, :, :], (ET, 128))
        for sl in range(NH // 2):
            toks["wq"] = self._conv_slab(
                self.wq[:, sl * 256:(sl + 1) * 256].rearrange(
                    "(a p) o -> p a o", p=128),
                self.wqR[sl, :, :, :], (ET, 256))
        for h in range(NH):
            toks["wku"] = self._conv_slab(
                self.wku[:, h * 128:(h + 1) * 128].rearrange(
                    "(a p) o -> p a o", p=128),
                self.wkuR[h, :, :, :], (LT, 128))
        for dch in range(self.E // 512):
            toks["wvu"] = self._conv_slab(
                self.wvu[:, dch * 512:(dch + 1) * 512].rearrange(
                    "(a p) o -> p a o", p=128),
                self.wvuR[dch, :, :, :], (LT, 512))
        self.tok_img = toks
        self.tok_conv_p1 = self._cv_tok[:]

    def _conv_part2_fill(self):
        ET, S = self.ET, self.S
        FSLAB = self.FSLAB
        NSL = self.NSL
        W = min(2048, (self.M4W - 2 * self.S) // 2)
        self._conv_w2 = W
        AW = max(1, W // FSLAB)
        jobs = []
        for b in range(1, self.B):
            for a in range(ET):
                jobs.append((self.xT[b, a * 128:(a + 1) * 128, :],
                             self.xb[b, a, :, :], (S,)))
        for wsrc, wdst in ((self.w1t, self.w1R), (self.w3t, self.w3R)):
            for si in range(NSL):
                for a0 in range(0, ET, AW):
                    jobs.append((
                        wsrc[a0 * 128:(a0 + AW) * 128,
                             si * FSLAB:(si + 1) * FSLAB].rearrange(
                            "(a p) o -> p a o", p=128),
                        wdst[si, :, a0:a0 + AW, :], (AW, FSLAB)))
        self._conv_jobs = jobs
        self._conv_ji = 0

    def _pump_conv(self, n):
        while n > 0 and self._conv_ji < len(self._conv_jobs):
            src, dst, shp = self._conv_jobs[self._conv_ji]
            self._conv_ji += 1
            self._conv_slab(src, dst, shp)
            n -= 1

    def _drain_conv(self):
        self._pump_conv(10 ** 9)
        self.tok_conv_p2 = self._cv_tok[:]

    def p0(self):
        p = self.p
        t = []
        t.append(p.dma("sync", self.h64s[:, :], self.h64c[:, :]))
        t.append(p.dma("sync", self.h128s[:, :], self.h128c[:, :]))
        t.append(p.dma("sync", self.onescs[:, :], self.onescd[:, :]))
        t.append(p.dma("sync", self.onesrs[:, :], self.onesrd[:, :]))
        t.append(p.dma("sync", self.nrows[:, :], self.nrowd[:, :]))
        t.append(p.dma("sync", self.maskss[:, :], self.masksd[:, :]))
        t.append(p.dma("sync", self.miscs[:, :], self.miscd[:, :]))
        self.tok_consts = list(t)
        self._conv_part1()
        self._fold_w2()
        p.barrier()

    def _fold_w2(self):
        p, HA, E = self.p, self.HA, self.E
        DCW = min(128, (self.ET * self.S) // 256)
        fins = [self.A8[:, 0:128 * DCW], self.A8[:, 128 * DCW:256 * DCW]]
        fouts = [self.B8[:, 0:128 * DCW], self.B8[:, 128 * DCW:256 * DCW]]
        DC = E // DCW
        w1s = min(512, 128 * DCW)
        w2s = min(512, HA * DCW)
        s1_done = {}
        mm_by = {}
        ev_by = {}
        st_by = {}
        for dc in range(DC):
            par = dc % 2
            fin, fout = fins[par], fouts[par]
            src = self.w2t[:, dc * DCW:(dc + 1) * DCW].rearrange(
                "(a b) d -> a b d", a=HA)
            tin = p.dma("gpsimd", fin[0:HA, :].rearrange("a (b d) -> a b d", b=128),
                        src, after=[ev_by.get(dc - 2, None), self.tok_consts[0]])
            evs = []
            for j in range((128 * DCW) // w1s):
                tmm = p.mm(self.pA(j % 4)[0:HA, 0:w1s], self.h64s[:, :],
                           fin[0:HA, j * w1s:(j + 1) * w1s],
                           start=True, stop=True,
                           after=[tin] + ([evs[-4]] if len(evs) >= 4
                                          else [ev_by.get(dc - 1, None)]))
                evs.append(p.sop(
                    lambda e, j=j, fout=fout: e.copy(
                        fout[0:HA, j * w1s:(j + 1) * w1s],
                        self.pA(j % 4)[0:HA, 0:w1s]),
                    after=[tmm, st_by.get(dc - 2, None)]))
            ev_by[dc] = evs[-1]
            dst = self.w2s1[:, dc * DCW:(dc + 1) * DCW].rearrange(
                "(a b) d -> a b d", a=HA)
            st_by[dc] = p.dma(
                "sync", dst, fout[0:HA, :].rearrange("a (b d) -> a b d", b=128),
                after=[evs[-1]])
            s1_done[dc] = st_by[dc]
        s1_ev_final = ev_by[DC - 1]
        ev_by, st_by = {-1: s1_ev_final, -2: s1_ev_final}, {}
        last_st = None
        for dc in range(DC):
            par = dc % 2
            fin, fout = fins[par], fouts[par]
            src = self.w2s1[:, dc * DCW:(dc + 1) * DCW].rearrange(
                "(a b) d -> b a d", b=128)
            tin = p.dma("gpsimd",
                        fin[:, 0:HA * DCW].rearrange("b (a d) -> b a d", a=HA),
                        src, after=[ev_by.get(dc - 2), s1_done[dc]])
            evs = []
            for j in range((HA * DCW) // w2s):
                tmm = p.mm(self.pB(j % 4)[:, 0:w2s], self.h128s[:, :],
                           fin[:, j * w2s:(j + 1) * w2s],
                           start=True, stop=True,
                           after=[tin] + ([evs[-4]] if len(evs) >= 4
                                          else [ev_by.get(dc - 1)]))
                evs.append(p.sop(
                    lambda e, j=j, fout=fout: e.activation(
                        fout[:, j * w2s:(j + 1) * w2s],
                        self.pB(j % 4)[:, 0:w2s], AF.Copy,
                        bias=0.0, scale=self.FSC),
                    after=[tmm, st_by.get(dc - 2, s1_done[DC - 2 + (dc % 2)])]))
            ev_by[dc] = evs[-1]
            dt0 = (dc * DCW) // 128
            off = (dc * DCW) % 128
            dst = self.w2ppR[dt0, :, :, off:off + DCW]
            st_by[dc] = p.dma(
                "sync", dst, fout[:, 0:HA * DCW].rearrange("b (a d) -> b a d", a=HA),
                after=[evs[-1]])
        self.tok_w2pp = [st_by[dc] for dc in range(max(0, DC - NPOOL), DC)]
        if self.dbg:
            self.tok_w2pp = [p.dma("sync", self.dbg_t["d_w2pp"][:, :, :, :],
                                   self.w2ppR[:, :, :, :], after=self.tok_w2pp)]

    def _rope_evac(self, pq, dst, tcol, tok_mm, eng=0, parity=0,
                   sq_war=None):
        p = self.p
        cos = self.trig[0:48, tcol:tcol + 512]
        sin = self.trig[64:112, tcol:tcol + 512]
        sin2 = self.trig2[0:48, tcol:tcol + 512]
        cos2 = self.trig2[64:112, tcol:tcol + 512]
        T = self.bc[eng]
        ta, tb = T[64:112, 0:512], T[0:48, 0:512]
        op = p.vop if eng == 0 else p.gop
        t1 = p.sop(lambda e: e.copy(dst, pq[:, :]), after=[tok_mm])
        sq = self.qbuf[parity][:, 0:512]
        tsq = p.vop(lambda e: e.tensor_mul(sq, dst, dst),
                    after=[t1, sq_war])
        prev = self._rope_last.get(eng)
        o1 = op(lambda e: e.tensor_mul(ta, dst[0:48, :], sin2), after=[t1, prev])
        o2 = op(lambda e: e.tensor_mul(tb, dst[64:112, :], sin), after=[t1])
        o3 = op(lambda e: e.tensor_mul(dst[0:48, :], dst[0:48, :], cos),
                after=[o1, tsq])
        o4 = op(lambda e: e.tensor_sub(dst[0:48, :], dst[0:48, :], tb), after=[o2, o3])
        o5 = op(lambda e: e.tensor_mul(dst[64:112, :], dst[64:112, :], cos2),
                after=[o2, tsq])
        o6 = op(lambda e: e.tensor_add(dst[64:112, :], dst[64:112, :], ta),
                after=[o1, o5])
        self._rope_last[eng] = o6
        return t1, o6, tsq

    def _norm_fin(self, h, tc, ntc, parity, tsq_tok, nrm, prev_red):
        p = self.p
        sq = self.qbuf[parity][:, 0:512]
        tmm = p.mm(self.pB(3)[0:1, 0:512], self.onescs[:, 0:1], sq,
                   start=(tc == 0), stop=(tc == ntc - 1),
                   after=[tsq_tok, prev_red if tc == 0 else None], inc=True)
        if tc < ntc - 1:
            return None, tmm
        tmx = p.vop(lambda e: e.reduce_max(self.t11[0:1, 0:1],
                                           self.pB(3)[0:1, 0:512],
                                           axis=AX.X), after=[tmm])
        tac = p.vop(lambda e: e.tensor_max(nrm[0:1, h:h + 1], nrm[0:1, h:h + 1],
                                           self.t11[0:1, 0:1]), after=[tmx])
        return tac, tmx

    def p1a(self, b):
        p = self.p
        ET, SH, NH, LT, S = self.ET, self.SH, self.NH, self.LT, self.S
        xf = self.A8[:, 0:ET * SH]
        qfh = self.A8[:, ET * SH:2 * ET * SH]
        t_trig = p.dma("sync", self.trig, self.trigc[:, :],
                       after=(self.tok_conv_p1 if b == 0 else []))
        t_trig2a = p.dma("sync", self.trig2[64:112, :], self.trigc[0:48, :],
                         after=[t_trig])
        t_trig = p.dma("sync", self.trig2[0:48, :], self.trigc[64:112, :],
                       after=[t_trig2a])
        if b == 0:
            self._conv_part2_fill()
            self._conv_init(0, self._conv_w2)
        p.vop(lambda e: e.memset(self.nq2[0:1, :], 0.0))
        p.vop(lambda e: e.memset(self.nk2[0:1, :], 0.0))
        spill_tok = None
        cf_ev = None
        q_ev_last = None
        tac = None
        for half in range(2):
            t0 = half * SH
            t_xf = p.dma("sync", xf.rearrange("p (a t) -> p a t", a=ET),
                         self.xb[b, :, :, t0:t0 + SH].rearrange("a p t -> p a t"),
                         after=[spill_tok, q_ev_last, cf_ev,
                                self._rope_last.get(0), self._rope_last.get(1),
                                self.tok_img["xb0"] if b == 0 else None] +
                               (self.tok_conv_p2 if b > 0 else []))
            wslab = self.W2b[:, 0:ET * 128]
            cf_by = {}
            for lt in range(LT):
                t_w = p.dma("gpsimd", wslab.rearrange("p (a o) -> p a o", a=ET),
                            self.wkvR[lt, :, :, :],
                            after=[cf_ev, self.tok_img["wkv"]])
                if b == 0:
                    self._pump_conv(4)
                for tc in range(SH // 512):
                    gc = lt * (SH // 512) + tc
                    last = None
                    for dt in range(ET):
                        last = p.mm(
                            self.pA(gc % 2), wslab[:, dt * 128:(dt + 1) * 128],
                            xf[:, dt * SH + tc * 512: dt * SH + tc * 512 + 512],
                            start=(dt == 0), stop=(dt == ET - 1),
                            after=[t_w, t_xf, cf_by.get(gc - 2, None)])
                    cf_ev = p.sop(lambda e, lt=lt, t0=t0, tc=tc, gc=gc: e.copy(
                        self.C2[:, lt * S + t0 + tc * 512:
                                lt * S + t0 + tc * 512 + 512],
                        self.pA(gc % 2)[:, :]), after=[last])
                    cf_by[gc] = cf_ev
            NQS = 2
            qsl = [self.B8[:, 0:ET * 128 * NQS],
                   self.B8[:, ET * 128 * NQS:2 * ET * 128 * NQS]]
            t_qs = [None, None]
            ev_by_h = {}
            nrm_mm_prev = None
            pend_norm = None
            prev_red = None
            for h in range(NH):
                sbi = (h // NQS) % 2
                if h % NQS == 0:
                    t_qs[sbi] = p.dma(
                        "gpsimd", qsl[sbi].rearrange("p (a o) -> p a o", a=ET),
                        self.wqR[h // NQS, :, :, :],
                        after=[ev_by_h.get(h - NQS - 1, None),
                               self.tok_img["wq"]])
                    if b == 0:
                        self._pump_conv(5)
                for tc in range(SH // 512):
                    gq = h * (SH // 512) + tc
                    pq = self.pA(2 + gq % 2)
                    last = None
                    for dt in range(ET):
                        lhsT = qsl[sbi][:, dt * 128 * NQS + (h % NQS) * 128:
                                        dt * 128 * NQS + (h % NQS) * 128 + 128]
                        last = p.mm(pq, lhsT,
                                    xf[:, dt * SH + tc * 512:dt * SH + tc * 512 + 512],
                                    start=(dt == 0), stop=(dt == ET - 1),
                                    after=[t_qs[sbi], t_xf, t_trig,
                                           ev_by_h.get(("w", gq - 2), None)])
                    dst = qfh[:, h * SH + tc * 512: h * SH + tc * 512 + 512]
                    ev_t1, ev, tsq = self._rope_evac(
                        pq, dst, t0 + tc * 512, last, eng=(gq % 5 >= 3),
                        parity=gq % 2, sq_war=nrm_mm_prev)
                    ev_by_h[("w", gq)] = ev_t1
                    ev_by_h[h] = ev
                    if pend_norm is not None:
                        tac2, nrm_mm_prev = self._norm_fin(*pend_norm,
                                                           self.nq2, prev_red)
                        if tac2 is not None:
                            tac, prev_red = tac2, tac2
                    pend_norm = (h, tc, SH // 512, gq % 2, tsq)
            if pend_norm is not None:
                tac2, nrm_mm_prev = self._norm_fin(*pend_norm, self.nq2, prev_red)
                if tac2 is not None:
                    tac = tac2
                pend_norm = None
            q_ev_last = ev_by_h[NH - 1]
            spill_tok = p.dma("sync", self.qfs[half, :, :, :],
                              qfh.rearrange("p (h t) -> p h t", h=NH),
                              after=[q_ev_last, tac, self._rope_last.get(0),
                                     self._rope_last.get(1)])
        self.tok_cf = cf_ev
        self.tok_spill = spill_tok
        if self.dbg:
            p.dma("sync", self.dbg_t["d_cf"][:, :], self.C2[:, 0:LT * S],
                  after=[cf_ev])
            for half in range(2):
                p.dma("sync", self.dbg_t["d_qf"][half, :, :, :],
                      self.qfs[half, :, :, :], after=[spill_tok])

    def p1b(self, b):
        p = self.p
        ET, NH, LT, S, E = self.ET, self.NH, self.LT, self.S, self.E
        kf, vt = self.A8, self.B8
        ksl = [self.W2b[:, 0:LT * 128], self.W2b[:, LT * 128:2 * LT * 128]]
        t_ks = [None, None]
        ev_by = {}
        nrm_mm_prev = None
        tac = None
        pend_norm = None
        prev_red = None
        for h in range(NH):
            sbi = h % 2
            t_ks[sbi] = p.dma(
                "gpsimd", ksl[sbi].rearrange("p (a o) -> p a o", a=LT),
                self.wkuR[h, :, :, :],
                after=[ev_by.get(h - 2, None), self.tok_img["wku"]])
            if b == 0:
                self._pump_conv(5)
            for tc in range(S // 512):
                gq = h * (S // 512) + tc
                pq = self.pA(2 + gq % 2)
                last = None
                for lt in range(LT):
                    last = p.mm(pq, ksl[sbi][:, lt * 128:(lt + 1) * 128],
                                self.C2[:, lt * S + tc * 512: lt * S + tc * 512 + 512],
                                start=(lt == 0), stop=(lt == LT - 1),
                                after=[t_ks[sbi], self.tok_cf,
                                       ev_by.get(("w", gq - 2), None)])
                dst = kf[:, h * S + tc * 512: h * S + tc * 512 + 512]
                ev_t1, ev, tsq = self._rope_evac(
                    pq, dst, tc * 512, last, eng=(gq % 5 >= 3),
                    parity=gq % 2, sq_war=nrm_mm_prev)
                ev_by[("w", gq)] = ev_t1
                ev_by[h] = ev
                if pend_norm is not None:
                    tac2, nrm_mm_prev = self._norm_fin(*pend_norm,
                                                       self.nk2, prev_red)
                    if tac2 is not None:
                        tac, prev_red = tac2, tac2
                pend_norm = (h, tc, S // 512, gq % 2, tsq)
        if pend_norm is not None:
            tac2, nrm_mm_prev = self._norm_fin(*pend_norm, self.nk2, prev_red)
            if tac2 is not None:
                tac = tac2
        self.tok_kf = ev_by[NH - 1]
        self.tok_kf2 = [self._rope_last.get(0), self._rope_last.get(1)]
        tm1 = p.vop(lambda e: e.tensor_mul(self.mrow[0:1, :], self.nq2[0:1, :],
                                           self.nk2[0:1, :]), after=[tac])
        tm2 = p.sop(lambda e: e.activation(self.rows[0:1, 0:NH], self.mrow[0:1, :],
                                           AF.Ln, bias=self.eps_row(), scale=1.0),
                    after=[tm1])
        tm3 = p.sop(lambda e: e.activation(self.mrow[0:1, :], self.rows[0:1, 0:NH],
                                           AF.Exp, bias=self.zrow(), scale=0.5),
                    after=[tm2])
        tm4 = p.mm(self.psB[:, 1024:1024 + NH], self.nrows[0:1, :],
                   self.mrow[0:1, :], start=True, stop=True,
                   after=[tm3, nrm_mm_prev])
        tm5 = p.sop(lambda e: e.copy(self.negm[:, :], self.psB[:, 1024:1024 + NH]),
                    after=[tm4])
        self.tok_negm = tm5
        vsl = self.W2b[:, 2 * LT * 128: 2 * LT * 128 + LT * 512]
        evp = {}
        for dch in range(E // 512):
            t_wv = p.dma("gpsimd", vsl.rearrange("p (a o) -> p a o", a=LT),
                         self.wvuR[dch, :, :, :],
                         after=[evp.get(dch - 1, None), self.tok_img["wvu"]])
            if b == 0:
                self._pump_conv(8)
            evl = None
            for tt in range(S // 128):
                gv = dch * (S // 128) + tt
                last = None
                for lt in range(LT):
                    last = p.mm(self.pB(gv % 2),
                                self.C2[:, lt * S + tt * 128: lt * S + tt * 128 + 128],
                                vsl[:, lt * 512:(lt + 1) * 512],
                                start=(lt == 0), stop=(lt == LT - 1),
                                after=[t_wv, self.tok_cf,
                                       evp.get(("w", gv - 2), None)])
                evl = p.sop(lambda e, tt=tt, dch=dch, gv=gv: e.copy(
                    vt[:, tt * E + dch * 512: tt * E + dch * 512 + 512],
                    self.pB(gv % 2)[:, :]), after=[last])
                evp[("w", gv)] = evl
            evp[dch] = evl
        self.tok_vt = evp[E // 512 - 1]
        if b == 0:
            self._drain_conv()
        if self.dbg:
            p.dma("sync", self.dbg_t["d_kf"][:, :], kf[:, 0:NH * S],
                  after=[self.tok_kf])
            p.dma("sync", self.dbg_t["d_vt"][:, :], vt[:, 0:(S // 128) * E],
                  after=[self.tok_vt])
            p.dma("sync", self.dbg_t["d_negm"][:, :], self.negm[:, :],
                  after=[self.tok_negm])

    def p1c(self, b):
        p = self.p
        ET, NH, S, TCH, ND, E = self.ET, self.NH, self.S, self.TCH, self.ND, self.E
        kf, vt = self.A8, self.B8
        r_mean = self.rows[0:1, 0:TCH]
        r_m2 = self.rows[0:1, TCH:2 * TCH]
        r_tmp = self.rows[0:1, 2 * TCH:3 * TCH]
        r_rstd = self.rows[0:1, 3 * TCH:4 * TCH]
        tok_y, tok_yb = None, None
        self._tok_ys = getattr(self, "_tok_ys", {})
        exp_tok = {}
        sum_tok = {}
        ev1_tok, ev2_tok = {}, {}
        sum_war = {}
        bc_war = {}
        tok_tr = [None, None]
        tok_ap_last = None
        tbc_prev, tcp_prev = None, None
        gp = 0

        def sbank(par):
            return self.psA[:, par * 2 * TCH: (par + 1) * 2 * TCH]

        def att_bank(gh):
            return self.psB[:, (gh % 2) * 512:(gh % 2) * 512 + TCH]

        def sum_row(gh):
            return self.psB[0:1, 1024 + (gh % 2) * 512:1024 + (gh % 2) * 512 + TCH]

        def emit_tail(qc, h, gh, mm_att_l, mm_sum_l):
            nonlocal tbc_prev, tcp_prev
            tpar = gp % 2
            tln = p.sop(lambda e, gh=gh: e.activation(
                r_tmp, sum_row(gh), AF.Ln,
                bias=self.zrow(), scale=1.0), after=[mm_sum_l, tbc_prev])
            trs = p.sop(lambda e: e.activation(r_rstd, r_tmp, AF.Exp,
                                               bias=self.zrow(), scale=-1.0),
                        after=[tln])
            sum_war[gh] = tln
            bb = sbank(tpar)[:, 0:TCH]
            tbc = p.mm(bb, self.onesrs[0:1, :], r_rstd,
                       start=True, stop=True,
                       after=[trs, tcp_prev, exp_tok.get(gp - 2, None)])
            tcp = p.sop(lambda e, bb=bb: e.copy(self.bc[0][:, 0:TCH], bb),
                        after=[tbc, ev1_tok.get(gh - 1, None), tok_ap_last])
            bc_war[tpar] = tcp
            att_t = self.M4[:, h * TCH:(h + 1) * TCH]
            tt1 = p.vop(lambda e, gh=gh: e.tensor_mul(
                self.bc[1][:, 0:TCH], att_bank(gh), self.bc[0][:, 0:TCH]),
                after=[tcp, mm_att_l, ev2_tok.get(gh - 1, None)])
            tt2 = p.vop(lambda e, att_t=att_t: e.tensor_add(
                att_t, att_t, self.bc[1][:, 0:TCH]), after=[tt1, t_x])
            ev1_tok[gh] = tt1
            ev2_tok[gh] = tt2
            tbc_prev, tcp_prev = tbc, tcp

        for qc in range(self.NCH):
            q0 = qc * TCH
            half = q0 // self.SH
            tc0 = q0 % self.SH
            t_qfc = p.dma("sync",
                          self.C2[:, 0:NH * TCH].rearrange("p (h t) -> p h t", h=NH),
                          self.qfs[half, :, :, tc0:tc0 + TCH],
                          after=[self.tok_spill, exp_tok.get(gp - 1, None),
                                 sum_tok.get(qc * NH - 1, None)])
            t_x = p.dma("sync",
                        self.M4[:, 0:ET * TCH].rearrange("p (a t) -> p a t", a=ET),
                        self.xT[b, :, q0:q0 + TCH].rearrange("(a p) t -> p a t", p=128),
                        after=[tok_y, tok_yb, tok_ap_last])
            nk = (qc + 1) * ND
            npair = nk // 2
            pend = None
            for h in range(NH):
                gh = qc * NH + h
                mm_att_l, mm_sum_l = None, None
                lag = []
                for kp in range(npair + 2):
                    if len(lag) >= 2 or (kp >= npair and lag):
                        (kpd, tpar_d) = lag.pop(0)
                        for half2 in range(2):
                            k2 = kpd * 2 + half2
                            pbf2 = self.pbuf[tpar_d * 2 + half2]
                            mm_att_l = p.mm(
                                att_bank(gh),
                                vt[:, k2 * E + h * 128: k2 * E + h * 128 + 128],
                                pbf2, start=(k2 == 0), stop=(k2 == nk - 1),
                                after=[exp_tok[(gh, kpd)],
                                       ev1_tok.get(gh - 2, None)],
                                inc=True)
                            mm_sum_l = p.mm(
                                sum_row(gh), self.onescs[:, 0:1],
                                pbf2, start=(k2 == 0), stop=(k2 == nk - 1),
                                after=[exp_tok[(gh, kpd)],
                                       sum_war.get(gh - 2, None)],
                                inc=True)
                    if kp < npair:
                        tpar = gp % 2
                        kt0 = 2 * kp
                        mms = []
                        for half2 in range(2):
                            kt = kt0 + half2
                            mms.append(p.mm(
                                sbank(tpar)[:, half2 * TCH:half2 * TCH + TCH],
                                kf[:, h * S + kt * 128: h * S + kt * 128 + 128],
                                self.C2[:, h * TCH:(h + 1) * TCH],
                                start=True, stop=True,
                                after=[t_qfc, exp_tok.get(gp - 2, None),
                                       bc_war.get(tpar, None)]))
                        te = p.sop(lambda e, h=h, tpar=tpar: e.activation(
                            self.pb2h[tpar], sbank(tpar), AF.Exp,
                            bias=self.negm[:, h:h + 1], scale=self.SC),
                            after=[mms[0], mms[1], self.tok_negm,
                                   sum_tok.get(gh - 1, None) if kp < 2 else None,
                                   mm_sum_l if kp >= 2 else None])
                        for half2 in range(2):
                            kt = kt0 + half2
                            if kt >= qc * ND:
                                r = kt - qc * ND
                                m0 = (self.ND - 1 - r) * 128
                                pbf = self.pbuf[tpar * 2 + half2]
                                te = p.vop(lambda e, pbf=pbf, m0=m0: e.tensor_mul(
                                    pbf, pbf, self.maskss[:, m0:m0 + TCH]),
                                    after=[te])
                        exp_tok[(gh, kp)] = te
                        exp_tok[gp] = te
                        lag.append((kp, tpar))
                        gp += 1
                sum_tok[gh] = mm_sum_l
                if pend is not None:
                    emit_tail(*pend)
                pend = (qc, h, gh, mm_att_l, mm_sum_l)
            emit_tail(*pend)
            mm_mean_prev, mm_var_prev = None, None
            stat_m = self.psB[0:1, 1024:1024 + TCH]
            stat_v = self.psB[0:1, 1536:1536 + TCH]
            for i in range(ET):
                att_t = self.M4[:, i * TCH:(i + 1) * TCH]
                tca = p.sop(lambda e, att_t=att_t, i=i: e.copy(
                    self.abuf[i % 2][:, :], att_t),
                    after=[ev2_tok[qc * NH + i], mm_mean_prev, tok_tr[0]])
                tsq = p.vop(lambda e, i=i: e.tensor_mul(
                    self.qbuf[i % 2][:, 0:TCH], self.abuf[i % 2][:, :],
                    self.abuf[i % 2][:, :]), after=[tca, mm_var_prev, tok_tr[1]])
                mm_mean_prev = p.mm(stat_m, self.onescs[:, 0:1],
                                    self.abuf[i % 2][:, :],
                                    start=(i == 0), stop=(i == ET - 1),
                                    after=[tca, sum_war.get(qc * NH + NH - 2, None)
                                           if i == 0 else None],
                                    inc=True)
                mm_var_prev = p.mm(stat_v, self.onescs[:, 0:1],
                                   self.qbuf[i % 2][:, 0:TCH],
                                   start=(i == 0), stop=(i == ET - 1),
                                   after=[tsq, sum_war.get(qc * NH + NH - 1, None)
                                          if i == 0 else None],
                                   inc=True)
            tr1 = p.sop(lambda e: e.activation(r_mean, stat_m, AF.Copy,
                                               bias=0.0, scale=1.0 / E),
                        after=[mm_mean_prev, tbc_prev])
            tr2 = p.sop(lambda e: e.activation(r_m2, stat_v, AF.Copy,
                                               bias=0.0, scale=1.0 / E),
                        after=[mm_var_prev])
            tok_tr = [tr1, tr2]
            sum_war[qc * NH + NH - 2] = tr1
            sum_war[qc * NH + NH - 1] = tr2
            tr3 = p.vop(lambda e: e.tensor_mul(r_tmp, r_mean, r_mean),
                        after=[tr1])
            tr4 = p.vop(lambda e: e.tensor_sub(r_tmp, r_m2, r_tmp), after=[tr2, tr3])
            tr5 = p.sop(lambda e: e.activation(r_tmp, r_tmp, AF.Ln,
                                               bias=self.eps_row(), scale=1.0),
                        after=[tr4])
            tr6 = p.sop(lambda e: e.activation(r_rstd, r_tmp, AF.Exp,
                                               bias=self.zrow(), scale=-0.5),
                        after=[tr5])
            tr7 = p.vop(lambda e: e.tensor_mul(r_tmp, r_mean, r_rstd), after=[tr6])
            bpar = gp % 2
            bb1 = sbank(bpar)[:, 0:TCH]
            bb2 = sbank((bpar + 1) % 2)[:, 0:TCH]
            tb1 = p.mm(bb1, self.onesrs[0:1, :], r_rstd,
                       start=True, stop=True,
                       after=[tr6, exp_tok.get(gp - 2, None),
                              bc_war.get(bpar, None)])
            tc1 = p.sop(lambda e: e.copy(self.bc[0][:, 0:TCH], bb1),
                        after=[tb1, ev1_tok[qc * NH + NH - 1]])
            tb2 = p.mm(bb2, self.onesrs[0:1, :], r_tmp,
                       start=True, stop=True,
                       after=[tr7, exp_tok.get(gp - 1, None),
                              bc_war.get((bpar + 1) % 2, None), tcp_prev])
            tc2 = p.sop(lambda e: e.copy(self.bc[1][:, 0:TCH], bb2),
                        after=[tb2, ev2_tok[qc * NH + NH - 1]])
            bc_war[bpar] = tc1
            bc_war[(bpar + 1) % 2] = tc2
            tbc_prev, tcp_prev = tb2, tc2
            ap_toks = []
            for i in range(ET):
                att_t = self.M4[:, i * TCH:(i + 1) * TCH]
                ta1 = p.vop(lambda e, att_t=att_t: e.tensor_mul(
                    att_t, att_t, self.bc[0][:, 0:TCH]), after=[tc1])
                ap_toks.append(p.vop(lambda e, att_t=att_t: e.tensor_sub(
                    att_t, att_t, self.bc[1][:, 0:TCH]), after=[ta1, tc2]))
            tok_ap_last = ap_toks[-1]
            tok_y = p.dma("sync", self.ysR[b, qc, :, :],
                          self.M4[:, 0:ET * TCH], after=ap_toks)
            tok_yb = p.dma("gpsimd", self.ysb[b, qc, :, :],
                           self.M4[:, 0:ET * TCH], after=ap_toks)
            self._tok_ys[(b, qc)] = (tok_y, tok_yb)
        self.tok_ylast = tok_y
        self.tok_yblast = tok_yb
        if self.dbg:
            p.dma("sync", self.dbg_t["d_y"][b, :, :, :], self.ysR[b, :, :, :],
                  after=[tok_y])

    def p2(self, b):
        p = self.p
        ET, NH, S, TCH, E, HF = self.ET, self.NH, self.S, self.TCH, self.E, self.HF
        hid = self.A8
        FSLAB = self.FSLAB
        NSL = self.NSL
        NHT = FSLAB // 128
        NBUF = 4
        w1sl = [self.B8[:, i * ET * FSLAB:(i + 1) * ET * FSLAB] for i in range(NBUF)]
        w3sl = [self.B8[:, (NBUF + i) * ET * FSLAB:(NBUF + 1 + i) * ET * FSLAB]
                for i in range(NBUF)]
        W2Q = min(1024, HF // 4)
        NW2 = HF // W2Q
        NW2B = 4
        w2sl = [self.W2b[:, i * (W2Q // 128) * 128:(i + 1) * (W2Q // 128) * 128]
                for i in range(NW2B)]
        r_mean = self.rows[0:1, 0:TCH]
        r_m2 = self.rows[0:1, TCH:2 * TCH]
        r_tmp = self.rows[0:1, 2 * TCH:3 * TCH]
        r_rstd = self.rows[0:1, 3 * TCH:4 * TCH]
        tok_z = None
        pend_ln = None
        gate_tok = {}
        hmul = {}
        silu_tok = {}
        lcp_tok = {}
        w2_slab_prev = [None] * 4
        w2_last_of_chunk = {}
        tok_tr = [None, None]
        tok_tc2 = None
        tok_ap_last = None
        for ch in range(self.NCH):
            t0 = ch * TCH
            t_ybf = p.dma("sync",
                          self.C2[:, 0:ET * TCH],
                          self.ysb[b, ch, :, :],
                          after=[self._tok_ys[(b, ch)][1],
                                 gate_tok.get(ch - 1, None)])
            t_yres = None
            t_w1 = [None] * NBUF
            t_w3 = [None] * NBUF
            lastmm = None
            for si in range(NSL):
                par = si % NBUF
                conv_w = self.tok_conv_p2 if (b == 0 and ch == 0) else []
                t_w1[par] = p.dma("gpsimd",
                                  w1sl[par].rearrange("p (a o) -> p a o", a=ET),
                                  self.w1R[si, :, :, :],
                                  after=[gate_tok.get((ch, si - NBUF, par), None),
                                         w2_last_of_chunk.get(ch - 1, None)
                                         if si < NBUF else None] + conv_w)
                t_w3[par] = p.dma("gpsimd",
                                  w3sl[par].rearrange("p (a o) -> p a o", a=ET),
                                  self.w3R[si, :, :, :],
                                  after=[gate_tok.get((ch, si - NBUF, par), None),
                                         w2_last_of_chunk.get(ch - 1, None)
                                         if si < NBUF else None] + conv_w)
                for ht in range(NHT):
                    hg = si * NHT + ht
                    pg = self.pA(hg % 2)
                    pl = self.pA(2 + hg % 2)
                    lg, ll = None, None
                    for dt in range(ET):
                        lg = p.mm(pg[:, 0:TCH], w1sl[par][:, dt * FSLAB + ht * 128:
                                                dt * FSLAB + ht * 128 + 128],
                                  self.C2[:, dt * TCH:(dt + 1) * TCH],
                                  start=(dt == 0), stop=(dt == ET - 1),
                                  after=[t_w1[par], t_ybf,
                                         silu_tok.get((ch, hg - 2), None)],
                                  inc=(dt == ET - 1))
                    for dt in range(ET):
                        ll = p.mm(pl[:, 0:TCH], w3sl[par][:, dt * FSLAB + ht * 128:
                                                dt * FSLAB + ht * 128 + 128],
                                  self.C2[:, dt * TCH:(dt + 1) * TCH],
                                  start=(dt == 0), stop=(dt == ET - 1),
                                  after=[t_w3[par], t_ybf,
                                         lcp_tok.get((ch, hg - 2), None)],
                                  inc=(dt == ET - 1))
                    ts = p.sop(lambda e, pg=pg, hg=hg: e.activation(
                        self.pbuf[hg % 2], pg[:, 0:TCH], AF.Silu,
                        bias=self.zcol(), scale=1.0),
                        after=[lg, hmul.get((ch, hg - 2), None)])
                    tl = p.vop(lambda e, pl=pl, hg=hg: e.tensor_copy(
                        self.abuf[hg % 2][:, 0:TCH], pl[:, 0:TCH]),
                        after=[ll, hmul.get((ch, hg - 2), None)])
                    tm = p.vop(lambda e, hg=hg: e.tensor_mul(
                        hid[:, hg * TCH:(hg + 1) * TCH],
                        self.pbuf[hg % 2], self.abuf[hg % 2][:, 0:TCH]),
                        after=[ts, tl, w2_last_of_chunk.get(ch - 1, None)])
                    silu_tok[(ch, hg)] = ts
                    lcp_tok[(ch, hg)] = tl
                    hmul[(ch, hg)] = tm
                    lastmm = ll
                gate_tok[(ch, si, par)] = lastmm
                if si == 0:
                    if pend_ln is not None:
                        pend_ln()
                        pend_ln = None
                    t_yres = p.dma("sync", self.M4[:, 0:ET * TCH],
                                   self.ysR[b, ch, :, :],
                                   after=[self._tok_ys[(b, ch)][0], tok_z,
                                          tok_ap_last])
            gate_tok[ch] = lastmm
            mm_mean_prev, mm_var_prev = None, None
            zevs = []
            for dt in range(ET):
                pz = self.pB(dt % 2)
                lastz = None
                for q in range(NW2):
                    par = q % NW2B
                    tw2 = p.dma(
                        "sync", w2sl[par][:, 0:(W2Q // 128) * 128],
                        self.w2ppR[dt, :, q * (W2Q // 128):(q + 1) * (W2Q // 128), :]
                        .rearrange("p j d -> p (j d)"),
                        after=[w2_slab_prev[par]] + self.tok_w2pp)
                    for j in range(W2Q // 128):
                        hg = q * (W2Q // 128) + j
                        lastz = p.mm(pz[:, 0:TCH], w2sl[par][:, j * 128:(j + 1) * 128],
                                     hid[:, hg * TCH:(hg + 1) * TCH],
                                     start=(hg == 0), stop=(hg == HF // 128 - 1),
                                     after=[tw2, hmul[(ch, hg)],
                                            zevs[dt - 2] if dt >= 2 else None,
                                            tok_tc2 if (hg == 0 and dt < 2)
                                            else None],
                                     inc=(j == W2Q // 128 - 1))
                    w2_slab_prev[par] = lastz
                zt = self.M4[:, dt * TCH:(dt + 1) * TCH]
                tz1 = p.vop(lambda e, zt=zt, pz=pz: e.tensor_add(zt, zt,
                                                                 pz[:, 0:TCH]),
                            after=[lastz, t_yres])
                zevs.append(tz1)
                tca = p.sop(lambda e, zt=zt, dt=dt: e.copy(
                    self.abuf[dt % 2][:, :], zt),
                    after=[tz1, mm_mean_prev, tok_tr[0],
                           silu_tok.get((ch, HF // 128 - 2 + dt), None)])
                tsq = p.vop(lambda e, dt=dt: e.tensor_mul(
                    self.qbuf[dt % 2][:, 0:TCH], self.abuf[dt % 2][:, :],
                    self.abuf[dt % 2][:, :]), after=[tca, mm_var_prev, tok_tr[1]])
                mm_mean_prev = p.mm(self.pB(2)[0:1, 0:TCH], self.onescs[:, 0:1],
                                    self.abuf[dt % 2][:, :],
                                    start=(dt == 0), stop=(dt == ET - 1),
                                    after=[tca], inc=True)
                mm_var_prev = p.mm(self.pB(3)[0:1, 0:TCH], self.onescs[:, 0:1],
                                   self.qbuf[dt % 2][:, 0:TCH],
                                   start=(dt == 0), stop=(dt == ET - 1),
                                   after=[tsq], inc=True)
            w2_last_of_chunk[ch] = w2_slab_prev[(NW2 - 1) % NW2B]

            def emit_ln2(ch=ch, t0=t0, mm_mean_prev=mm_mean_prev,
                         mm_var_prev=mm_var_prev, zevs=tuple(zevs)):
                nonlocal tok_z, tok_ap_last, tok_tc2, tok_tr
                tr1 = p.sop(lambda e: e.activation(r_mean, self.pB(2)[0:1, 0:TCH],
                                                   AF.Copy, bias=0.0, scale=1.0 / E),
                            after=[mm_mean_prev])
                tr2 = p.sop(lambda e: e.activation(r_m2, self.pB(3)[0:1, 0:TCH],
                                                   AF.Copy, bias=0.0, scale=1.0 / E),
                            after=[mm_var_prev])
                tok_tr = [tr1, tr2]
                tr3 = p.vop(lambda e: e.tensor_mul(r_tmp, r_mean, r_mean), after=[tr1])
                tr4 = p.vop(lambda e: e.tensor_sub(r_tmp, r_m2, r_tmp),
                            after=[tr2, tr3])
                tr5 = p.sop(lambda e: e.activation(r_tmp, r_tmp, AF.Ln,
                                                   bias=self.eps_row(), scale=1.0),
                            after=[tr4])
                tr6 = p.sop(lambda e: e.activation(r_rstd, r_tmp, AF.Exp,
                                                   bias=self.zrow(), scale=-0.5),
                            after=[tr5])
                tr7 = p.vop(lambda e: e.tensor_mul(r_tmp, r_mean, r_rstd), after=[tr6])
                tb1 = p.mm(self.pB(1)[:, 0:TCH], self.onesrs[0:1, :], r_rstd,
                           start=True, stop=True, after=[tr6, zevs[-1], tr2])
                tc1 = p.sop(lambda e: e.copy(self.bc[0][:, 0:TCH],
                                             self.pB(1)[:, 0:TCH]),
                            after=[tb1, tok_ap_last])
                tb2 = p.mm(self.pB(1)[:, 0:TCH], self.onesrs[0:1, :], r_tmp,
                           start=True, stop=True, after=[tr7, tc1])
                tc2 = p.sop(lambda e: e.copy(self.bc[1][:, 0:TCH],
                                             self.pB(1)[:, 0:TCH]),
                            after=[tb2, tok_tc2])
                tok_tc2 = tc2
                ap_toks = []
                for i in range(ET):
                    zt = self.M4[:, i * TCH:(i + 1) * TCH]
                    ta1 = p.vop(lambda e, zt=zt: e.tensor_mul(
                        zt, zt, self.bc[0][:, 0:TCH]), after=[tc1])
                    ap_toks.append(p.vop(lambda e, zt=zt: e.tensor_sub(
                        zt, zt, self.bc[1][:, 0:TCH]), after=[ta1, tc2]))
                tok_ap_last = ap_toks[-1]
                tok_z = p.dma("sync", self.zT[b, :, t0:t0 + TCH].rearrange(
                    "(a p) t -> p a t", p=128),
                    self.M4[:, 0:ET * TCH].rearrange("p (a t) -> p a t", a=ET),
                    after=ap_toks)
            pend_ln = emit_ln2
        pend_ln()
        pend_ln = None
        self.tok_zlast = tok_z

    def build(self):
        p = self.p
        upto = self.cfg.get("phases", "all")
        self.p0()
        if upto != "p0":
            for b in range(self.B):
                self.p1a(b)
                p.barrier()
                if upto == "p1a":
                    break
                self.p1b(b)
                p.barrier()
                if upto == "p1b":
                    break
                self.p1c(b)
                p.barrier()
                if upto == "p1c":
                    break
                self.p2(b)
                p.barrier()
        p.barrier()
        p.emit(self.es)
        self.es.close()
        return self.nc


def head_perm(HD):
    nope = HD // 4
    xe = np.arange(nope, HD, 2)
    xo = np.arange(nope + 1, HD, 2)
    return np.concatenate([xe, np.arange(0, nope // 2), xo,
                           np.arange(nope // 2, nope)])


def host_consts(cfg):
    S, E, HD, HF, TCH = cfg["S"], cfg["E"], cfg["HD"], cfg["HF"], cfg["TCH"]
    HA = HF // 128
    ND = TCH // 128
    nope = HD // 4
    rope_d = HD - nope
    freqs = (np.float32(1.0) / (np.float32(10000.0) **
             (np.arange(0, rope_d, 2, dtype=np.float32) / np.float32(rope_d))))
    emb = np.arange(S, dtype=np.float32)[None, :] * freqs[:, None].astype(np.float32)
    trig = np.zeros((128, S), dtype=np.float32)
    trig[0:48] = np.cos(emb)
    trig[64:112] = np.sin(emb)

    def syl(n):
        H = np.array([[1.0]], dtype=np.float32)
        while H.shape[0] < n:
            H = np.block([[H, H], [H, -H]])
        return H

    h128 = syl(128).astype(BFD)
    h64 = syl(HA).astype(BFD)
    onesc = np.ones((128, 1), dtype=BFD)
    onesr = np.ones((1, 128), dtype=np.float32)
    nrow = np.full((1, 128), -1.0 / np.sqrt(HD), dtype=np.float32)
    k = np.arange(128)[:, None]
    c = np.arange((ND - 1) * 128 + TCH)[None, :]
    masks = (c >= k + (ND - 1) * 128).astype(BFD)
    misc = np.zeros((128, 4), dtype=np.float32)
    misc[:, 0] = 1e-5
    return dict(trigc=trig, h64c=np.ascontiguousarray(h64),
                h128c=np.ascontiguousarray(h128), onescd=onesc, onesrd=onesr,
                nrowd=nrow, masksd=masks, miscd=misc)


def host_in_maps(cfg, n_cores, x, Wq, Wkv, Wku, Wvu, W1, W2, W3):
    E, HD = cfg["E"], cfg["HD"]
    NH = E // HD
    perm = head_perm(HD)
    full_perm = np.concatenate([h * HD + perm for h in range(NH)])
    consts = host_consts(cfg)
    wq_p = np.ascontiguousarray(Wq[:, full_perm])
    wku_p = np.ascontiguousarray(Wku[:, full_perm])
    w1t = np.ascontiguousarray(W1.T)
    w3t = np.ascontiguousarray(W3.T)
    w2t = np.ascontiguousarray(W2.T)
    B = cfg["B"]
    maps = []
    for c in range(n_cores):
        xs = np.ascontiguousarray(x[c * B:(c + 1) * B].transpose(0, 2, 1))
        m = dict(xT=xs, wq=wq_p, wkv=np.ascontiguousarray(Wkv),
                 wku=wku_p, wvu=np.ascontiguousarray(Wvu),
                 w1t=w1t, w3t=w3t, w2t=w2t, **consts)
        maps.append(m)
    return maps


def kernel(L0_attn_x, L0_attn_Wq, L0_attn_Wkv_down, L0_attn_Wk_up, L0_attn_Wv_up,
           L0_ffn_W1, L0_ffn_W2, L0_ffn_W3):
    cfg = mk_cfg()
    x = np.asarray(L0_attn_x, dtype=np.float32)
    maps = host_in_maps(cfg, N_CORES, x,
                        np.asarray(L0_attn_Wq, np.float32),
                        np.asarray(L0_attn_Wkv_down, np.float32),
                        np.asarray(L0_attn_Wk_up, np.float32),
                        np.asarray(L0_attn_Wv_up, np.float32),
                        np.asarray(L0_ffn_W1, np.float32),
                        np.asarray(L0_ffn_W2, np.float32),
                        np.asarray(L0_ffn_W3, np.float32))
    nc = Builder(cfg).build()
    res = run_bass_kernel_spmd(nc, maps, core_ids=list(range(N_CORES)))
    B = cfg["B"]
    outs = []
    for c in range(N_CORES):
        zt = res.results[c]["zT"]
        outs.append(np.ascontiguousarray(zt.transpose(0, 2, 1)))
    return np.concatenate(outs, axis=0).astype(np.float32)


# version 16 (final)
# speedup vs baseline: 1.0843x; 1.0312x over previous
import contextlib
import numpy as np
import ml_dtypes
import concourse.bass as bass
import concourse.mybir as mybir
from concourse.bass_utils import run_bass_kernel_spmd

F32 = mybir.dt.float32
BF16 = mybir.dt.bfloat16
AF = mybir.ActivationFunctionType
ALU = mybir.AluOpType
AX = mybir.AxisListType
BFD = ml_dtypes.bfloat16

N_CORES = 8
ENGINES = ("sync", "gpsimd", "vector", "scalar", "tensor")


def mk_cfg(dev=False):
    if dev:
        return dict(B=1, S=1024, E=512, LAT=256, HF=1024, TCH=256, HD=128, dbg=True)
    return dict(B=2, S=2048, E=2048, LAT=512, HF=8192, TCH=512, HD=128, dbg=False)


NPOOL = 12


class Prog:
    def __init__(self, nc):
        self.nc = nc
        self.ops = {e: [] for e in ENGINES}
        self.sem_objs = {}
        self.counts = {s: 0 for s in ("sT", "sV", "sS", "sG", "sY")}
        for q in ("Y", "G", "S"):
            for i in range(NPOOL):
                self.counts[f"d{q}{i}"] = 0
        self.dma_idx = {"sync": 0, "gpsimd": 0, "scalar": 0}

    def op(self, engine, fn, after=(), inc=None, inc_amt=1):
        waits = {}
        for tok in after:
            if tok is None:
                continue
            s, v = tok
            if waits.get(s, 0) < v:
                waits[s] = v
        tok_out = None
        if inc is not None:
            self.counts[inc] += inc_amt
            tok_out = (inc, self.counts[inc])
        self.ops[engine].append((fn, waits, (inc, inc_amt) if inc else None))
        return tok_out

    QL = {"sync": "Y", "gpsimd": "G", "scalar": "S"}

    def dma(self, queue, out, in_, after=(), **kw):
        i = self.dma_idx[queue]
        self.dma_idx[queue] = i + 1
        semn = f"d{self.QL[queue]}{i % NPOOL}"
        after = list(after)
        if i >= NPOOL:
            after.append((semn, self.counts[semn]))
        return self.op(queue, lambda e: e.dma_start(out=out, in_=in_, **kw),
                       after=after, inc=semn, inc_amt=16)

    def mm(self, out, lhsT, rhs, start, stop, after=(), inc=None):
        if inc is None:
            inc = stop
        return self.op(
            "tensor",
            lambda e: e.matmul(out, lhsT, rhs, start=start, stop=stop),
            after=after, inc=("sT" if inc else None))

    def vop(self, fn, after=(), inc=True):
        return self.op("vector", fn, after=after, inc=("sV" if inc else None))

    def sop(self, fn, after=(), inc=True):
        return self.op("scalar", fn, after=after, inc=("sS" if inc else None))

    def gop(self, fn, after=(), inc=True):
        return self.op("gpsimd", fn, after=after, inc=("sG" if inc else None))

    def barrier(self):
        waits = {}
        for e, sem in (("tensor", "sT"), ("vector", "sV"), ("scalar", "sS"),
                       ("gpsimd", "sG"), ("sync", "sY")):
            if not self.ops[e]:
                continue
            fn, w, inc = self.ops[e][-1]
            if inc is None:
                self.counts[sem] += 1
                inc = (sem, 1)
                self.ops[e][-1] = (fn, w, inc)
                waits[sem] = self.counts[sem]
            else:
                waits[inc[0]] = self.counts[inc[0]]
        for s, v in self.counts.items():
            if s[0] == "d" and v:
                waits[s] = v
        for e in ENGINES:
            self.ops[e].append((lambda eng: eng.nop(), dict(waits), None))

    def emit(self, es):
        nc = self.nc
        for s in self.counts:
            self.sem_objs[s] = es.enter_context(nc.semaphore(s))
        block = es.enter_context(nc.Block())
        for eng_name in ENGINES:
            ops = self.ops[eng_name]

            def make_emitter(ops=ops):
                def emitter(eng):
                    last = {}
                    for fn, waits, inc in ops:
                        for s, v in waits.items():
                            if last.get(s, 0) < v:
                                eng.wait_ge(self.sem_objs[s], v)
                                last[s] = v
                        inst = fn(eng)
                        if inc is not None:
                            s, amt = inc
                            inst.then_inc(self.sem_objs[s], amt)
                return emitter

            getattr(block, eng_name)(make_emitter())


class Builder:
    def __init__(self, cfg):
        self.cfg = cfg
        for k, v in cfg.items():
            setattr(self, k, v)
        self.NH = self.E // self.HD
        self.ET = self.E // 128
        self.LT = self.LAT // 128
        self.HA = self.HF // 128
        self.ND = self.TCH // 128
        self.NCH = self.S // self.TCH
        self.SH = self.S // 2
        self.FSLAB = min(256, self.S // 8)
        self.NSL = self.HF // self.FSLAB
        self.SC = 1.0 / float(np.sqrt(self.HD))
        self.FSC = 1.0 / float(np.sqrt(self.HF))

        self.nc = bass.Bass("TRN2", target_bir_lowering=False)
        self.p = Prog(self.nc)
        self._rope_last = {}
        self.es = contextlib.ExitStack()
        self._declare_dram()
        self._declare_sbuf()

    def _declare_dram(self):
        nc, B, S, E, LAT, HF = self.nc, self.B, self.S, self.E, self.LAT, self.HF
        dt_ = nc.dram_tensor
        self.xT = dt_("xT", [B, E, S], F32, kind="ExternalInput").ap()
        self.wq = dt_("wq", [E, E], F32, kind="ExternalInput").ap()
        self.wkv = dt_("wkv", [E, LAT], F32, kind="ExternalInput").ap()
        self.wku = dt_("wku", [LAT, E], F32, kind="ExternalInput").ap()
        self.wvu = dt_("wvu", [LAT, E], F32, kind="ExternalInput").ap()
        self.w1t = dt_("w1t", [E, HF], F32, kind="ExternalInput").ap()
        self.w3t = dt_("w3t", [E, HF], F32, kind="ExternalInput").ap()
        self.w2t = dt_("w2t", [HF, E], F32, kind="ExternalInput").ap()
        self.trigc = dt_("trigc", [128, S], F32, kind="ExternalInput").ap()
        self.h64c = dt_("h64c", [self.HA, self.HA], BF16, kind="ExternalInput").ap()
        self.h128c = dt_("h128c", [128, 128], BF16, kind="ExternalInput").ap()
        self.onescd = dt_("onescd", [128, 1], BF16, kind="ExternalInput").ap()
        self.onesrd = dt_("onesrd", [1, 128], F32, kind="ExternalInput").ap()
        self.nrowd = dt_("nrowd", [1, 128], F32, kind="ExternalInput").ap()
        self.masksd = dt_("masksd", [128, (self.ND - 1) * 128 + self.TCH], BF16,
                          kind="ExternalInput").ap()
        self.miscd = dt_("miscd", [128, 4], F32, kind="ExternalInput").ap()

        ET, LT, NH, HA = self.ET, self.LT, self.NH, self.HA
        NQS = 2
        FSLAB = self.FSLAB
        NSL = self.NSL
        TCH, NCH = self.TCH, self.NCH
        self.w2s1 = dt_("w2s1", [HF, E], BF16).ap()
        self.w2ppR = dt_("w2ppR", [ET, 128, HA, 128], BF16).ap()
        self.qfs = dt_("qfs", [2, 128, self.NH, self.SH], BF16).ap()
        self.xb = dt_("xb", [B, ET, 128, S], BF16).ap()
        self.wqR = dt_("wqR", [NH // NQS, 128, ET, 128 * NQS], BF16).ap()
        self.wkvR = dt_("wkvR", [LT, 128, ET, 128], BF16).ap()
        self.wkuR = dt_("wkuR", [NH, 128, LT, 128], BF16).ap()
        self.wvuR = dt_("wvuR", [E // 512, 128, LT, 512], BF16).ap()
        self.w1R = dt_("w1R", [NSL, 128, ET, FSLAB], BF16).ap()
        self.w3R = dt_("w3R", [NSL, 128, ET, FSLAB], BF16).ap()
        self.ysR = dt_("ysR", [B, NCH, 128, ET * TCH], F32).ap()
        self.ysb = dt_("ysb", [B, NCH, 128, ET * TCH], BF16).ap()
        self.zT = dt_("zT", [B, NCH, 128, ET * TCH], F32,
                      kind="ExternalOutput").ap()
        if self.dbg:
            self.dbg_t = {}
            for nm, shp, dt2 in [("d_cf", [128, self.LT * S], BF16),
                                 ("d_kf", [128, self.NH * S], BF16),
                                 ("d_vt", [128, (S // 128) * E], BF16),
                                 ("d_qf", [2, 128, self.NH, self.SH], BF16),
                                 ("d_negm", [128, self.NH], F32),
                                 ("d_y", [B, self.NCH, 128, self.ET * self.TCH], F32),
                                 ("d_w2pp", [self.ET, 128, self.HA, 128], BF16)]:
                self.dbg_t[nm] = dt_(nm, shp, dt2, kind="ExternalOutput").ap()

    def _declare_sbuf(self):
        es, nc = self.es, self.nc
        S, E, TCH, ET, LT, NH = self.S, self.E, self.TCH, self.ET, self.LT, self.NH
        sbt = nc.sbuf_tensor
        self.A8 = es.enter_context(sbt("A8", [128, ET * S], BF16))
        self.B8 = es.enter_context(sbt("B8", [128, ET * S], BF16))
        self.M4W = max(ET * TCH, 2 * S + 1024)
        self.M4 = es.enter_context(sbt("M4", [128, self.M4W], F32))
        self.C2 = es.enter_context(sbt("C2", [128, max(LT * S, ET * TCH)], BF16))
        self.W2b = es.enter_context(sbt("W2b", [128, 4096], BF16))
        self.pb4 = es.enter_context(sbt("pb4", [128, 4 * TCH], BF16))
        self.pb2h = [self.pb4[:, 0:2 * TCH], self.pb4[:, 2 * TCH:4 * TCH]]
        self.pbuf = [self.pb4[:, i * TCH:(i + 1) * TCH] for i in range(4)]
        self.abuf = [es.enter_context(sbt(f"ab{i}", [128, TCH], BF16)) for i in range(2)]
        self.qbuf = [es.enter_context(sbt(f"qb{i}", [128, max(TCH, 512)], BF16))
                     for i in range(2)]
        self.bc = [es.enter_context(sbt(f"bc{i}", [128, max(TCH, 512)], F32))
                   for i in range(2)]
        self.h64s = es.enter_context(sbt("h64s", [self.HA, self.HA], BF16))
        self.h128s = es.enter_context(sbt("h128s", [128, 128], BF16))
        self.onescs = es.enter_context(sbt("onescs", [128, 1], BF16))
        self.onesrs = es.enter_context(sbt("onesrs", [1, 128], F32))
        self.nrows = es.enter_context(sbt("nrows", [1, 128], F32))
        self.maskss = es.enter_context(
            sbt("maskss", [128, (self.ND - 1) * 128 + TCH], BF16))
        self.miscs = es.enter_context(sbt("miscs", [128, 4], F32))
        self.rows = es.enter_context(sbt("rows", [1, 4 * TCH], F32))
        self.t11 = es.enter_context(sbt("t11", [1, 2], F32))
        self.nq2 = es.enter_context(sbt("nq2", [1, NH], F32))
        self.nk2 = es.enter_context(sbt("nk2", [1, NH], F32))
        self.mrow = es.enter_context(sbt("mrow", [1, NH], F32))
        self.negm = es.enter_context(sbt("negm", [128, NH], F32))
        self.psA = es.enter_context(nc.psum_tensor("psA", [128, 2048], F32))
        self.psB = es.enter_context(nc.psum_tensor("psB", [128, 2048], F32))
        self.trig = self.M4[:, self.M4W - S: self.M4W]
        self.trig2 = self.M4[:, self.M4W - 2 * S: self.M4W - S]

    def pA(self, i):
        return self.psA[:, i * 512:(i + 1) * 512]

    def pB(self, i):
        return self.psB[:, i * 512:(i + 1) * 512]

    def zcol(self):
        return self.miscs[:, 1:2]

    def zrow(self):
        return self.miscs[0:1, 1:2]

    def eps_row(self):
        return self.miscs[0:1, 0:1]

    def _conv_init(self, base, width):
        self._cv_base, self._cv_w = base, width
        self._cv_par = 0
        self._cv_tok = [None, None]

    def _conv_slab(self, src, dst, shp):
        p = self.p
        n = int(np.prod(shp))
        assert n <= self._cv_w, (n, self._cv_w)
        par = self._cv_par
        self._cv_par ^= 1
        h = self.M4[:, self._cv_base + par * self._cv_w:
                    self._cv_base + par * self._cv_w + n]
        if len(shp) == 2:
            hv = h.rearrange("p (x y) -> p x y", x=shp[0])
        else:
            hv = h
        tin = p.dma("sync", hv, src, after=[self._cv_tok[par]])
        tout = p.dma("gpsimd", dst, hv, after=[tin])
        self._cv_tok[par] = tout
        return tout

    def _conv_part1(self):
        ET, S, LT, NH = self.ET, self.S, self.LT, self.NH
        W = min(4096, self.M4W // 2)
        self._conv_init(0, W)
        toks = {}
        XA = max(1, W // S)
        for a0 in range(0, ET, XA):
            toks["xb0"] = self._conv_slab(
                self.xT[0, a0 * 128:(a0 + XA) * 128, :].rearrange(
                    "(a p) t -> p a t", p=128),
                self.xb[0, a0:a0 + XA, :, :].rearrange("a p t -> p a t"),
                (XA, S))
        for lt in range(LT):
            toks["wkv"] = self._conv_slab(
                self.wkv[:, lt * 128:(lt + 1) * 128].rearrange(
                    "(a p) o -> p a o", p=128),
                self.wkvR[lt, :, :, :], (ET, 128))
        for sl in range(NH // 2):
            toks["wq"] = self._conv_slab(
                self.wq[:, sl * 256:(sl + 1) * 256].rearrange(
                    "(a p) o -> p a o", p=128),
                self.wqR[sl, :, :, :], (ET, 256))
        for h in range(NH):
            toks["wku"] = self._conv_slab(
                self.wku[:, h * 128:(h + 1) * 128].rearrange(
                    "(a p) o -> p a o", p=128),
                self.wkuR[h, :, :, :], (LT, 128))
        for dch in range(self.E // 512):
            toks["wvu"] = self._conv_slab(
                self.wvu[:, dch * 512:(dch + 1) * 512].rearrange(
                    "(a p) o -> p a o", p=128),
                self.wvuR[dch, :, :, :], (LT, 512))
        self.tok_img = toks
        self.tok_conv_p1 = self._cv_tok[:]

    def _conv_part2_fill(self):
        ET, S = self.ET, self.S
        FSLAB = self.FSLAB
        NSL = self.NSL
        W = min(2048, (self.M4W - 2 * self.S) // 2)
        self._conv_w2 = W
        AW = max(1, W // FSLAB)
        jobs = []
        for b in range(1, self.B):
            for a in range(ET):
                jobs.append((self.xT[b, a * 128:(a + 1) * 128, :],
                             self.xb[b, a, :, :], (S,)))
        for wsrc, wdst in ((self.w1t, self.w1R), (self.w3t, self.w3R)):
            for si in range(NSL):
                for a0 in range(0, ET, AW):
                    jobs.append((
                        wsrc[a0 * 128:(a0 + AW) * 128,
                             si * FSLAB:(si + 1) * FSLAB].rearrange(
                            "(a p) o -> p a o", p=128),
                        wdst[si, :, a0:a0 + AW, :], (AW, FSLAB)))
        self._conv_jobs = jobs
        self._conv_ji = 0

    def _pump_conv(self, n):
        while n > 0 and self._conv_ji < len(self._conv_jobs):
            src, dst, shp = self._conv_jobs[self._conv_ji]
            self._conv_ji += 1
            self._conv_slab(src, dst, shp)
            n -= 1

    def _drain_conv(self):
        self._pump_conv(10 ** 9)
        self.tok_conv_p2 = self._cv_tok[:]

    def p0(self):
        p = self.p
        t = []
        t.append(p.dma("sync", self.h64s[:, :], self.h64c[:, :]))
        t.append(p.dma("sync", self.h128s[:, :], self.h128c[:, :]))
        t.append(p.dma("sync", self.onescs[:, :], self.onescd[:, :]))
        t.append(p.dma("sync", self.onesrs[:, :], self.onesrd[:, :]))
        t.append(p.dma("sync", self.nrows[:, :], self.nrowd[:, :]))
        t.append(p.dma("sync", self.maskss[:, :], self.masksd[:, :]))
        t.append(p.dma("sync", self.miscs[:, :], self.miscd[:, :]))
        self.tok_consts = list(t)
        self._conv_part1()
        self._fold_w2()
        p.barrier()

    def _fold_w2(self):
        p, HA, E = self.p, self.HA, self.E
        DCW = min(64, (self.ET * self.S) // 512)
        fins1 = [self.A8[:, i * 128 * DCW:(i + 1) * 128 * DCW] for i in range(2)]
        fouts1 = [self.B8[:, i * 128 * DCW:(i + 1) * 128 * DCW] for i in range(2)]
        fins2 = [self.A8[:, (2 + i) * 128 * DCW:(3 + i) * 128 * DCW] for i in range(2)]
        fouts2 = [self.B8[:, (2 + i) * 128 * DCW:(3 + i) * 128 * DCW] for i in range(2)]
        DC = E // DCW
        w1s = min(512, 128 * DCW)
        w2s = min(512, HA * DCW)
        s1_done = {}
        ev1, st1, ev2, st2 = {}, {}, {}, {}
        st_all = []

        def stage1(dc):
            par = dc % 2
            fin, fout = fins1[par], fouts1[par]
            src_ = self.w2t[:, dc * DCW:(dc + 1) * DCW].rearrange(
                "(a b) d -> a b d", a=HA)
            tin = p.dma("gpsimd", fin[0:HA, :].rearrange("a (b d) -> a b d", b=128),
                        src_, after=[ev1.get(dc - 2, None), self.tok_consts[0]])
            evs = []
            for j in range((128 * DCW) // w1s):
                tmm = p.mm(self.pA(j % 2)[0:HA, 0:w1s], self.h64s[:, :],
                           fin[0:HA, j * w1s:(j + 1) * w1s],
                           start=True, stop=True,
                           after=[tin] + ([evs[-2]] if len(evs) >= 2
                                          else [ev1.get(dc - 1, None)]))
                evs.append(p.sop(
                    lambda e, j=j, fout=fout: e.copy(
                        fout[0:HA, j * w1s:(j + 1) * w1s],
                        self.pA(j % 2)[0:HA, 0:w1s]),
                    after=[tmm, st1.get(dc - 2, None)]))
            ev1[dc] = evs[-1]
            dst = self.w2s1[:, dc * DCW:(dc + 1) * DCW].rearrange(
                "(a b) d -> a b d", a=HA)
            st1[dc] = p.dma(
                "sync", dst, fout[0:HA, :].rearrange("a (b d) -> a b d", b=128),
                after=[evs[-1]])
            s1_done[dc] = st1[dc]

        def stage2(dc):
            par = dc % 2
            fin, fout = fins2[par], fouts2[par]
            src_ = self.w2s1[:, dc * DCW:(dc + 1) * DCW].rearrange(
                "(a b) d -> b a d", b=128)
            tin = p.dma("gpsimd",
                        fin[:, 0:HA * DCW].rearrange("b (a d) -> b a d", a=HA),
                        src_, after=[ev2.get(dc - 2, None), s1_done[dc]])
            evs = []
            for j in range((HA * DCW) // w2s):
                tmm = p.mm(self.pA(2 + j % 2)[:, 0:w2s], self.h128s[:, :],
                           fin[:, j * w2s:(j + 1) * w2s],
                           start=True, stop=True,
                           after=[tin] + ([evs[-2]] if len(evs) >= 2
                                          else [ev2.get(dc - 1, None)]))
                evs.append(p.sop(
                    lambda e, j=j, fout=fout: e.activation(
                        fout[:, j * w2s:(j + 1) * w2s],
                        self.pA(2 + j % 2)[:, 0:w2s], AF.Copy,
                        bias=0.0, scale=self.FSC),
                    after=[tmm, st2.get(dc - 2, None)]))
            ev2[dc] = evs[-1]
            dt0 = (dc * DCW) // 128
            off = (dc * DCW) % 128
            dst = self.w2ppR[dt0, :, :, off:off + DCW]
            st2[dc] = p.dma(
                "sync", dst, fout[:, 0:HA * DCW].rearrange("b (a d) -> b a d", a=HA),
                after=[evs[-1]])
            st_all.append(st2[dc])

        for dc in range(DC):
            stage1(dc)
            if dc >= 1:
                stage2(dc - 1)
        stage2(DC - 1)
        self.tok_w2pp = st_all[-NPOOL:]
        if self.dbg:
            self.tok_w2pp = [p.dma("sync", self.dbg_t["d_w2pp"][:, :, :, :],
                                   self.w2ppR[:, :, :, :], after=self.tok_w2pp)]

    def _conv_init(self, base, width):
        self._cv_base, self._cv_w = base, width
        self._cv_par = 0
        self._cv_tok = [None, None]

    def _conv_slab(self, src, dst, shp):
        p = self.p
        n = int(np.prod(shp))
        assert n <= self._cv_w, (n, self._cv_w)
        par = self._cv_par
        self._cv_par ^= 1
        h = self.M4[:, self._cv_base + par * self._cv_w:
                    self._cv_base + par * self._cv_w + n]
        if len(shp) == 2:
            hv = h.rearrange("p (x y) -> p x y", x=shp[0])
        else:
            hv = h
        tin = p.dma("sync", hv, src, after=[self._cv_tok[par]])
        tout = p.dma("gpsimd", dst, hv, after=[tin])
        self._cv_tok[par] = tout
        return tout

    def _conv_part1(self):
        ET, S, LT, NH = self.ET, self.S, self.LT, self.NH
        W = min(4096, self.M4W // 2)
        self._conv_init(0, W)
        toks = {}
        XA = max(1, W // S)
        for a0 in range(0, ET, XA):
            toks["xb0"] = self._conv_slab(
                self.xT[0, a0 * 128:(a0 + XA) * 128, :].rearrange(
                    "(a p) t -> p a t", p=128),
                self.xb[0, a0:a0 + XA, :, :].rearrange("a p t -> p a t"),
                (XA, S))
        for lt in range(LT):
            toks["wkv"] = self._conv_slab(
                self.wkv[:, lt * 128:(lt + 1) * 128].rearrange(
                    "(a p) o -> p a o", p=128),
                self.wkvR[lt, :, :, :], (ET, 128))
        for sl in range(NH // 2):
            toks["wq"] = self._conv_slab(
                self.wq[:, sl * 256:(sl + 1) * 256].rearrange(
                    "(a p) o -> p a o", p=128),
                self.wqR[sl, :, :, :], (ET, 256))
        for h in range(NH):
            toks["wku"] = self._conv_slab(
                self.wku[:, h * 128:(h + 1) * 128].rearrange(
                    "(a p) o -> p a o", p=128),
                self.wkuR[h, :, :, :], (LT, 128))
        for dch in range(self.E // 512):
            toks["wvu"] = self._conv_slab(
                self.wvu[:, dch * 512:(dch + 1) * 512].rearrange(
                    "(a p) o -> p a o", p=128),
                self.wvuR[dch, :, :, :], (LT, 512))
        self.tok_img = toks
        self.tok_conv_p1 = self._cv_tok[:]

    def _conv_part2_fill(self):
        ET, S = self.ET, self.S
        FSLAB = self.FSLAB
        NSL = self.NSL
        W = min(2048, (self.M4W - 2 * self.S) // 2)
        self._conv_w2 = W
        AW = max(1, W // FSLAB)
        jobs = []
        for b in range(1, self.B):
            for a in range(ET):
                jobs.append((self.xT[b, a * 128:(a + 1) * 128, :],
                             self.xb[b, a, :, :], (S,)))
        for wsrc, wdst in ((self.w1t, self.w1R), (self.w3t, self.w3R)):
            for si in range(NSL):
                for a0 in range(0, ET, AW):
                    jobs.append((
                        wsrc[a0 * 128:(a0 + AW) * 128,
                             si * FSLAB:(si + 1) * FSLAB].rearrange(
                            "(a p) o -> p a o", p=128),
                        wdst[si, :, a0:a0 + AW, :], (AW, FSLAB)))
        self._conv_jobs = jobs
        self._conv_ji = 0

    def _pump_conv(self, n):
        while n > 0 and self._conv_ji < len(self._conv_jobs):
            src, dst, shp = self._conv_jobs[self._conv_ji]
            self._conv_ji += 1
            self._conv_slab(src, dst, shp)
            n -= 1

    def _drain_conv(self):
        self._pump_conv(10 ** 9)
        self.tok_conv_p2 = self._cv_tok[:]

    def p0(self):
        p = self.p
        t = []
        t.append(p.dma("sync", self.h64s[:, :], self.h64c[:, :]))
        t.append(p.dma("sync", self.h128s[:, :], self.h128c[:, :]))
        t.append(p.dma("sync", self.onescs[:, :], self.onescd[:, :]))
        t.append(p.dma("sync", self.onesrs[:, :], self.onesrd[:, :]))
        t.append(p.dma("sync", self.nrows[:, :], self.nrowd[:, :]))
        t.append(p.dma("sync", self.maskss[:, :], self.masksd[:, :]))
        t.append(p.dma("sync", self.miscs[:, :], self.miscd[:, :]))
        self.tok_consts = list(t)
        self._conv_part1()
        self._fold_w2()
        p.barrier()

    def _fold_w2(self):
        p, HA, E = self.p, self.HA, self.E
        DCW = min(128, (self.ET * self.S) // 256)
        fins = [self.A8[:, 0:128 * DCW], self.A8[:, 128 * DCW:256 * DCW]]
        fouts = [self.B8[:, 0:128 * DCW], self.B8[:, 128 * DCW:256 * DCW]]
        DC = E // DCW
        w1s = min(512, 128 * DCW)
        w2s = min(512, HA * DCW)
        s1_done = {}
        mm_by = {}
        ev_by = {}
        st_by = {}
        for dc in range(DC):
            par = dc % 2
            fin, fout = fins[par], fouts[par]
            src = self.w2t[:, dc * DCW:(dc + 1) * DCW].rearrange(
                "(a b) d -> a b d", a=HA)
            tin = p.dma("gpsimd", fin[0:HA, :].rearrange("a (b d) -> a b d", b=128),
                        src, after=[ev_by.get(dc - 2, None), self.tok_consts[0]])
            evs = []
            for j in range((128 * DCW) // w1s):
                tmm = p.mm(self.pA(j % 4)[0:HA, 0:w1s], self.h64s[:, :],
                           fin[0:HA, j * w1s:(j + 1) * w1s],
                           start=True, stop=True,
                           after=[tin] + ([evs[-4]] if len(evs) >= 4
                                          else [ev_by.get(dc - 1, None)]))
                evs.append(p.sop(
                    lambda e, j=j, fout=fout: e.copy(
                        fout[0:HA, j * w1s:(j + 1) * w1s],
                        self.pA(j % 4)[0:HA, 0:w1s]),
                    after=[tmm, st_by.get(dc - 2, None)]))
            ev_by[dc] = evs[-1]
            dst = self.w2s1[:, dc * DCW:(dc + 1) * DCW].rearrange(
                "(a b) d -> a b d", a=HA)
            st_by[dc] = p.dma(
                "sync", dst, fout[0:HA, :].rearrange("a (b d) -> a b d", b=128),
                after=[evs[-1]])
            s1_done[dc] = st_by[dc]
        s1_ev_final = ev_by[DC - 1]
        ev_by, st_by = {-1: s1_ev_final, -2: s1_ev_final}, {}
        last_st = None
        for dc in range(DC):
            par = dc % 2
            fin, fout = fins[par], fouts[par]
            src = self.w2s1[:, dc * DCW:(dc + 1) * DCW].rearrange(
                "(a b) d -> b a d", b=128)
            tin = p.dma("gpsimd",
                        fin[:, 0:HA * DCW].rearrange("b (a d) -> b a d", a=HA),
                        src, after=[ev_by.get(dc - 2), s1_done[dc]])
            evs = []
            for j in range((HA * DCW) // w2s):
                tmm = p.mm(self.pB(j % 4)[:, 0:w2s], self.h128s[:, :],
                           fin[:, j * w2s:(j + 1) * w2s],
                           start=True, stop=True,
                           after=[tin] + ([evs[-4]] if len(evs) >= 4
                                          else [ev_by.get(dc - 1)]))
                evs.append(p.sop(
                    lambda e, j=j, fout=fout: e.activation(
                        fout[:, j * w2s:(j + 1) * w2s],
                        self.pB(j % 4)[:, 0:w2s], AF.Copy,
                        bias=0.0, scale=self.FSC),
                    after=[tmm, st_by.get(dc - 2, s1_done[DC - 2 + (dc % 2)])]))
            ev_by[dc] = evs[-1]
            dt0 = (dc * DCW) // 128
            off = (dc * DCW) % 128
            dst = self.w2ppR[dt0, :, :, off:off + DCW]
            st_by[dc] = p.dma(
                "sync", dst, fout[:, 0:HA * DCW].rearrange("b (a d) -> b a d", a=HA),
                after=[evs[-1]])
        self.tok_w2pp = [st_by[dc] for dc in range(max(0, DC - NPOOL), DC)]
        if self.dbg:
            self.tok_w2pp = [p.dma("sync", self.dbg_t["d_w2pp"][:, :, :, :],
                                   self.w2ppR[:, :, :, :], after=self.tok_w2pp)]

    def _rope_evac(self, pq, dst, tcol, tok_mm, eng=0, parity=0,
                   sq_war=None):
        p = self.p
        cos = self.trig[0:48, tcol:tcol + 512]
        sin = self.trig[64:112, tcol:tcol + 512]
        sin2 = self.trig2[0:48, tcol:tcol + 512]
        cos2 = self.trig2[64:112, tcol:tcol + 512]
        T = self.bc[eng]
        ta, tb = T[64:112, 0:512], T[0:48, 0:512]
        op = p.vop if eng == 0 else p.gop
        t1 = p.sop(lambda e: e.copy(dst, pq[:, :]), after=[tok_mm])
        sq = self.qbuf[parity][:, 0:512]
        tsq = p.vop(lambda e: e.tensor_mul(sq, dst, dst),
                    after=[t1, sq_war])
        prev = self._rope_last.get(eng)
        o1 = op(lambda e: e.tensor_mul(ta, dst[0:48, :], sin2), after=[t1, prev])
        o2 = op(lambda e: e.tensor_mul(tb, dst[64:112, :], sin), after=[t1])
        o3 = op(lambda e: e.tensor_mul(dst[0:48, :], dst[0:48, :], cos),
                after=[o1, tsq])
        o4 = op(lambda e: e.tensor_sub(dst[0:48, :], dst[0:48, :], tb), after=[o2, o3])
        o5 = op(lambda e: e.tensor_mul(dst[64:112, :], dst[64:112, :], cos2),
                after=[o2, tsq])
        o6 = op(lambda e: e.tensor_add(dst[64:112, :], dst[64:112, :], ta),
                after=[o1, o5])
        self._rope_last[eng] = o6
        return t1, o6, tsq

    def _norm_fin(self, h, tc, ntc, parity, tsq_tok, nrm, prev_red):
        p = self.p
        sq = self.qbuf[parity][:, 0:512]
        tmm = p.mm(self.pB(3)[0:1, 0:512], self.onescs[:, 0:1], sq,
                   start=(tc == 0), stop=(tc == ntc - 1),
                   after=[tsq_tok, prev_red if tc == 0 else None], inc=True)
        if tc < ntc - 1:
            return None, tmm
        tmx = p.vop(lambda e: e.reduce_max(self.t11[0:1, 0:1],
                                           self.pB(3)[0:1, 0:512],
                                           axis=AX.X), after=[tmm])
        tac = p.vop(lambda e: e.tensor_max(nrm[0:1, h:h + 1], nrm[0:1, h:h + 1],
                                           self.t11[0:1, 0:1]), after=[tmx])
        return tac, tmx

    def p1a(self, b):
        p = self.p
        ET, SH, NH, LT, S = self.ET, self.SH, self.NH, self.LT, self.S
        xf = self.A8[:, 0:ET * SH]
        qfh = self.A8[:, ET * SH:2 * ET * SH]
        t_trig = p.dma("sync", self.trig, self.trigc[:, :],
                       after=(self.tok_conv_p1 if b == 0 else []))
        t_trig2a = p.dma("sync", self.trig2[64:112, :], self.trigc[0:48, :],
                         after=[t_trig])
        t_trig = p.dma("sync", self.trig2[0:48, :], self.trigc[64:112, :],
                       after=[t_trig2a])
        if b == 0:
            self._conv_part2_fill()
            self._conv_init(0, self._conv_w2)
        p.vop(lambda e: e.memset(self.nq2[0:1, :], 0.0))
        p.vop(lambda e: e.memset(self.nk2[0:1, :], 0.0))
        spill_tok = None
        cf_ev = None
        q_ev_last = None
        tac = None
        for half in range(2):
            t0 = half * SH
            t_xf = p.dma("sync", xf.rearrange("p (a t) -> p a t", a=ET),
                         self.xb[b, :, :, t0:t0 + SH].rearrange("a p t -> p a t"),
                         after=[spill_tok, q_ev_last, cf_ev,
                                self._rope_last.get(0), self._rope_last.get(1),
                                self.tok_img["xb0"] if b == 0 else None] +
                               (self.tok_conv_p2 if b > 0 else []))
            wslab = self.W2b[:, 0:ET * 128]
            cf_by = {}
            for lt in range(LT):
                t_w = p.dma("gpsimd", wslab.rearrange("p (a o) -> p a o", a=ET),
                            self.wkvR[lt, :, :, :],
                            after=[cf_ev, self.tok_img["wkv"]])
                if b == 0:
                    self._pump_conv(4)
                for tc in range(SH // 512):
                    gc = lt * (SH // 512) + tc
                    last = None
                    for dt in range(ET):
                        last = p.mm(
                            self.pA(gc % 2), wslab[:, dt * 128:(dt + 1) * 128],
                            xf[:, dt * SH + tc * 512: dt * SH + tc * 512 + 512],
                            start=(dt == 0), stop=(dt == ET - 1),
                            after=[t_w, t_xf, cf_by.get(gc - 2, None)])
                    cf_ev = p.sop(lambda e, lt=lt, t0=t0, tc=tc, gc=gc: e.copy(
                        self.C2[:, lt * S + t0 + tc * 512:
                                lt * S + t0 + tc * 512 + 512],
                        self.pA(gc % 2)[:, :]), after=[last])
                    cf_by[gc] = cf_ev
            NQS = 2
            qsl = [self.B8[:, 0:ET * 128 * NQS],
                   self.B8[:, ET * 128 * NQS:2 * ET * 128 * NQS]]
            t_qs = [None, None]
            ev_by_h = {}
            nrm_mm_prev = None
            pend_norm = None
            prev_red = None
            for h in range(NH):
                sbi = (h // NQS) % 2
                if h % NQS == 0:
                    t_qs[sbi] = p.dma(
                        "gpsimd", qsl[sbi].rearrange("p (a o) -> p a o", a=ET),
                        self.wqR[h // NQS, :, :, :],
                        after=[ev_by_h.get(h - NQS - 1, None),
                               self.tok_img["wq"]])
                    if b == 0:
                        self._pump_conv(5)
                for tc in range(SH // 512):
                    gq = h * (SH // 512) + tc
                    pq = self.pA(2 + gq % 2)
                    last = None
                    for dt in range(ET):
                        lhsT = qsl[sbi][:, dt * 128 * NQS + (h % NQS) * 128:
                                        dt * 128 * NQS + (h % NQS) * 128 + 128]
                        last = p.mm(pq, lhsT,
                                    xf[:, dt * SH + tc * 512:dt * SH + tc * 512 + 512],
                                    start=(dt == 0), stop=(dt == ET - 1),
                                    after=[t_qs[sbi], t_xf, t_trig,
                                           ev_by_h.get(("w", gq - 2), None)])
                    dst = qfh[:, h * SH + tc * 512: h * SH + tc * 512 + 512]
                    ev_t1, ev, tsq = self._rope_evac(
                        pq, dst, t0 + tc * 512, last, eng=(gq % 5 >= 3),
                        parity=gq % 2, sq_war=nrm_mm_prev)
                    ev_by_h[("w", gq)] = ev_t1
                    ev_by_h[h] = ev
                    if pend_norm is not None:
                        tac2, nrm_mm_prev = self._norm_fin(*pend_norm,
                                                           self.nq2, prev_red)
                        if tac2 is not None:
                            tac, prev_red = tac2, tac2
                    pend_norm = (h, tc, SH // 512, gq % 2, tsq)
            if pend_norm is not None:
                tac2, nrm_mm_prev = self._norm_fin(*pend_norm, self.nq2, prev_red)
                if tac2 is not None:
                    tac = tac2
                pend_norm = None
            q_ev_last = ev_by_h[NH - 1]
            spill_tok = p.dma("sync", self.qfs[half, :, :, :],
                              qfh.rearrange("p (h t) -> p h t", h=NH),
                              after=[q_ev_last, tac, self._rope_last.get(0),
                                     self._rope_last.get(1)])
        self.tok_cf = cf_ev
        self.tok_spill = spill_tok
        if self.dbg:
            p.dma("sync", self.dbg_t["d_cf"][:, :], self.C2[:, 0:LT * S],
                  after=[cf_ev])
            for half in range(2):
                p.dma("sync", self.dbg_t["d_qf"][half, :, :, :],
                      self.qfs[half, :, :, :], after=[spill_tok])

    def p1b(self, b):
        p = self.p
        ET, NH, LT, S, E = self.ET, self.NH, self.LT, self.S, self.E
        kf, vt = self.A8, self.B8
        ksl = [self.W2b[:, 0:LT * 128], self.W2b[:, LT * 128:2 * LT * 128]]
        t_ks = [None, None]
        ev_by = {}
        nrm_mm_prev = None
        tac = None
        pend_norm = None
        prev_red = None
        for h in range(NH):
            sbi = h % 2
            t_ks[sbi] = p.dma(
                "gpsimd", ksl[sbi].rearrange("p (a o) -> p a o", a=LT),
                self.wkuR[h, :, :, :],
                after=[ev_by.get(h - 2, None), self.tok_img["wku"]])
            if b == 0:
                self._pump_conv(5)
            for tc in range(S // 512):
                gq = h * (S // 512) + tc
                pq = self.pA(2 + gq % 2)
                last = None
                for lt in range(LT):
                    last = p.mm(pq, ksl[sbi][:, lt * 128:(lt + 1) * 128],
                                self.C2[:, lt * S + tc * 512: lt * S + tc * 512 + 512],
                                start=(lt == 0), stop=(lt == LT - 1),
                                after=[t_ks[sbi], self.tok_cf,
                                       ev_by.get(("w", gq - 2), None)])
                dst = kf[:, h * S + tc * 512: h * S + tc * 512 + 512]
                ev_t1, ev, tsq = self._rope_evac(
                    pq, dst, tc * 512, last, eng=(gq % 5 >= 3),
                    parity=gq % 2, sq_war=nrm_mm_prev)
                ev_by[("w", gq)] = ev_t1
                ev_by[h] = ev
                if pend_norm is not None:
                    tac2, nrm_mm_prev = self._norm_fin(*pend_norm,
                                                       self.nk2, prev_red)
                    if tac2 is not None:
                        tac, prev_red = tac2, tac2
                pend_norm = (h, tc, S // 512, gq % 2, tsq)
        if pend_norm is not None:
            tac2, nrm_mm_prev = self._norm_fin(*pend_norm, self.nk2, prev_red)
            if tac2 is not None:
                tac = tac2
        self.tok_kf = ev_by[NH - 1]
        self.tok_kf2 = [self._rope_last.get(0), self._rope_last.get(1)]
        tm1 = p.vop(lambda e: e.tensor_mul(self.mrow[0:1, :], self.nq2[0:1, :],
                                           self.nk2[0:1, :]), after=[tac])
        tm2 = p.sop(lambda e: e.activation(self.rows[0:1, 0:NH], self.mrow[0:1, :],
                                           AF.Ln, bias=self.eps_row(), scale=1.0),
                    after=[tm1])
        tm3 = p.sop(lambda e: e.activation(self.mrow[0:1, :], self.rows[0:1, 0:NH],
                                           AF.Exp, bias=self.zrow(), scale=0.5),
                    after=[tm2])
        tm4 = p.mm(self.psB[:, 1024:1024 + NH], self.nrows[0:1, :],
                   self.mrow[0:1, :], start=True, stop=True,
                   after=[tm3, nrm_mm_prev])
        tm5 = p.sop(lambda e: e.copy(self.negm[:, :], self.psB[:, 1024:1024 + NH]),
                    after=[tm4])
        self.tok_negm = tm5
        vsl = self.W2b[:, 2 * LT * 128: 2 * LT * 128 + LT * 512]
        evp = {}
        for dch in range(E // 512):
            t_wv = p.dma("gpsimd", vsl.rearrange("p (a o) -> p a o", a=LT),
                         self.wvuR[dch, :, :, :],
                         after=[evp.get(dch - 1, None), self.tok_img["wvu"]])
            if b == 0:
                self._pump_conv(8)
            evl = None
            for tt in range(S // 128):
                gv = dch * (S // 128) + tt
                last = None
                for lt in range(LT):
                    last = p.mm(self.pB(gv % 2),
                                self.C2[:, lt * S + tt * 128: lt * S + tt * 128 + 128],
                                vsl[:, lt * 512:(lt + 1) * 512],
                                start=(lt == 0), stop=(lt == LT - 1),
                                after=[t_wv, self.tok_cf,
                                       evp.get(("w", gv - 2), None)])
                evl = p.sop(lambda e, tt=tt, dch=dch, gv=gv: e.copy(
                    vt[:, tt * E + dch * 512: tt * E + dch * 512 + 512],
                    self.pB(gv % 2)[:, :]), after=[last])
                evp[("w", gv)] = evl
            evp[dch] = evl
        self.tok_vt = evp[E // 512 - 1]
        if b == 0:
            self._drain_conv()
        if self.dbg:
            p.dma("sync", self.dbg_t["d_kf"][:, :], kf[:, 0:NH * S],
                  after=[self.tok_kf])
            p.dma("sync", self.dbg_t["d_vt"][:, :], vt[:, 0:(S // 128) * E],
                  after=[self.tok_vt])
            p.dma("sync", self.dbg_t["d_negm"][:, :], self.negm[:, :],
                  after=[self.tok_negm])

    def p1c(self, b):
        p = self.p
        ET, NH, S, TCH, ND, E = self.ET, self.NH, self.S, self.TCH, self.ND, self.E
        kf, vt = self.A8, self.B8
        r_mean = self.rows[0:1, 0:TCH]
        r_m2 = self.rows[0:1, TCH:2 * TCH]
        r_tmp = self.rows[0:1, 2 * TCH:3 * TCH]
        r_rstd = self.rows[0:1, 3 * TCH:4 * TCH]
        tok_y, tok_yb = None, None
        self._tok_ys = getattr(self, "_tok_ys", {})
        exp_tok = {}
        sum_tok = {}
        ev1_tok, ev2_tok = {}, {}
        sum_war = {}
        bc_war = {}
        tok_tr = [None, None]
        tok_ap_last = None
        tbc_prev, tcp_prev = None, None
        gp = 0

        def sbank(par):
            return self.psA[:, par * 2 * TCH: (par + 1) * 2 * TCH]

        def att_bank(gh):
            return self.psB[:, (gh % 2) * 512:(gh % 2) * 512 + TCH]

        def sum_row(gh):
            return self.psB[0:1, 1024 + (gh % 2) * 512:1024 + (gh % 2) * 512 + TCH]

        def emit_tail(qc, h, gh, mm_att_l, mm_sum_l):
            nonlocal tbc_prev, tcp_prev
            tpar = gp % 2
            tln = p.sop(lambda e, gh=gh: e.activation(
                r_tmp, sum_row(gh), AF.Ln,
                bias=self.zrow(), scale=1.0), after=[mm_sum_l, tbc_prev])
            trs = p.sop(lambda e: e.activation(r_rstd, r_tmp, AF.Exp,
                                               bias=self.zrow(), scale=-1.0),
                        after=[tln])
            sum_war[gh] = tln
            bb = sbank(tpar)[:, 0:TCH]
            tbc = p.mm(bb, self.onesrs[0:1, :], r_rstd,
                       start=True, stop=True,
                       after=[trs, tcp_prev, exp_tok.get(gp - 2, None)])
            tcp = p.sop(lambda e, bb=bb: e.copy(self.bc[0][:, 0:TCH], bb),
                        after=[tbc, ev1_tok.get(gh - 1, None), tok_ap_last])
            bc_war[tpar] = tcp
            att_t = self.M4[:, h * TCH:(h + 1) * TCH]
            tt1 = p.vop(lambda e, gh=gh: e.tensor_mul(
                self.bc[1][:, 0:TCH], att_bank(gh), self.bc[0][:, 0:TCH]),
                after=[tcp, mm_att_l, ev2_tok.get(gh - 1, None)])
            tt2 = p.vop(lambda e, att_t=att_t: e.tensor_add(
                att_t, att_t, self.bc[1][:, 0:TCH]), after=[tt1, t_x])
            ev1_tok[gh] = tt1
            ev2_tok[gh] = tt2
            tbc_prev, tcp_prev = tbc, tcp

        for qc in range(self.NCH):
            q0 = qc * TCH
            half = q0 // self.SH
            tc0 = q0 % self.SH
            t_qfc = p.dma("sync",
                          self.C2[:, 0:NH * TCH].rearrange("p (h t) -> p h t", h=NH),
                          self.qfs[half, :, :, tc0:tc0 + TCH],
                          after=[self.tok_spill, exp_tok.get(gp - 1, None),
                                 sum_tok.get(qc * NH - 1, None)])
            t_x = p.dma("sync",
                        self.M4[:, 0:ET * TCH].rearrange("p (a t) -> p a t", a=ET),
                        self.xT[b, :, q0:q0 + TCH].rearrange("(a p) t -> p a t", p=128),
                        after=[tok_y, tok_yb, tok_ap_last])
            nk = (qc + 1) * ND
            npair = nk // 2
            pend = None
            for h in range(NH):
                gh = qc * NH + h
                mm_att_l, mm_sum_l = None, None
                lag = []
                for kp in range(npair + 2):
                    if len(lag) >= 2 or (kp >= npair and lag):
                        (kpd, tpar_d) = lag.pop(0)
                        for half2 in range(2):
                            k2 = kpd * 2 + half2
                            pbf2 = self.pbuf[tpar_d * 2 + half2]
                            mm_att_l = p.mm(
                                att_bank(gh),
                                vt[:, k2 * E + h * 128: k2 * E + h * 128 + 128],
                                pbf2, start=(k2 == 0), stop=(k2 == nk - 1),
                                after=[exp_tok[(gh, kpd)],
                                       ev1_tok.get(gh - 2, None)],
                                inc=True)
                            mm_sum_l = p.mm(
                                sum_row(gh), self.onescs[:, 0:1],
                                pbf2, start=(k2 == 0), stop=(k2 == nk - 1),
                                after=[exp_tok[(gh, kpd)],
                                       sum_war.get(gh - 2, None)],
                                inc=True)
                    if kp < npair:
                        tpar = gp % 2
                        kt0 = 2 * kp
                        mms = []
                        for half2 in range(2):
                            kt = kt0 + half2
                            mms.append(p.mm(
                                sbank(tpar)[:, half2 * TCH:half2 * TCH + TCH],
                                kf[:, h * S + kt * 128: h * S + kt * 128 + 128],
                                self.C2[:, h * TCH:(h + 1) * TCH],
                                start=True, stop=True,
                                after=[t_qfc, exp_tok.get(gp - 2, None),
                                       bc_war.get(tpar, None)]))
                        te = p.sop(lambda e, h=h, tpar=tpar: e.activation(
                            self.pb2h[tpar], sbank(tpar), AF.Exp,
                            bias=self.negm[:, h:h + 1], scale=self.SC),
                            after=[mms[0], mms[1], self.tok_negm,
                                   sum_tok.get(gh - 1, None) if kp < 2 else None,
                                   mm_sum_l if kp >= 2 else None])
                        for half2 in range(2):
                            kt = kt0 + half2
                            if kt >= qc * ND:
                                r = kt - qc * ND
                                m0 = (self.ND - 1 - r) * 128
                                pbf = self.pbuf[tpar * 2 + half2]
                                te = p.vop(lambda e, pbf=pbf, m0=m0: e.tensor_mul(
                                    pbf, pbf, self.maskss[:, m0:m0 + TCH]),
                                    after=[te])
                        exp_tok[(gh, kp)] = te
                        exp_tok[gp] = te
                        lag.append((kp, tpar))
                        gp += 1
                sum_tok[gh] = mm_sum_l
                if pend is not None:
                    emit_tail(*pend)
                pend = (qc, h, gh, mm_att_l, mm_sum_l)
            emit_tail(*pend)
            mm_mean_prev, mm_var_prev = None, None
            stat_m = self.psB[0:1, 1024:1024 + TCH]
            stat_v = self.psB[0:1, 1536:1536 + TCH]
            for i in range(ET):
                att_t = self.M4[:, i * TCH:(i + 1) * TCH]
                tca = p.sop(lambda e, att_t=att_t, i=i: e.copy(
                    self.abuf[i % 2][:, :], att_t),
                    after=[ev2_tok[qc * NH + i], mm_mean_prev, tok_tr[0]])
                tsq = p.vop(lambda e, i=i: e.tensor_mul(
                    self.qbuf[i % 2][:, 0:TCH], self.abuf[i % 2][:, :],
                    self.abuf[i % 2][:, :]), after=[tca, mm_var_prev, tok_tr[1]])
                mm_mean_prev = p.mm(stat_m, self.onescs[:, 0:1],
                                    self.abuf[i % 2][:, :],
                                    start=(i == 0), stop=(i == ET - 1),
                                    after=[tca, sum_war.get(qc * NH + NH - 2, None)
                                           if i == 0 else None],
                                    inc=True)
                mm_var_prev = p.mm(stat_v, self.onescs[:, 0:1],
                                   self.qbuf[i % 2][:, 0:TCH],
                                   start=(i == 0), stop=(i == ET - 1),
                                   after=[tsq, sum_war.get(qc * NH + NH - 1, None)
                                          if i == 0 else None],
                                   inc=True)
            tr1 = p.sop(lambda e: e.activation(r_mean, stat_m, AF.Copy,
                                               bias=0.0, scale=1.0 / E),
                        after=[mm_mean_prev, tbc_prev])
            tr2 = p.sop(lambda e: e.activation(r_m2, stat_v, AF.Copy,
                                               bias=0.0, scale=1.0 / E),
                        after=[mm_var_prev])
            tok_tr = [tr1, tr2]
            sum_war[qc * NH + NH - 2] = tr1
            sum_war[qc * NH + NH - 1] = tr2
            tr3 = p.vop(lambda e: e.tensor_mul(r_tmp, r_mean, r_mean),
                        after=[tr1])
            tr4 = p.vop(lambda e: e.tensor_sub(r_tmp, r_m2, r_tmp), after=[tr2, tr3])
            tr5 = p.sop(lambda e: e.activation(r_tmp, r_tmp, AF.Ln,
                                               bias=self.eps_row(), scale=1.0),
                        after=[tr4])
            tr6 = p.sop(lambda e: e.activation(r_rstd, r_tmp, AF.Exp,
                                               bias=self.zrow(), scale=-0.5),
                        after=[tr5])
            tr7 = p.vop(lambda e: e.tensor_mul(r_tmp, r_mean, r_rstd), after=[tr6])
            bpar = gp % 2
            bb1 = sbank(bpar)[:, 0:TCH]
            bb2 = sbank((bpar + 1) % 2)[:, 0:TCH]
            tb1 = p.mm(bb1, self.onesrs[0:1, :], r_rstd,
                       start=True, stop=True,
                       after=[tr6, exp_tok.get(gp - 2, None),
                              bc_war.get(bpar, None)])
            tc1 = p.sop(lambda e: e.copy(self.bc[0][:, 0:TCH], bb1),
                        after=[tb1, ev1_tok[qc * NH + NH - 1]])
            tb2 = p.mm(bb2, self.onesrs[0:1, :], r_tmp,
                       start=True, stop=True,
                       after=[tr7, exp_tok.get(gp - 1, None),
                              bc_war.get((bpar + 1) % 2, None), tcp_prev])
            tc2 = p.sop(lambda e: e.copy(self.bc[1][:, 0:TCH], bb2),
                        after=[tb2, ev2_tok[qc * NH + NH - 1]])
            bc_war[bpar] = tc1
            bc_war[(bpar + 1) % 2] = tc2
            tbc_prev, tcp_prev = tb2, tc2
            ap_toks = []
            for i in range(ET):
                att_t = self.M4[:, i * TCH:(i + 1) * TCH]
                ta1 = p.vop(lambda e, att_t=att_t: e.tensor_mul(
                    att_t, att_t, self.bc[0][:, 0:TCH]), after=[tc1])
                ap_toks.append(p.vop(lambda e, att_t=att_t: e.tensor_sub(
                    att_t, att_t, self.bc[1][:, 0:TCH]), after=[ta1, tc2]))
            tok_ap_last = ap_toks[-1]
            tok_y = p.dma("sync", self.ysR[b, qc, :, :],
                          self.M4[:, 0:ET * TCH], after=ap_toks)
            tok_yb = p.dma("gpsimd", self.ysb[b, qc, :, :],
                           self.M4[:, 0:ET * TCH], after=ap_toks)
            self._tok_ys[(b, qc)] = (tok_y, tok_yb)
        self.tok_ylast = tok_y
        self.tok_yblast = tok_yb
        if self.dbg:
            p.dma("sync", self.dbg_t["d_y"][b, :, :, :], self.ysR[b, :, :, :],
                  after=[tok_y])

    def p2(self, b):
        p = self.p
        ET, NH, S, TCH, E, HF = self.ET, self.NH, self.S, self.TCH, self.E, self.HF
        hid = self.A8
        FSLAB = self.FSLAB
        NSL = self.NSL
        NHT = FSLAB // 128
        NBUF = 4
        w1sl = [self.B8[:, i * ET * FSLAB:(i + 1) * ET * FSLAB] for i in range(NBUF)]
        w3sl = [self.B8[:, (NBUF + i) * ET * FSLAB:(NBUF + 1 + i) * ET * FSLAB]
                for i in range(NBUF)]
        W2Q = min(1024, HF // 4)
        NW2 = HF // W2Q
        NW2B = 4
        w2sl = [self.W2b[:, i * (W2Q // 128) * 128:(i + 1) * (W2Q // 128) * 128]
                for i in range(NW2B)]
        r_mean = self.rows[0:1, 0:TCH]
        r_m2 = self.rows[0:1, TCH:2 * TCH]
        r_tmp = self.rows[0:1, 2 * TCH:3 * TCH]
        r_rstd = self.rows[0:1, 3 * TCH:4 * TCH]
        tok_z = None
        pend_ln = None
        gate_tok = {}
        hmul = {}
        silu_tok = {}
        lcp_tok = {}
        w2_slab_prev = [None] * 4
        w2_last_of_chunk = {}
        tok_tr = [None, None]
        tok_tc2 = None
        tok_ap_last = None
        for ch in range(self.NCH):
            t0 = ch * TCH
            t_ybf = p.dma("sync",
                          self.C2[:, 0:ET * TCH],
                          self.ysb[b, ch, :, :],
                          after=[self._tok_ys[(b, ch)][1],
                                 gate_tok.get(ch - 1, None)])
            t_yres = None
            t_w1 = [None] * NBUF
            t_w3 = [None] * NBUF
            lastmm = None
            for si in range(NSL):
                par = si % NBUF
                conv_w = self.tok_conv_p2 if (b == 0 and ch == 0) else []
                t_w1[par] = p.dma("gpsimd",
                                  w1sl[par].rearrange("p (a o) -> p a o", a=ET),
                                  self.w1R[si, :, :, :],
                                  after=[gate_tok.get((ch, si - NBUF, par), None),
                                         w2_last_of_chunk.get(ch - 1, None)
                                         if si < NBUF else None] + conv_w)
                t_w3[par] = p.dma("gpsimd",
                                  w3sl[par].rearrange("p (a o) -> p a o", a=ET),
                                  self.w3R[si, :, :, :],
                                  after=[gate_tok.get((ch, si - NBUF, par), None),
                                         w2_last_of_chunk.get(ch - 1, None)
                                         if si < NBUF else None] + conv_w)
                for ht in range(NHT):
                    hg = si * NHT + ht
                    pg = self.pA(hg % 2)
                    pl = self.pA(2 + hg % 2)
                    lg, ll = None, None
                    for dt in range(ET):
                        lg = p.mm(pg[:, 0:TCH], w1sl[par][:, dt * FSLAB + ht * 128:
                                                dt * FSLAB + ht * 128 + 128],
                                  self.C2[:, dt * TCH:(dt + 1) * TCH],
                                  start=(dt == 0), stop=(dt == ET - 1),
                                  after=[t_w1[par], t_ybf,
                                         silu_tok.get((ch, hg - 2), None)],
                                  inc=(dt == ET - 1))
                    for dt in range(ET):
                        ll = p.mm(pl[:, 0:TCH], w3sl[par][:, dt * FSLAB + ht * 128:
                                                dt * FSLAB + ht * 128 + 128],
                                  self.C2[:, dt * TCH:(dt + 1) * TCH],
                                  start=(dt == 0), stop=(dt == ET - 1),
                                  after=[t_w3[par], t_ybf,
                                         lcp_tok.get((ch, hg - 2), None)],
                                  inc=(dt == ET - 1))
                    ts = p.sop(lambda e, pg=pg, hg=hg: e.activation(
                        self.pbuf[hg % 2], pg[:, 0:TCH], AF.Silu,
                        bias=self.zcol(), scale=1.0),
                        after=[lg, hmul.get((ch, hg - 2), None)])
                    tl = p.vop(lambda e, pl=pl, hg=hg: e.tensor_copy(
                        self.abuf[hg % 2][:, 0:TCH], pl[:, 0:TCH]),
                        after=[ll, hmul.get((ch, hg - 2), None)])
                    tm = p.vop(lambda e, hg=hg: e.tensor_mul(
                        hid[:, hg * TCH:(hg + 1) * TCH],
                        self.pbuf[hg % 2], self.abuf[hg % 2][:, 0:TCH]),
                        after=[ts, tl, w2_last_of_chunk.get(ch - 1, None)])
                    silu_tok[(ch, hg)] = ts
                    lcp_tok[(ch, hg)] = tl
                    hmul[(ch, hg)] = tm
                    lastmm = ll
                gate_tok[(ch, si, par)] = lastmm
                if si == 0:
                    if pend_ln is not None:
                        pend_ln()
                        pend_ln = None
                    t_yres = p.dma("sync", self.M4[:, 0:ET * TCH],
                                   self.ysR[b, ch, :, :],
                                   after=[self._tok_ys[(b, ch)][0], tok_z,
                                          tok_ap_last])
            gate_tok[ch] = lastmm
            mm_mean_prev, mm_var_prev = None, None
            zevs = []
            for dt in range(ET):
                pz = self.pB(dt % 2)
                lastz = None
                for q in range(NW2):
                    par = q % NW2B
                    tw2 = p.dma(
                        "sync", w2sl[par][:, 0:(W2Q // 128) * 128],
                        self.w2ppR[dt, :, q * (W2Q // 128):(q + 1) * (W2Q // 128), :]
                        .rearrange("p j d -> p (j d)"),
                        after=[w2_slab_prev[par]] + self.tok_w2pp)
                    for j in range(W2Q // 128):
                        hg = q * (W2Q // 128) + j
                        lastz = p.mm(pz[:, 0:TCH], w2sl[par][:, j * 128:(j + 1) * 128],
                                     hid[:, hg * TCH:(hg + 1) * TCH],
                                     start=(hg == 0), stop=(hg == HF // 128 - 1),
                                     after=[tw2, hmul[(ch, hg)],
                                            zevs[dt - 2] if dt >= 2 else None,
                                            tok_tc2 if (hg == 0 and dt < 2)
                                            else None],
                                     inc=(j == W2Q // 128 - 1))
                    w2_slab_prev[par] = lastz
                zt = self.M4[:, dt * TCH:(dt + 1) * TCH]
                tz1 = p.vop(lambda e, zt=zt, pz=pz: e.tensor_add(zt, zt,
                                                                 pz[:, 0:TCH]),
                            after=[lastz, t_yres])
                zevs.append(tz1)
                tca = p.sop(lambda e, zt=zt, dt=dt: e.copy(
                    self.abuf[dt % 2][:, :], zt),
                    after=[tz1, mm_mean_prev, tok_tr[0],
                           silu_tok.get((ch, HF // 128 - 2 + dt), None)])
                tsq = p.vop(lambda e, dt=dt: e.tensor_mul(
                    self.qbuf[dt % 2][:, 0:TCH], self.abuf[dt % 2][:, :],
                    self.abuf[dt % 2][:, :]), after=[tca, mm_var_prev, tok_tr[1]])
                mm_mean_prev = p.mm(self.pB(2)[0:1, 0:TCH], self.onescs[:, 0:1],
                                    self.abuf[dt % 2][:, :],
                                    start=(dt == 0), stop=(dt == ET - 1),
                                    after=[tca], inc=True)
                mm_var_prev = p.mm(self.pB(3)[0:1, 0:TCH], self.onescs[:, 0:1],
                                   self.qbuf[dt % 2][:, 0:TCH],
                                   start=(dt == 0), stop=(dt == ET - 1),
                                   after=[tsq], inc=True)
            w2_last_of_chunk[ch] = w2_slab_prev[(NW2 - 1) % NW2B]

            def emit_ln2(ch=ch, t0=t0, mm_mean_prev=mm_mean_prev,
                         mm_var_prev=mm_var_prev, zevs=tuple(zevs)):
                nonlocal tok_z, tok_ap_last, tok_tc2, tok_tr
                tr1 = p.sop(lambda e: e.activation(r_mean, self.pB(2)[0:1, 0:TCH],
                                                   AF.Copy, bias=0.0, scale=1.0 / E),
                            after=[mm_mean_prev])
                tr2 = p.sop(lambda e: e.activation(r_m2, self.pB(3)[0:1, 0:TCH],
                                                   AF.Copy, bias=0.0, scale=1.0 / E),
                            after=[mm_var_prev])
                tok_tr = [tr1, tr2]
                tr3 = p.vop(lambda e: e.tensor_mul(r_tmp, r_mean, r_mean), after=[tr1])
                tr4 = p.vop(lambda e: e.tensor_sub(r_tmp, r_m2, r_tmp),
                            after=[tr2, tr3])
                tr5 = p.sop(lambda e: e.activation(r_tmp, r_tmp, AF.Ln,
                                                   bias=self.eps_row(), scale=1.0),
                            after=[tr4])
                tr6 = p.sop(lambda e: e.activation(r_rstd, r_tmp, AF.Exp,
                                                   bias=self.zrow(), scale=-0.5),
                            after=[tr5])
                tr7 = p.vop(lambda e: e.tensor_mul(r_tmp, r_mean, r_rstd), after=[tr6])
                tb1 = p.mm(self.pB(1)[:, 0:TCH], self.onesrs[0:1, :], r_rstd,
                           start=True, stop=True, after=[tr6, zevs[-1], tr2])
                tc1 = p.sop(lambda e: e.copy(self.bc[0][:, 0:TCH],
                                             self.pB(1)[:, 0:TCH]),
                            after=[tb1, tok_ap_last])
                tb2 = p.mm(self.pB(1)[:, 0:TCH], self.onesrs[0:1, :], r_tmp,
                           start=True, stop=True, after=[tr7, tc1])
                tc2 = p.sop(lambda e: e.copy(self.bc[1][:, 0:TCH],
                                             self.pB(1)[:, 0:TCH]),
                            after=[tb2, tok_tc2])
                tok_tc2 = tc2
                ap_toks = []
                for i in range(ET):
                    zt = self.M4[:, i * TCH:(i + 1) * TCH]
                    ta1 = p.vop(lambda e, zt=zt: e.tensor_mul(
                        zt, zt, self.bc[0][:, 0:TCH]), after=[tc1])
                    ap_toks.append(p.vop(lambda e, zt=zt: e.tensor_sub(
                        zt, zt, self.bc[1][:, 0:TCH]), after=[ta1, tc2]))
                tok_ap_last = ap_toks[-1]
                tok_z = p.dma("sync", self.zT[b, ch, :, :],
                              self.M4[:, 0:ET * TCH], after=ap_toks)
            pend_ln = emit_ln2
        pend_ln()
        pend_ln = None
        self.tok_zlast = tok_z

    def build(self):
        p = self.p
        upto = self.cfg.get("phases", "all")
        self.p0()
        if upto != "p0":
            for b in range(self.B):
                self.p1a(b)
                p.barrier()
                if upto == "p1a":
                    break
                self.p1b(b)
                p.barrier()
                if upto == "p1b":
                    break
                self.p1c(b)
                p.barrier()
                if upto == "p1c":
                    break
                self.p2(b)
                p.barrier()
        p.barrier()
        p.emit(self.es)
        self.es.close()
        return self.nc


def head_perm(HD):
    nope = HD // 4
    xe = np.arange(nope, HD, 2)
    xo = np.arange(nope + 1, HD, 2)
    return np.concatenate([xe, np.arange(0, nope // 2), xo,
                           np.arange(nope // 2, nope)])


def host_consts(cfg):
    S, E, HD, HF, TCH = cfg["S"], cfg["E"], cfg["HD"], cfg["HF"], cfg["TCH"]
    HA = HF // 128
    ND = TCH // 128
    nope = HD // 4
    rope_d = HD - nope
    freqs = (np.float32(1.0) / (np.float32(10000.0) **
             (np.arange(0, rope_d, 2, dtype=np.float32) / np.float32(rope_d))))
    emb = np.arange(S, dtype=np.float32)[None, :] * freqs[:, None].astype(np.float32)
    trig = np.zeros((128, S), dtype=np.float32)
    trig[0:48] = np.cos(emb)
    trig[64:112] = np.sin(emb)

    def syl(n):
        H = np.array([[1.0]], dtype=np.float32)
        while H.shape[0] < n:
            H = np.block([[H, H], [H, -H]])
        return H

    h128 = syl(128).astype(BFD)
    h64 = syl(HA).astype(BFD)
    onesc = np.ones((128, 1), dtype=BFD)
    onesr = np.ones((1, 128), dtype=np.float32)
    nrow = np.full((1, 128), -1.0 / np.sqrt(HD), dtype=np.float32)
    k = np.arange(128)[:, None]
    c = np.arange((ND - 1) * 128 + TCH)[None, :]
    masks = (c >= k + (ND - 1) * 128).astype(BFD)
    misc = np.zeros((128, 4), dtype=np.float32)
    misc[:, 0] = 1e-5
    return dict(trigc=trig, h64c=np.ascontiguousarray(h64),
                h128c=np.ascontiguousarray(h128), onescd=onesc, onesrd=onesr,
                nrowd=nrow, masksd=masks, miscd=misc)


def host_in_maps(cfg, n_cores, x, Wq, Wkv, Wku, Wvu, W1, W2, W3):
    E, HD = cfg["E"], cfg["HD"]
    NH = E // HD
    perm = head_perm(HD)
    full_perm = np.concatenate([h * HD + perm for h in range(NH)])
    consts = host_consts(cfg)
    wq_p = np.ascontiguousarray(Wq[:, full_perm])
    wku_p = np.ascontiguousarray(Wku[:, full_perm])
    w1t = np.ascontiguousarray(W1.T)
    w3t = np.ascontiguousarray(W3.T)
    w2t = np.ascontiguousarray(W2.T)
    B = cfg["B"]
    maps = []
    for c in range(n_cores):
        xs = np.ascontiguousarray(x[c * B:(c + 1) * B].transpose(0, 2, 1))
        m = dict(xT=xs, wq=wq_p, wkv=np.ascontiguousarray(Wkv),
                 wku=wku_p, wvu=np.ascontiguousarray(Wvu),
                 w1t=w1t, w3t=w3t, w2t=w2t, **consts)
        maps.append(m)
    return maps


def kernel(L0_attn_x, L0_attn_Wq, L0_attn_Wkv_down, L0_attn_Wk_up, L0_attn_Wv_up,
           L0_ffn_W1, L0_ffn_W2, L0_ffn_W3):
    cfg = mk_cfg()
    x = np.asarray(L0_attn_x, dtype=np.float32)
    maps = host_in_maps(cfg, N_CORES, x,
                        np.asarray(L0_attn_Wq, np.float32),
                        np.asarray(L0_attn_Wkv_down, np.float32),
                        np.asarray(L0_attn_Wk_up, np.float32),
                        np.asarray(L0_attn_Wv_up, np.float32),
                        np.asarray(L0_ffn_W1, np.float32),
                        np.asarray(L0_ffn_W2, np.float32),
                        np.asarray(L0_ffn_W3, np.float32))
    nc = Builder(cfg).build()
    res = run_bass_kernel_spmd(nc, maps, core_ids=list(range(N_CORES)))
    B, S, E, TCH = cfg["B"], cfg["S"], cfg["E"], cfg["TCH"]
    NCH, ET = S // TCH, E // 128
    outs = []
    for c in range(N_CORES):
        zt = res.results[c]["zT"].reshape(B, NCH, 128, ET, TCH)
        outs.append(np.ascontiguousarray(
            zt.transpose(0, 1, 4, 3, 2).reshape(B, S, E)))
    return np.concatenate(outs, axis=0).astype(np.float32)
